# version 12
# speedup vs baseline: 1.6840x; 1.0177x over previous
import math
import contextlib
import numpy as np
import ml_dtypes
import concourse.bass as bass
import concourse.mybir as mybir
from concourse.bass_utils import run_bass_kernel_spmd

F32 = mybir.dt.float32
BF16 = mybir.dt.bfloat16
OP = mybir.AluOpType
AF = mybir.ActivationFunctionType
AX = mybir.AxisListType.X

NCORES = 8
S = 2048
D = 1024
NH = 16
DK = 64
TQ = 512
NT = S // TQ
NC = D // 128
E = 8
H_FF = 4096

DEV_TAPS = []
DEV_FORCE_COMBINE1 = False
DEV_FORCE_FLUSH = False
SHUF_EO = []
for _g in range(16):
    SHUF_EO += [2 * _g + 1, 2 * _g]


def _bfa(a):
    return np.asarray(a, dtype=np.float64).astype(ml_dtypes.bfloat16)


class Sem:
    def __init__(self, nc, name):
        self.h = nc.alloc_semaphore(name=name)
        self.n = 0

    def mark(self, inst, k=1):
        inst.then_inc(self.h, k)
        self.n += k
        return self.n

    def wait(self, eng, at=None):
        v = self.n if at is None else at
        if v > 0:
            eng.wait_ge(self.h, v)


def build_program(taps):
    global DEV_FORCE_COMBINE1
    nc = bass.Bass("TRN2", target_bir_lowering=False)
    T, V, A, G, Y = nc.tensor, nc.vector, nc.scalar, nc.gpsimd, nc.sync
    s_w = Sem(nc, "s_w")
    s_io = Sem(nc, "s_io")
    s_mm = Sem(nc, "s_mm")
    s_act = Sem(nc, "s_act")
    s_dve = Sem(nc, "s_dve")
    s_cc = Sem(nc, "s_cc")
    s_st = Sem(nc, "s_st")

    dm = {}

    def din(name, shape, dt=F32):
        dm[name] = nc.dram_tensor(name, list(shape), dt, kind="ExternalInput")
        return dm[name]

    din("x_in", (D, S))
    din("WqT", (D, D)); din("WkT", (D, D)); din("WvT", (D, D)); din("WoT", (D, D))
    din("bq_p", (128, NC)); din("bk_p", (128, NC)); din("bo_p", (128, NC))
    din("bv", (1, D))
    din("Wg", (D, E))
    din("W1", (E, D, H_FF)); din("b1", (E, H_FF))
    din("W2", (E, H_FF, D)); din("b2", (E, D))
    din("W_DQ", (D, 256)); din("W_DKV", (D, 256))
    din("W_UQ", (256, D)); din("W_QR", (256, 512)); din("Wqcat", (256, NH * 96))
    din("W_UK", (256, D)); din("W_UV", (256, D))
    din("W_KR", (D, 32))
    din("W_O", (D, D))
    din("Win", (D, 64)); din("binc", (64, 1)); din("Wout", (64, D))
    din("W1f", (64, 16)); din("b1f", (64, 16)); din("W2f", (64, 16))
    din("b2f", (64, 1)); din("bout_p", (128, NC))
    din("b1_p", (128, E * H_FF // 128))
    din("tri", (128, 128), BF16)
    din("ones", (128, 128), BF16)
    din("ident", (128, 128), BF16)
    din("trif", (128, 128))
    din("identf", (128, 128))
    din("mask16f", (128, 16))
    din("mask16fT", (16, 128))
    din("maskPfx16", (128, 128))
    din("maskJ16", (128, 128))
    din("shift8", (16, 16))
    din("oh127f", (128, 1))
    din("sel8", (8, 8 * 128), BF16)
    din("mask_lt", (8, 1))
    din("b2_p", (128, E * NC // 1))
    din("rcos0", (128, S), BF16); din("rsin0", (128, S), BF16)
    din("rcos1", (128, S), BF16); din("rsin1", (128, S), BF16)

    out_x = nc.dram_tensor("out_x", [D, S], F32, kind="ExternalOutput")
    dbg = nc.dram_tensor("dbg", [128, 4096], F32, kind="ExternalOutput") if "dbg" in taps else None
    cc_in = nc.dram_tensor("cc_in", [1, 16], F32, kind="Internal")
    cc_out = nc.dram_tensor("cc_out", [8, 16], F32, kind="Internal")
    tap_dram = {t: nc.dram_tensor("tap_" + t, [D, S], F32, kind="ExternalOutput")
                for t in taps}

    def sb(name, shape, dt=BF16):
        return nc.alloc_sbuf_tensor(name, list(shape), dt)

    x = [sb(f"x{c}", [128, S], F32) for c in range(NC)]
    xb = [sb(f"xb{c}", [128, S], BF16) for c in range(NC)]
    tri = sb("tri_s", [128, 128]); ones = sb("ones_s", [128, 128])
    onescf = sb("onescf_s", [128, 1], F32)
    w1f_s = sb("w1f_s", [64, 16], F32); b1f_s = sb("b1f_s", [64, 16], F32)
    w2f_s = sb("w2f_s", [64, 16], F32); binc_s = sb("binc_s", [64, 1], F32)
    b2f_s = sb("b2f_s", [64, 1], F32); bout_p = sb("bout_p_s", [128, NC], F32)
    bq_p = sb("bq_p_s", [128, NC], F32); bk_p = sb("bk_p_s", [128, NC], F32)
    bo_p = sb("bo_p_s", [128, NC], F32)
    bvr = sb("bv_s", [1, D])
    identb = sb("identb_s", [128, 128])
    epsln = sb("epsln", [128, 1], F32)

    for c in range(NC):
        s_io.mark(Y.dma_start(out=x[c][:, :], in_=dm["x_in"].ap()[128 * c:128 * (c + 1), :]), 16)
    for name, t_ in (("tri", tri), ("ones", ones), ("ident", identb),
                     ("bq_p", bq_p), ("bk_p", bk_p), ("bo_p", bo_p),
                     ("W1f", w1f_s), ("b1f", b1f_s), ("W2f", w2f_s),
                     ("binc", binc_s), ("b2f", b2f_s), ("bout_p", bout_p)):
        s_io.mark(Y.dma_start(out=t_[:, :], in_=dm[name].ap()), 16)
    s_w.mark(G.dma_start(out=bvr[:, :], in_=dm["bv"].ap()), 16)
    s_dve.mark(V.memset(epsln[:, :], 1e-5))
    s_dve.mark(V.memset(onescf[:, :], 1.0))
    io_consts = s_io.n
    w_consts = s_w.n
    eps_n = s_dve.n

    ps = [nc.alloc_psum_tensor(f"ps{i}", [128, TQ], F32) for i in range(8)]
    bank_free = [(s_mm, 0)] * 8

    def t_take_bank(i):
        sem, n = bank_free[i]
        sem.wait(T, n)
        return ps[i]

    def evac_mark(i, sem):
        bank_free[i] = (sem, sem.n)

    def vseq(inst):
        s_dve.mark(inst)
        s_dve.wait(V)
        return s_dve.n

    def cast_xb():
        for c in range(NC):
            i = V.tensor_copy(xb[c][:, :], x[c][:, :])
        s_dve.mark(i)
        return s_dve.n

    def rope_cols(z, cosT, sinT, shufbuf, csl, base=0, nrows=128):
        sl = slice(base, base + nrows)
        V.stream_shuffle(shufbuf[sl, csl], z[sl, csl], SHUF_EO)
        V.tensor_tensor(z[sl, csl], z[sl, csl], cosT[sl, csl], OP.mult)
        V.tensor_tensor(shufbuf[sl, csl], shufbuf[sl, csl], sinT[sl, csl], OP.mult)
        i = V.tensor_tensor(z[sl, csl], z[sl, csl], shufbuf[sl, csl], OP.add)
        s_dve.mark(i)
        return s_dve.n

    def rope_inplace(z, nrows, cosT, sinT, shufbuf, base=0):
        sl = slice(base, base + nrows)
        V.stream_shuffle(shufbuf[sl, :], z[sl, :], SHUF_EO)
        V.tensor_tensor(z[sl, :], z[sl, :], cosT[sl, :], OP.mult)
        V.tensor_tensor(shufbuf[sl, :], shufbuf[sl, :], sinT[sl, :], OP.mult)
        i = V.tensor_tensor(z[sl, :], z[sl, :], shufbuf[sl, :], OP.add)
        s_dve.mark(i)
        return s_dve.n

    def postln():
        with contextlib.ExitStack() as stl:
            uid = str(s_dve.n)
            sqb = [stl.enter_context(nc.sbuf_tensor(f"sqb{uid}_{i}", [128, TQ], BF16))
                   for i in range(2)]
            mtile = stl.enter_context(nc.sbuf_tensor("mt" + uid, [128, TQ], F32))
            ritile = stl.enter_context(nc.sbuf_tensor("rt" + uid, [128, TQ], F32))
            s_act.wait(V)
            def ln_copies(t):
                tsl = slice(TQ * t, TQ * (t + 1))
                for c in range(NC):
                    i1 = V.tensor_copy(xb[c][:, tsl], x[c][:, tsl])
                s_dve.mark(i1)
                return s_dve.n

            cp_n = [0] * NT
            cp_n[0] = ln_copies(0)
            for t in range(NT):
                tsl = slice(TQ * t, TQ * (t + 1))
                sq_free = [(s_mm, 0), (s_mm, 0)]
                for c in range(NC):
                    sem, n = sq_free[c % 2]
                    sem.wait(A, n)
                    A.wait_ge(s_dve.h, cp_n[t])
                    i2 = A.activation(sqb[c % 2][:, :], xb[c][:, tsl], AF.Square)
                    s_act.mark(i2)
                    if c == 0:
                        b6 = t_take_bank(6)
                        b7 = t_take_bank(7)
                    T.wait_ge(s_dve.h, cp_n[t])
                    T.matmul(b6[:, :], ones[:, :], xb[c][:, tsl],
                             start=(c == 0), stop=(c == NC - 1))
                    s_act.wait(T)
                    i3 = T.matmul(b7[:, :], ones[:, :], sqb[c % 2][:, :],
                                  start=(c == 0), stop=(c == NC - 1))
                    s_mm.mark(i3)
                    sq_free[c % 2] = (s_mm, s_mm.n)
                s_mm.wait(V)
                V.tensor_scalar(mtile[:, :], b6[:, :], 1.0 / D, None, OP.mult)
                V.tensor_scalar(ritile[:, :], b7[:, :], 1.0 / D, None, OP.mult)
                V.tensor_tensor(b6[:, :], mtile[:, :], mtile[:, :], OP.mult)
                i4 = V.tensor_tensor(ritile[:, :], ritile[:, :], b6[:, :], OP.subtract)
                s_dve.mark(i4)
                evac_mark(6, s_dve)
                evac_mark(7, s_dve)
                s_dve.wait(A)
                i5 = A.activation(ritile[:, :], ritile[:, :], AF.Sqrt, bias=epsln[:, :])
                s_act.mark(i5)
                s_act.wait(V)
                V.reciprocal(ritile[:, :], ritile[:, :])
                if t + 1 < NT:
                    cp_n[t + 1] = ln_copies(t + 1)
                for c in range(NC):
                    V.tensor_tensor(x[c][:, tsl], x[c][:, tsl], mtile[:, :], OP.subtract)
                    i7 = V.tensor_tensor(x[c][:, tsl], x[c][:, tsl], ritile[:, :], OP.mult)
                s_dve.mark(i7)
        return s_dve.n

    def store_tap(name):
        if name in tap_dram:
            s_dve.wait(Y)
            s_act.wait(Y)
            for c in range(NC):
                s_st.mark(Y.dma_start(out=tap_dram[name].ap()[128 * c:128 * (c + 1), :],
                                      in_=x[c][:, :]), 16)

    def l0_attention(st):
        wbig = st.enter_context(nc.sbuf_tensor("wbig", [128, NC * 1024], BF16))
        vtm = [st.enter_context(nc.sbuf_tensor(f"vtm{t}", [128, D], BF16)) for t in range(16)]
        qa = [st.enter_context(nc.sbuf_tensor(f"qa{i}", [128, S], BF16)) for i in range(2)]
        ka = [st.enter_context(nc.sbuf_tensor(f"ka{i}", [128, S], BF16)) for i in range(2)]
        wqr = [st.enter_context(nc.sbuf_tensor(f"wqr{i}", [128, NC * 128], BF16)) for i in range(2)]
        pt = [st.enter_context(nc.sbuf_tensor(f"pt{k}", [128, TQ], BF16)) for k in range(3)]
        ctxa = [st.enter_context(nc.sbuf_tensor(f"ctxa{c}", [128, S], BF16)) for c in range(6)]
        stA = st.enter_context(contextlib.ExitStack())
        shufb = stA.enter_context(nc.sbuf_tensor("shufb", [128, S], BF16))
        rcos0 = stA.enter_context(nc.sbuf_tensor("rcos0_s", [128, S], BF16))
        rsin0 = stA.enter_context(nc.sbuf_tensor("rsin0_s", [128, S], BF16))
        s_io.mark(Y.dma_start(out=rcos0[:, :], in_=dm["rcos0"].ap()), 16)
        s_io.mark(Y.dma_start(out=rsin0[:, :], in_=dm["rsin0"].ap()), 16)
        print("sbuf left (l0 attn):", nc.sbuf_bytes_remaining)

        s_io.wait(V, io_consts)
        cast_xb()

        wv_n = s_w.mark(G.dma_start(
            out=wbig[:, :].rearrange("p (c n) -> p c n", n=1024),
            in_=dm["WvT"].ap().rearrange("(c p) n -> p c n", p=128)), 16)
        T.wait_ge(s_w.h, wv_n)
        s_dve.wait(T)
        s_io.wait(T, io_consts)
        for tt in range(16):
            tsl = slice(128 * tt, 128 * (tt + 1))
            for half in range(2):
                bi = (2 * tt + half) % 4
                bank = t_take_bank(bi)
                for c in range(NC):
                    T.matmul(bank[:, :], xb[c][:, tsl],
                             wbig[:, 1024 * c + 512 * half:1024 * c + 512 * (half + 1)],
                             start=(c == 0), stop=False)
                i = T.matmul(bank[:, :], ones[0:1, 0:128],
                             bvr[0:1, 512 * half:512 * (half + 1)], start=False, stop=True)
                s_mm.mark(i)
                s_mm.wait(A)
                ia = A.activation(vtm[tt][:, 512 * half:512 * (half + 1)], bank[:, :], AF.Copy)
                s_act.mark(ia)
                evac_mark(bi, s_act)

        G.wait_ge(s_mm.h, s_mm.n)
        wo_n = s_w.mark(G.dma_start(
            out=wbig[:, :].rearrange("p (c n) -> p c n", n=1024),
            in_=dm["WoT"].ap().rearrange("(c p) n -> p c n", p=128)), 16)

        ring_use = [0, 0]
        qk_rope_n = [0, 0]
        wld = [[0] * 8, [0] * 8]

        def l0_load_pair(p):
            csl = slice(128 * p, 128 * (p + 1))
            for wi, wname in enumerate(("WqT", "WkT")):
                G.wait_ge(s_mm.h, ring_use[wi])
                wld[wi][p] = s_w.mark(G.dma_start(
                    out=wqr[wi][:, :].rearrange("p (c n) -> p c n", n=128),
                    in_=dm[wname].ap()[:, csl].rearrange("(c p) n -> p c n", p=128)), 16)

        def l0_gen_pair(p, defer_rope=False):
            pb = p % 2
            for wi, (dst, bias_) in enumerate(((qa[pb], bq_p), (ka[pb], bk_p))):
                ring = wqr[wi]
                T.wait_ge(s_w.h, wld[wi][p])
                for t in range(NT):
                    tsl = slice(TQ * t, TQ * (t + 1))
                    bi = 7 if t % 2 == 0 else 0
                    bank = t_take_bank(bi)
                    for c in range(NC):
                        i = T.matmul(bank[:, :], ring[:, 128 * c:128 * (c + 1)],
                                     xb[c][:, tsl], start=(c == 0), stop=(c == NC - 1))
                    s_mm.mark(i)
                    A.wait_ge(s_mm.h, s_mm.n)
                    ia = A.activation(dst[:, tsl], bank[:, :], AF.Identity,
                                      bias=bias_[:, p:p + 1])
                    s_act.mark(ia)
                    evac_mark(bi, s_act)
                ring_use[wi] = s_mm.n
            if p + 1 < 8:
                l0_load_pair(p + 1)
            if not defer_rope:
                _rope_pair(p)

        def _rope_pair(p):
            pb = p % 2
            s_act.wait(V)
            rope_inplace(qa[pb], 128, rcos0, rsin0, shufb)
            rope_inplace(ka[pb], 128, rcos0, rsin0, shufb)
            qk_rope_n[pb] = s_dve.n

        l0_load_pair(0)
        l0_gen_pair(0)
        pt_reader = [0] * 3
        blk_i = 0
        for p in range(8):
            pb = p % 2
            rope_chunks = []
            if p + 1 < 8:
                dr = (p + 1 < 7)
                l0_gen_pair(p + 1, defer_rope=dr)
                if dr:
                    npb = (p + 1) % 2
                    rope_chunks = [(qa[npb], j) for j in range(NT)] + \
                                  [(ka[npb], j) for j in range(NT)]
            if p == 6:
                stA.close()
                for c in (6, 7):
                    ctxa.append(st.enter_context(nc.sbuf_tensor(f"ctxa{c}", [128, S], BF16)))
            T.wait_ge(s_dve.h, qk_rope_n[pb])
            for qt in range(NT):
                qsl = slice(TQ * qt, TQ * (qt + 1))
                nkc = 4 * qt + 4
                for hh in range(2):
                    head = 2 * p + hh
                    hsl = slice(64 * hh, 64 * hh + 64)
                    dbi, cbi = (3, 4) if blk_i % 2 == 0 else (5, 6)
                    blk_i += 1
                    dps = t_take_bank(dbi)[hsl, :]
                    cps = t_take_bank(cbi)[hsl, :]
                    pt_ready = [0] * 3

                    def emit_S(kc):
                        bi = kc % 3
                        bank = t_take_bank(bi)
                        i = T.matmul(bank[:, :], ka[pb][hsl, 128 * kc:128 * (kc + 1)],
                                     qa[pb][hsl, qsl], start=True, stop=True)
                        s_mm.mark(i)
                        A.wait_ge(s_mm.h, s_mm.n)
                        A.wait_ge(s_mm.h, pt_reader[kc % 3])
                        ia = A.activation(pt[kc % 3][:, :], bank[:, :], AF.Exp, scale=0.125)
                        s_act.mark(ia)
                        evac_mark(bi, s_act)
                        s_ = kc - 4 * qt
                        if s_ >= 0:
                            s_act.wait(V)
                            blk = slice(128 * s_, 128 * (s_ + 1))
                            i = V.tensor_tensor(pt[kc % 3][:, blk], pt[kc % 3][:, blk],
                                                tri[:, :], OP.mult)
                            if s_ > 0:
                                i = V.memset(pt[kc % 3][:, 0:128 * s_], 0.0)
                            s_dve.mark(i)
                            pt_ready[kc % 3] = ("dve", s_dve.n)
                        else:
                            pt_ready[kc % 3] = ("act", s_act.n)

                    emit_S(0)
                    if nkc > 1:
                        emit_S(1)
                    for kc in range(nkc):
                        if kc + 2 < nkc:
                            emit_S(kc + 2)
                        kind, n_ = pt_ready[kc % 3]
                        if kind == "dve":
                            T.wait_ge(s_dve.h, n_)
                        else:
                            T.wait_ge(s_act.h, n_)
                        T.matmul(dps, ones[:, hsl], pt[kc % 3][:, :],
                                 start=(kc == 0), stop=(kc == nkc - 1))
                        i = T.matmul(cps, vtm[kc][:, 64 * head:64 * (head + 1)],
                                     pt[kc % 3][:, :], start=(kc == 0), stop=(kc == nkc - 1))
                        s_mm.mark(i)
                        pt_reader[kc % 3] = s_mm.n
                    s_mm.wait(V)
                    V.tensor_scalar(dps, dps, 1e-8, None, OP.add)
                    V.reciprocal(dps, dps)
                    V.tensor_copy(ctxa[p][hsl, qsl], cps)
                    i = V.tensor_tensor(ctxa[p][hsl, qsl], ctxa[p][hsl, qsl], dps, OP.mult)
                    s_dve.mark(i)
                    evac_mark(dbi, s_dve)
                    evac_mark(cbi, s_dve)
                    for _ in range(2):
                        if rope_chunks:
                            if len(rope_chunks) == 2 * NT:
                                s_act.wait(V)
                            z_, j_ = rope_chunks.pop(0)
                            n_ = rope_cols(z_, rcos0, rsin0, shufb,
                                           slice(TQ * j_, TQ * (j_ + 1)))
                            if not rope_chunks:
                                qk_rope_n[(p + 1) % 2] = n_
        ctx_n = s_dve.n
        T.wait_ge(s_dve.h, ctx_n)
        T.wait_ge(s_w.h, wo_n)
        for qt in range(NT):
            qsl = slice(TQ * qt, TQ * (qt + 1))
            for oc in range(NC):
                bi = 3 + (oc % 2)
                bank = t_take_bank(bi)
                for c in range(NC):
                    T.matmul(bank[:, :],
                             wbig[:, 1024 * c + 128 * oc:1024 * c + 128 * (oc + 1)],
                             ctxa[c][:, qsl], start=(c == 0), stop=False)
                i = T.matmul(bank[:, :], identb[:, :], xb[oc][:, qsl],
                             start=False, stop=True)
                s_mm.mark(i)
                A.wait_ge(s_mm.h, s_mm.n)
                ia = A.activation(x[oc][:, qsl], bank[:, :], AF.Identity,
                                  bias=bo_p[:, oc:oc + 1])
                s_act.mark(ia)
                evac_mark(bi, s_act)

    with contextlib.ExitStack() as st1, nc.named_scope("l0_attn"):
        l0_attention(st1)
    with nc.named_scope("ln0"):
        postln()
    store_tap("x1")

    def l0_moe(st):
        NTC = 16
        h_sb = [st.enter_context(nc.sbuf_tensor(f"hsb{i}", [128, 1024], BF16))
                for i in range(32)]
        w1r = [st.enter_context(nc.sbuf_tensor(f"w1r{i}", [128, 1024], BF16))
               for i in range(4)]
        w2r = [st.enter_context(nc.sbuf_tensor(f"w2r{i}", [128, 2048], BF16))
               for i in range(4)]
        cbc = st.enter_context(nc.sbuf_tensor("cbc", [128, S], BF16))
        cT = st.enter_context(nc.sbuf_tensor("cT", [8, S], BF16))
        gate_e = st.enter_context(nc.sbuf_tensor("gate_e", [128, 8], F32))
        cs_all = st.enter_context(nc.sbuf_tensor("cs_all", [128, 256], F32))
        oh_all = st.enter_context(nc.sbuf_tensor("oh_all", [128, 256], F32))
        w_all = st.enter_context(nc.sbuf_tensor("w_all", [128, 32], F32))
        sc1 = st.enter_context(nc.sbuf_tensor("sc1", [128, 1], F32))
        sc2 = st.enter_context(nc.sbuf_tensor("sc2", [128, 1], F32))
        sc3 = st.enter_context(nc.sbuf_tensor("sc3", [128, 1], F32))
        totb = [st.enter_context(nc.sbuf_tensor(f"totb{i}", [128, 1], F32)) for i in range(2)]
        ltot = st.enter_context(nc.sbuf_tensor("ltot", [16, 1], F32))
        gath = st.enter_context(nc.sbuf_tensor("gath", [8, 16], F32))
        cpfx = st.enter_context(nc.sbuf_tensor("cpfx", [16, 1], F32))
        offrow = st.enter_context(nc.sbuf_tensor("offrow", [1, 256], F32))
        prefb = st.enter_context(nc.sbuf_tensor("prefb", [128, 16], F32))
        tmpf = [st.enter_context(nc.sbuf_tensor("tmpf0", [128, TQ], F32))] * 2
        mm8 = st.enter_context(nc.sbuf_tensor("mm8", [128, 8], F32))
        trif = st.enter_context(nc.sbuf_tensor("trif_s", [128, 128], F32))
        identf = st.enter_context(nc.sbuf_tensor("identf_s", [128, 128], F32))
        mask16f = st.enter_context(nc.sbuf_tensor("mask16f_s", [128, 16], F32))
        mask16fT = st.enter_context(nc.sbuf_tensor("mask16fT_s", [16, 128], F32))
        maskPfx16 = st.enter_context(nc.sbuf_tensor("maskPfx16_s", [128, 128], F32))
        maskJ16 = st.enter_context(nc.sbuf_tensor("maskJ16_s", [128, 128], F32))
        shift8 = st.enter_context(nc.sbuf_tensor("shift8_s", [16, 16], F32))
        oh127f = st.enter_context(nc.sbuf_tensor("oh127f_s", [128, 1], F32))
        sel8 = st.enter_context(nc.sbuf_tensor("sel8_s", [8, 8 * 128], BF16))
        mask_lt = st.enter_context(nc.sbuf_tensor("mask_lt_s", [8, 1], F32))
        wg_s = st.enter_context(nc.sbuf_tensor("wg_s", [128, NC * E], F32))
        b1_p = st.enter_context(nc.sbuf_tensor("b1_p_s", [128, E * 32], F32))
        b2_p = st.enter_context(nc.sbuf_tensor("b2_p_s", [128, E * NC], F32))
        print("sbuf left (moe):", nc.sbuf_bytes_remaining)

        G.wait_ge(s_mm.h, s_mm.n)
        G.wait_ge(s_dve.h, s_dve.n)
        G.wait_ge(s_act.h, s_act.n)
        Y.wait_ge(s_dve.h, s_dve.n)
        Y.wait_ge(s_act.h, s_act.n)
        Y.wait_ge(s_mm.h, s_mm.n)
        for name, t_ in (("trif", trif), ("identf", identf),
                         ("mask16f", mask16f), ("mask16fT", mask16fT),
                         ("maskPfx16", maskPfx16), ("maskJ16", maskJ16),
                         ("shift8", shift8), ("oh127f", oh127f), ("sel8", sel8),
                         ("mask_lt", mask_lt), ("b2_p", b2_p), ("b1_p", b1_p)):
            s_io.mark(Y.dma_start(out=t_[:, :], in_=dm[name].ap()), 16)
        s_io.mark(Y.dma_start(out=wg_s[:, :].rearrange("p (c e) -> p c e", e=E),
                              in_=dm["Wg"].ap().rearrange("(c p) e -> p c e", p=128)), 16)

        cast_n = cast_xb()
        s_io.wait(T)

        for tc in range(NTC):
            tsl = slice(128 * tc, 128 * (tc + 1))
            b5 = t_take_bank(5)
            zp = b5[:, 0:8]
            for c in range(NC):
                i = T.matmul(zp, x[c][:, tsl], wg_s[:, E * c:E * (c + 1)],
                             start=(c == 0), stop=(c == NC - 1))
            s_mm.mark(i)
            s_mm.wait(V)
            i = V.reduce_max(sc1[:, :], zp, axis=AX, negate=True)
            s_dve.mark(i)
            s_dve.wait(A)
            ia = A.activation(gate_e[:, :], zp, AF.Exp, bias=sc1[:, :])
            s_act.mark(ia)
            evac_mark(5, s_act)
            s_act.wait(V)
            V.reduce_sum(sc3[:, :], gate_e[:, :], axis=AX)
            vseq(V.reduce_max(sc1[:, :], gate_e[:, :], axis=AX))
            vseq(V.tensor_scalar(oh_all[:, 16 * tc:16 * tc + 8], gate_e[:, :], sc1[:, :], None, OP.is_equal))
            vseq(V.tensor_scalar(mm8[:, :], oh_all[:, 16 * tc:16 * tc + 8], -1.0, 1.0, OP.mult, OP.add))
            vseq(V.tensor_tensor(mm8[:, :], gate_e[:, :], mm8[:, :], OP.mult))
            vseq(V.reduce_max(sc2[:, :], mm8[:, :], axis=AX))
            vseq(V.tensor_scalar(oh_all[:, 16 * tc + 8:16 * tc + 16], mm8[:, :], sc2[:, :], None, OP.is_equal))
            vseq(V.tensor_scalar(sc3[:, :], sc3[:, :], 1e-9, None, OP.mult))
            vseq(V.tensor_tensor(sc3[:, :], sc3[:, :], sc1[:, :], OP.add))
            vseq(V.tensor_tensor(sc3[:, :], sc3[:, :], sc2[:, :], OP.add))
            vseq(V.reciprocal(sc3[:, :], sc3[:, :]))
            V.tensor_tensor(w_all[:, 2 * tc:2 * tc + 1], sc1[:, :], sc3[:, :], OP.mult)
            i = V.tensor_tensor(w_all[:, 2 * tc + 1:2 * tc + 2], sc2[:, :], sc3[:, :], OP.mult)
            s_dve.mark(i)
            b6 = t_take_bank(6)
            s_dve.wait(T)
            i = T.matmul(b6[:, 0:16], trif[:, :], oh_all[:, 16 * tc:16 * (tc + 1)],
                         start=True, stop=True)
            s_mm.mark(i)
            s_mm.wait(V)
            i = V.tensor_copy(cs_all[:, 16 * tc:16 * (tc + 1)], b6[:, 0:16])
            s_dve.mark(i)
            evac_mark(6, s_dve)
        route_n = s_dve.n

        b7 = t_take_bank(7)
        T.wait_ge(s_dve.h, route_n)
        for half in range(2):
            i = T.matmul(b7[:, half:half + 1], cs_all[:, 128 * half:128 * (half + 1)],
                         oh127f[:, :], start=True, stop=True)
            s_mm.mark(i)
        s_mm.wait(V)
        V.tensor_copy(totb[0][:, :], b7[:, 0:1])
        i = V.tensor_copy(totb[1][:, :], b7[:, 1:2])
        s_dve.mark(i)
        evac_mark(7, s_dve)
        tot_n = s_dve.n
        b7 = t_take_bank(7)
        T.wait_ge(s_dve.h, tot_n)
        for half in range(2):
            i = T.matmul(b7[0:16, 2:3], mask16f[:, :], totb[half][:, :],
                         start=(half == 0), stop=(half == 1))
        s_mm.mark(i)
        s_mm.wait(V)
        i = V.tensor_copy(ltot[:, :], b7[0:16, 2:3])
        s_dve.mark(i)
        evac_mark(7, s_dve)
        G.wait_ge(s_dve.h, s_dve.n)
        s_w.mark(G.dma_start(out=cc_in.ap(), in_=ltot[:, :]), 16)
        G.wait_ge(s_w.h, s_w.n)
        ic = G.collective_compute(
            "AllGather", OP.bypass, replica_groups=[list(range(NCORES))],
            ins=[cc_in.ap()], outs=[cc_out.ap()])
        s_cc.mark(ic)
        G.wait_ge(s_cc.h, s_cc.n)
        gw = s_w.mark(G.dma_start(out=gath[:, :], in_=cc_out.ap()), 16)
        b7 = t_take_bank(7)
        T.wait_ge(s_w.h, gw)
        i = T.matmul(b7[0:16, 0:1], gath[:, :], mask_lt[:, :], start=True, stop=True)
        s_mm.mark(i)
        i = T.matmul(b7[0:16, 1:2], gath[:, :], onescf[0:8, 0:1], start=True, stop=True)
        s_mm.mark(i)
        s_mm.wait(V)
        vseq(V.tensor_copy(cpfx[:, :], b7[0:16, 0:1]))
        i = V.tensor_scalar(ltot[:, :], b7[0:16, 1:2], 2048.0, None, OP.min)
        s_dve.mark(i)
        evac_mark(7, s_dve)
        b7 = t_take_bank(7)
        T.wait_ge(s_dve.h, s_dve.n)
        i = T.matmul(b7[0:16, 3:4], shift8[:, :], ltot[:, :], start=True, stop=True)
        s_mm.mark(i)
        s_mm.wait(V)
        i = vseq(V.tensor_tensor(cpfx[:, :], cpfx[:, :], b7[0:16, 3:4], OP.add))
        i = s_dve
        evac_mark(7, s_dve)
        core_n = s_dve.n
        b7 = t_take_bank(7)
        T.wait_ge(s_dve.h, core_n)
        T.matmul(b7[:, 4:5], maskPfx16[:, :], totb[0][:, :], start=True, stop=False)
        i = T.matmul(b7[:, 4:5], mask16fT[:, :], cpfx[:, :], start=False, stop=True)
        s_mm.mark(i)
        T.matmul(b7[:, 5:6], maskPfx16[:, :], totb[1][:, :], start=True, stop=False)
        T.matmul(b7[:, 5:6], maskJ16[:, :], totb[0][:, :], start=False, stop=False)
        i = T.matmul(b7[:, 5:6], mask16fT[:, :], cpfx[:, :], start=False, stop=True)
        s_mm.mark(i)
        s_mm.wait(V)
        V.tensor_copy(totb[0][:, :], b7[:, 4:5])
        i = V.tensor_copy(totb[1][:, :], b7[:, 5:6])
        s_dve.mark(i)
        evac_mark(7, s_dve)
        off_n = s_dve.n
        for half in range(2):
            b7 = t_take_bank(7)
            T.wait_ge(s_dve.h, s_dve.n)
            i = T.matmul(b7[0:1, 0:128], totb[half][:, :], identf[:, :],
                         start=True, stop=True)
            s_mm.mark(i)
            s_mm.wait(V)
            i = V.tensor_copy(offrow[0:1, 128 * half:128 * (half + 1)], b7[0:1, 0:128])
            s_dve.mark(i)
            evac_mark(7, s_dve)

        for tc in range(NTC):
            half = tc // 8
            b7 = t_take_bank(7)
            T.wait_ge(s_dve.h, s_dve.n)
            i = T.matmul(b7[:, 0:16], trif[0:1, :],
                         offrow[0:1, 16 * tc:16 * (tc + 1)],
                         start=True, stop=True)
            s_mm.mark(i)
            s_mm.wait(V)
            vseq(V.tensor_tensor(prefb[:, :], cs_all[:, 16 * tc:16 * (tc + 1)], b7[:, 0:16], OP.add))
            vseq(V.tensor_scalar(prefb[:, :], prefb[:, :], 2048.5, None, OP.is_lt))
            vseq(V.tensor_tensor(prefb[:, :], prefb[:, :], oh_all[:, 16 * tc:16 * (tc + 1)], OP.mult))
            vseq(V.tensor_scalar(mm8[:, :], prefb[:, 0:8], w_all[:, 2 * tc:2 * tc + 1], None, OP.mult))
            vseq(V.tensor_scalar(prefb[:, 8:16], prefb[:, 8:16], w_all[:, 2 * tc + 1:2 * tc + 2], None, OP.mult))
            vseq(V.tensor_tensor(mm8[:, :], mm8[:, :], prefb[:, 8:16], OP.add))
            evac_mark(7, s_dve)
            i = V.tensor_copy(cbc[:, 8 * tc:8 * (tc + 1)], mm8[:, :])
            s_dve.mark(i)
            b6 = t_take_bank(6)
            s_dve.wait(T)
            i = T.matmul(b6[0:8, 0:128], cbc[:, 8 * tc:8 * (tc + 1)], identb[:, :],
                         start=True, stop=True)
            s_mm.mark(i)
            s_mm.wait(A)
            ia = A.activation(cT[0:8, 128 * tc:128 * (tc + 1)], b6[0:8, 0:128], AF.Copy)
            s_act.mark(ia)
            evac_mark(6, s_act)
        if DEV_FORCE_COMBINE1:
            s_act.wait(V)
            i = V.memset(cT[:, :], 1.0)
            s_dve.mark(i)
            s_dve.wait(A)
            ia = A.activation(cT[:, :], cT[:, :], AF.Copy)
            s_act.mark(ia)
        ct_n = s_act.n
        if dbg is not None:
            s_dve.wait(Y); s_act.wait(Y)
            s_dve.wait(G); s_act.wait(G); s_w.wait(G)
            _nn = nc.allow_non_contiguous_dma(reason="dbg"); _nn.__enter__()
            s_st.mark(Y.dma_start(out=dbg.ap()[0:8, 0:16], in_=gath[:, :]), 16)
            s_st.mark(Y.dma_start(out=dbg.ap()[0:16, 16:17], in_=ltot[:, :]), 16)
            s_st.mark(Y.dma_start(out=dbg.ap()[0:16, 17:18], in_=cpfx[:, :]), 16)
            s_st.mark(Y.dma_start(out=dbg.ap()[0:1, 128:384], in_=offrow[:, :]), 16)
            s_st.mark(Y.dma_start(out=dbg.ap()[:, 384:640], in_=cs_all[:, :]), 16)
            s_st.mark(Y.dma_start(out=dbg.ap()[:, 640:896], in_=oh_all[:, :]), 16)
            s_st.mark(Y.dma_start(out=dbg.ap()[:, 896:928], in_=w_all[:, :]), 16)
            s_st.mark(G.dma_start(out=dbg.ap()[0:8, 1024:3072], in_=cT[:, :]), 16)
            _nn.__exit__(None, None, None)

        w1_use = [0, 0, 0, 0]
        w2_use = [0, 0, 0, 0]
        for e in range(E):
            s_act.wait(T, ct_n)
            for t4 in range(NT):
                bi = 4 + (t4 % 2)
                bank = t_take_bank(bi)
                i = T.matmul(bank[:, :], sel8[:, 128 * e:128 * (e + 1)],
                             cT[:, TQ * t4:TQ * (t4 + 1)], start=True, stop=True)
                s_mm.mark(i)
                s_mm.wait(A)
                ia = A.activation(cbc[:, TQ * t4:TQ * (t4 + 1)], bank[:, :], AF.Copy)
                s_act.mark(ia)
                evac_mark(bi, s_act)
            cbc_n = s_act.n
            for th in range(2):
                hof = 1024 * th
                for hc in range(32):
                    rb = hc % 4
                    G.wait_ge(s_mm.h, w1_use[rb])
                    wn = s_w.mark(G.dma_start(
                        out=w1r[rb][:, :].rearrange("p (c n) -> p c n", n=128),
                        in_=dm["W1"].ap()[e, :, 128 * hc:128 * (hc + 1)]
                            .rearrange("(c p) n -> p c n", p=128)), 16)
                    T.wait_ge(s_w.h, wn)
                    for tt in range(2):
                        tsl = slice(hof + TQ * tt, hof + TQ * (tt + 1))
                        bi = tt
                        bank = t_take_bank(bi)
                        for c in range(NC):
                            i = T.matmul(bank[:, :], w1r[rb][:, 128 * c:128 * (c + 1)],
                                         xb[c][:, tsl], start=(c == 0), stop=(c == NC - 1))
                        s_mm.mark(i)
                        s_mm.wait(A)
                        ia = A.activation(h_sb[hc][:, TQ * tt:TQ * (tt + 1)], bank[:, :],
                                          AF.Gelu_apprx_tanh, bias=b1_p[:, 32 * e + hc:32 * e + hc + 1])
                        s_act.mark(ia)
                        evac_mark(bi, s_act)
                    w1_use[rb] = s_mm.n
                h_n = s_act.n
                for dc in range(NC):
                    for wh in range(2):
                        rb = (dc * 2 + wh) % 4
                        G.wait_ge(s_mm.h, w2_use[rb])
                        wn = s_w.mark(G.dma_start(
                            out=w2r[rb][:, :].rearrange("p (c n) -> p c n", n=128),
                            in_=dm["W2"].ap()[e, 2048 * wh:2048 * (wh + 1), 128 * dc:128 * (dc + 1)]
                                .rearrange("(c p) n -> p c n", p=128)), 16)
                        if wh == 0:
                            wn0 = wn
                        w2_use[rb] = max(w2_use[rb], 1)
                    A_done = h_n
                    T.wait_ge(s_act.h, A_done)
                    for tt in range(2):
                        tsl = slice(hof + TQ * tt, hof + TQ * (tt + 1))
                        bi = 2 + ((dc * 2 + tt) % 2)
                        bank = t_take_bank(bi)
                        for hc in range(32):
                            if hc == 0:
                                T.wait_ge(s_w.h, wn0)
                            if hc == 16:
                                T.wait_ge(s_w.h, wn)
                            rb = (dc * 2 + hc // 16) % 4
                            i = T.matmul(bank[:, :], w2r[rb][:, 128 * (hc % 16):128 * (hc % 16 + 1)],
                                         h_sb[hc][:, TQ * tt:TQ * (tt + 1)],
                                         start=(hc == 0), stop=(hc == 31))
                        s_mm.mark(i)
                        if tt == 1:
                            w2_use[(dc * 2) % 4] = s_mm.n
                            w2_use[(dc * 2 + 1) % 4] = s_mm.n
                        s_mm.wait(V)
                        s_act.wait(V, cbc_n)
                        V.scalar_tensor_tensor(tmpf[tt][:, :], bank[:, :],
                                               b2_p[:, 8 * e + dc:8 * e + dc + 1],
                                               cbc[:, tsl], OP.add, OP.mult)
                        iv = V.tensor_tensor(x[dc][:, tsl], x[dc][:, tsl], tmpf[tt][:, :], OP.add)
                        s_dve.mark(iv)
                        evac_mark(bi, s_dve)

    with contextlib.ExitStack() as st2, nc.named_scope("l0_moe"):
        l0_moe(st2)
    with nc.named_scope("ln1"):
        postln()
    store_tap("x2")

    def l1_mla(st):
        cq = [st.enter_context(nc.sbuf_tensor(f"cq{i}", [128, S], BF16)) for i in range(2)]
        ckv = [st.enter_context(nc.sbuf_tensor(f"ckv{i}", [128, S], BF16)) for i in range(2)]
        kr64 = st.enter_context(nc.sbuf_tensor("kr64", [96, S], BF16))
        vtm1 = [st.enter_context(nc.sbuf_tensor(f"vtm1_{t}", [128, D], BF16)) for t in range(16)]
        q96 = [st.enter_context(nc.sbuf_tensor(f"q96_{i}", [96, S], BF16)) for i in range(2)]
        k96 = [st.enter_context(nc.sbuf_tensor(f"k96_{i}", [96, S], BF16)) for i in range(2)]
        ctxp1 = [st.enter_context(nc.sbuf_tensor(f"ctxp1_{i}", [128, S], BF16))
                 for i in range(2)]
        pt1 = [st.enter_context(nc.sbuf_tensor(f"pt1_{k}", [128, TQ], BF16)) for k in range(3)]
        rsc = st.enter_context(nc.sbuf_tensor("rsc0", [128, S], BF16))
        rcos1 = st.enter_context(nc.sbuf_tensor("rcos1_s", [128, S], BF16))
        rsin1 = st.enter_context(nc.sbuf_tensor("rsin1_s", [128, S], BF16))
        stA = contextlib.ExitStack()
        wdq = stA.enter_context(nc.sbuf_tensor("wdq", [128, 2048], BF16))
        wdkv = stA.enter_context(nc.sbuf_tensor("wdkv", [128, 2048], BF16))
        wuv = stA.enter_context(nc.sbuf_tensor("wuv", [128, 2048], BF16))
        wkr2 = stA.enter_context(nc.sbuf_tensor("wkr2", [128, 256], BF16))
        print("sbuf left (l1 mla A):", nc.sbuf_bytes_remaining)

        G.wait_ge(s_mm.h, s_mm.n)
        G.wait_ge(s_dve.h, s_dve.n)
        G.wait_ge(s_act.h, s_act.n)
        Y.wait_ge(s_dve.h, s_dve.n)
        Y.wait_ge(s_act.h, s_act.n)
        Y.wait_ge(s_mm.h, s_mm.n)
        s_io.mark(Y.dma_start(out=rcos1[:, :], in_=dm["rcos1"].ap()), 16)
        s_io.mark(Y.dma_start(out=rsin1[:, :], in_=dm["rsin1"].ap()), 16)
        cast0 = cast_xb()
        s_io.wait(V)

        ldq = s_w.mark(G.dma_start(out=wdq[:, :].rearrange("p (c m) -> p c m", m=256),
                                   in_=dm["W_DQ"].ap().rearrange("(c p) m -> p c m", p=128)), 16)
        ldkv = s_w.mark(G.dma_start(out=wdkv[:, :].rearrange("p (c m) -> p c m", m=256),
                                    in_=dm["W_DKV"].ap().rearrange("(c p) m -> p c m", p=128)), 16)
        lkr = s_w.mark(G.dma_start(out=wkr2[:, :].rearrange("p (c m) -> p c m", m=32),
                                   in_=dm["W_KR"].ap().rearrange("(c p) m -> p c m", p=128)), 16)
        luv = s_w.mark(G.dma_start(out=wuv[:, :].rearrange("p (c m) -> p c m", m=1024),
                                   in_=dm["W_UV"].ap().rearrange("(c p) m -> p c m", p=128)), 16)
        lwo = luq = luq2 = lqr = None

        s_dve.wait(T)
        T.wait_ge(s_w.h, ldkv)
        for which, (w_, dst) in enumerate(((wdq, cq), (wdkv, ckv))):
            for m2 in range(2):
                for t in range(NT):
                    tsl = slice(TQ * t, TQ * (t + 1))
                    bi = 4 + (t % 2)
                    bank = t_take_bank(bi)
                    for c in range(NC):
                        i = T.matmul(bank[:, :], w_[:, 256 * c + 128 * m2:256 * c + 128 * (m2 + 1)],
                                     xb[c][:, tsl], start=(c == 0), stop=(c == NC - 1))
                    s_mm.mark(i)
                    s_mm.wait(A)
                    ia = A.activation(dst[m2][:, tsl], bank[:, :], AF.Copy)
                    s_act.mark(ia)
                    evac_mark(bi, s_act)

        T.wait_ge(s_w.h, lkr)
        for t in range(NT):
            tsl = slice(TQ * t, TQ * (t + 1))
            bi = 4 + (t % 2)
            bank = t_take_bank(bi)
            for c in range(NC):
                i = T.matmul(bank[64:96, :], wkr2[:, 32 * c:32 * (c + 1)],
                             xb[c][:, tsl], start=(c == 0), stop=(c == NC - 1))
            s_mm.mark(i)
            s_mm.wait(A)
            ia = A.activation(kr64[64:96, tsl], bank[64:96, :], AF.Copy)
            s_act.mark(ia)
            evac_mark(bi, s_act)
        s_act.wait(V)
        rope_inplace(kr64, 32, rcos1, rsin1, rsc, base=64)
        kr_n = s_dve.n

        T.wait_ge(s_w.h, luv)
        s_act.wait(T)
        for tt in range(16):
            tsl = slice(128 * tt, 128 * (tt + 1))
            for half in range(2):
                bi = (2 * tt + half) % 4
                bank = t_take_bank(bi)
                for c2 in range(2):
                    i = T.matmul(bank[:, :], ckv[c2][:, tsl],
                                 wuv[:, 1024 * c2 + 512 * half:1024 * c2 + 512 * (half + 1)],
                                 start=(c2 == 0), stop=(c2 == 1))
                s_mm.mark(i)
                s_mm.wait(A)
                ia = A.activation(vtm1[tt][:, 512 * half:512 * (half + 1)], bank[:, :], AF.Copy)
                s_act.mark(ia)
                evac_mark(bi, s_act)

        stA.close()
        wqc = st.enter_context(nc.sbuf_tensor("wqc", [128, 2 * 1536], BF16))
        wuk = st.enter_context(nc.sbuf_tensor("wuk", [128, 2048], BF16))
        wob = [st.enter_context(nc.sbuf_tensor(f"wob{i}", [128, 1024], BF16))
               for i in range(2)]
        print("sbuf left (l1 mla B):", nc.sbuf_bytes_remaining)
        G.wait_ge(s_mm.h, s_mm.n)
        luq = s_w.mark(G.dma_start(out=wqc[:, :].rearrange("p (c m) -> p c m", m=1536),
                                   in_=dm["Wqcat"].ap().rearrange("(c p) m -> p c m", p=128)), 16)
        lqr = s_w.mark(G.dma_start(out=wuk[:, :].rearrange("p (c m) -> p c m", m=1024),
                                    in_=dm["W_UK"].ap().rearrange("(c p) m -> p c m", p=128)), 16)
        wob_use = [0, 0]
        wob_n = [0, 0]

        def load_wob(p):
            rb = p % 2
            G.wait_ge(s_mm.h, wob_use[rb])
            wob_n[rb] = s_w.mark(G.dma_start(
                out=wob[rb][:, :].rearrange("p (c m) -> p c m", m=128),
                in_=dm["W_O"].ap()[128 * p:128 * (p + 1), :]
                    .rearrange("r (c m) -> r c m", m=128)), 16)
        load_wob(0)
        T.wait_ge(s_w.h, lqr)
        scale1 = 1.0 / math.sqrt(96.0)
        CLO = float(np.exp(np.float64(-80.0) / np.sqrt(np.float64(96.0))))
        CHI = float(np.exp(np.float64(80.0) / np.sqrt(np.float64(96.0))))
        gen_n = [0, 0]

        def l1_gen_head(h, defer_v=False):
            hb = h % 2
            for t in range(NT):
                tsl = slice(TQ * t, TQ * (t + 1))
                bi = 7
                bank = t_take_bank(bi)
                for c2 in range(2):
                    i = T.matmul(bank[0:96, :], wqc[:, 1536 * c2 + 96 * h:1536 * c2 + 96 * (h + 1)],
                                 cq[c2][:, tsl], start=(c2 == 0), stop=(c2 == 1))
                s_mm.mark(i)
                A.wait_ge(s_mm.h, s_mm.n)
                ia = A.activation(q96[hb][0:96, tsl], bank[0:96, :], AF.Copy)
                s_act.mark(ia)
                evac_mark(bi, s_act)
                bi2 = 0
                bank2 = t_take_bank(bi2)
                for c2 in range(2):
                    i = T.matmul(bank2[0:64, :], wuk[:, 1024 * c2 + 64 * h:1024 * c2 + 64 * (h + 1)],
                                 ckv[c2][:, tsl], start=(c2 == 0), stop=(c2 == 1))
                s_mm.mark(i)
                A.wait_ge(s_mm.h, s_mm.n)
                ia = A.activation(k96[hb][0:64, tsl], bank2[0:64, :], AF.Copy)
                s_act.mark(ia)
                evac_mark(bi2, s_act)
            if not defer_v:
                _gen_head_v(h)

        def _gen_head_v(h):
            hb = h % 2
            s_act.wait(V)
            V.wait_ge(s_dve.h, kr_n)
            i = V.tensor_copy(k96[hb][64:96, :], kr64[64:96, :])
            s_dve.mark(i)
            rope_inplace(q96[hb], 32, rcos1, rsin1, rsc, base=64)
            gen_n[hb] = s_dve.n

        l1_gen_head(0)
        pt_reader = [0] * 3
        blk1 = 0
        pend = {"p": 0, "rb": 0, "ctx_n": 0, "waited": False, "tiles": []}
        xb_cast_n = {(q_, o_): cast0 for q_ in range(NT) for o_ in range(NC)}

        def emit_proj_tile(bi):
            if not pend["tiles"]:
                return
            if not pend["waited"]:
                T.wait_ge(s_dve.h, pend["ctx_n"])
                T.wait_ge(s_w.h, wob_n[pend["rb"]])
                pend["waited"] = True
            qt_, oc_ = pend["tiles"].pop(0)
            qsl_ = slice(TQ * qt_, TQ * (qt_ + 1))
            bank = t_take_bank(bi)
            T.wait_ge(s_dve.h, xb_cast_n[(qt_, oc_)])
            T.matmul(bank[:, :], wob[pend["rb"]][:, 128 * oc_:128 * (oc_ + 1)],
                     ctxp1[pend["p"] % 2][:, qsl_], start=True, stop=False)
            i = T.matmul(bank[:, :], identb[:, :], xb[oc_][:, qsl_],
                         start=False, stop=True)
            s_mm.mark(i)
            A.wait_ge(s_mm.h, s_mm.n)
            ia = A.activation(x[oc_][:, qsl_], bank[:, :], AF.Copy)
            s_act.mark(ia)
            evac_mark(bi, s_act)
            V.wait_ge(s_act.h, s_act.n)
            iv = V.tensor_copy(xb[oc_][:, qsl_], x[oc_][:, qsl_])
            s_dve.mark(iv)
            xb_cast_n[(qt_, oc_)] = s_dve.n
            if not pend["tiles"]:
                wob_use[pend["rb"]] = s_mm.n

        for head in range(NH):
            hb = head % 2
            hh = head % 2
            hof = 64 * hh
            hsl = slice(hof, hof + 64)
            genv_chunks = []
            if head + 1 < NH:
                l1_gen_head(head + 1, defer_v=True)
                genv_chunks = [(j, (head + 1) % 2) for j in range(NT)]
            T.wait_ge(s_dve.h, gen_n[hb])
            for qt in range(NT):
                qsl = slice(TQ * qt, TQ * (qt + 1))
                nkc = 4 * qt + 4
                dbi, cbi = (3, 4) if blk1 % 2 == 0 else (5, 6)
                blk1 += 1
                dps = t_take_bank(dbi)[hsl, :]
                cps = t_take_bank(cbi)[hsl, :]
                pt_ready = [0] * 3

                def emit_S1(kc):
                    bi = kc % 3
                    bank = t_take_bank(bi)
                    i = T.matmul(bank[:, :], k96[hb][:, 128 * kc:128 * (kc + 1)],
                                 q96[hb][:, qsl], start=True, stop=True)
                    s_mm.mark(i)
                    A.wait_ge(s_mm.h, s_mm.n)
                    A.wait_ge(s_mm.h, pt_reader[kc % 3])
                    ia = A.activation(pt1[kc % 3][:, :], bank[:, :], AF.Exp, scale=scale1)
                    s_act.mark(ia)
                    evac_mark(bi, s_act)
                    s_act.wait(V)
                    i = V.tensor_scalar(pt1[kc % 3][:, :], pt1[kc % 3][:, :],
                                        CLO, CHI, OP.max, OP.min)
                    s_dve.mark(i)
                    s_ = kc - 4 * qt
                    if s_ >= 0:
                        blk = slice(128 * s_, 128 * (s_ + 1))
                        i = V.tensor_tensor(pt1[kc % 3][:, blk], pt1[kc % 3][:, blk],
                                            tri[:, :], OP.mult)
                        if s_ > 0:
                            i = V.memset(pt1[kc % 3][:, 0:128 * s_], 0.0)
                        s_dve.mark(i)
                    pt_ready[kc % 3] = s_dve.n

                emit_S1(0)
                if nkc > 1:
                    emit_S1(1)
                for kc in range(nkc):
                    if kc + 2 < nkc:
                        emit_S1(kc + 2)
                    T.wait_ge(s_dve.h, pt_ready[kc % 3])
                    T.matmul(dps, ones[:, 0:64], pt1[kc % 3][:, :],
                             start=(kc == 0), stop=(kc == nkc - 1))
                    i = T.matmul(cps, vtm1[kc][:, 64 * head:64 * (head + 1)],
                                 pt1[kc % 3][:, :], start=(kc == 0), stop=(kc == nkc - 1))
                    s_mm.mark(i)
                    pt_reader[kc % 3] = s_mm.n
                s_mm.wait(V)
                V.reciprocal(dps, dps)
                cdst = ctxp1[(head // 2) % 2]
                V.tensor_copy(cdst[hsl, qsl], cps)
                i = V.tensor_tensor(cdst[hsl, qsl], cdst[hsl, qsl], dps, OP.mult)
                s_dve.mark(i)
                evac_mark(dbi, s_dve)
                evac_mark(cbi, s_dve)
                ob1, ob2 = (5, 6) if dbi == 3 else (3, 4)
                for b_ in (7, ob1, ob2, 7):
                    emit_proj_tile(b_)
                for _ in range(2):
                    if genv_chunks:
                        if len(genv_chunks) == NT:
                            s_act.wait(V)
                            V.wait_ge(s_dve.h, kr_n)
                        j_, hb_ = genv_chunks.pop(0)
                        csl_ = slice(TQ * j_, TQ * (j_ + 1))
                        V.tensor_copy(k96[hb_][64:96, csl_], kr64[64:96, csl_])
                        n_ = rope_cols(q96[hb_], rcos1, rsin1, rsc, csl_,
                                       base=64, nrows=32)
                        if not genv_chunks:
                            gen_n[hb_] = n_
            if hh == 1:
                p = head // 2
                while pend["tiles"]:
                    emit_proj_tile(7)
                if p < 7:
                    load_wob(p + 1)
                pend.update(p=p, rb=p % 2, ctx_n=s_dve.n, waited=False,
                            tiles=[(q_, o_) for q_ in range(NT) for o_ in range(NC)])
                if DEV_FORCE_FLUSH:
                    while pend["tiles"]:
                        emit_proj_tile(7)
        rot = 0
        while pend["tiles"]:
            emit_proj_tile((3, 4, 5, 6, 7)[rot % 5])
            rot += 1

    with contextlib.ExitStack() as st3, nc.named_scope("l1_mla"):
        l1_mla(st3)
    with nc.named_scope("ln2"):
        postln()
    store_tap("x3")

    def l1_ffn(st):
        win = st.enter_context(nc.sbuf_tensor("win", [128, NC * 64], BF16))
        wout = st.enter_context(nc.sbuf_tensor("wout", [64, 1024], BF16))
        s_sb = st.enter_context(nc.sbuf_tensor("s_sb", [64, S], F32))
        gA = st.enter_context(nc.sbuf_tensor("gA", [64, S], F32))
        gB = st.enter_context(nc.sbuf_tensor("gB", [64, S], F32))
        gbf = st.enter_context(nc.sbuf_tensor("gbf", [64, S], BF16))
        h_i = [st.enter_context(nc.sbuf_tensor(f"h_i{k}", [64, S], F32))
               for k in range(2)]
        print("sbuf left (l1 ffn):", nc.sbuf_bytes_remaining)

        G.wait_ge(s_mm.h, s_mm.n)
        G.wait_ge(s_dve.h, s_dve.n)
        G.wait_ge(s_act.h, s_act.n)
        cast_xb()
        lwin = s_w.mark(G.dma_start(out=win[:, :].rearrange("p (c m) -> p c m", m=64),
                                    in_=dm["Win"].ap().rearrange("(c p) m -> p c m", p=128)), 16)
        lwout = s_w.mark(G.dma_start(out=wout[:, :], in_=dm["Wout"].ap()), 16)

        s_dve.wait(T)
        T.wait_ge(s_w.h, lwin)
        for t in range(NT):
            tsl = slice(TQ * t, TQ * (t + 1))
            bi = 4 + (t % 2)
            bank = t_take_bank(bi)
            for c in range(NC):
                i = T.matmul(bank[0:64, :], win[:, 64 * c:64 * (c + 1)],
                             xb[c][:, tsl], start=(c == 0), stop=(c == NC - 1))
            s_mm.mark(i)
            s_mm.wait(A)
            ia = A.activation(s_sb[:, tsl], bank[0:64, :], AF.Identity, bias=binc_s[:, :])
            s_act.mark(ia)
            evac_mark(bi, s_act)
        ia = A.activation(gA[:, :], s_sb[:, :], AF.Identity, bias=b2f_s[:, :], scale=0.0)
        s_act.mark(ia)
        gs = [gA, gB]
        h_marks = [0, 0]
        for it in range(16):
            hb_ = it % 2
            A.wait_ge(s_dve.h, h_marks[hb_])
            ia = A.activation(h_i[hb_][:, :], s_sb[:, :], AF.Relu,
                              scale=w1f_s[:, it:it + 1], bias=b1f_s[:, it:it + 1])
            s_act.mark(ia)
            s_act.wait(V)
            dst = gbf if it == 15 else gs[(it + 1) % 2]
            iv = V.scalar_tensor_tensor(dst[:, :], h_i[hb_][:, :], w2f_s[:, it:it + 1],
                                        gs[it % 2][:, :], OP.mult, OP.add)
            s_dve.mark(iv)
            h_marks[hb_] = s_dve.n
        gbf_n = s_dve.n
        T.wait_ge(s_w.h, lwout)
        T.wait_ge(s_dve.h, gbf_n)
        for qt in range(NT):
            qsl = slice(TQ * qt, TQ * (qt + 1))
            for oc in range(NC):
                bi = 4 + (oc % 2)
                bank = t_take_bank(bi)
                i = T.matmul(bank[:, :], wout[:, 128 * oc:128 * (oc + 1)],
                             gbf[:, qsl], start=True, stop=True)
                s_mm.mark(i)
                s_mm.wait(V)
                iv = V.scalar_tensor_tensor(x[oc][:, qsl], bank[:, :],
                                            bout_p[:, oc:oc + 1], x[oc][:, qsl],
                                            OP.add, OP.add)
                s_dve.mark(iv)
                evac_mark(bi, s_dve)

    with contextlib.ExitStack() as st4, nc.named_scope("l1_ffn"):
        l1_ffn(st4)
    with nc.named_scope("ln3"):
        postln()

    s_dve.wait(Y)
    s_act.wait(Y)
    for c in range(NC):
        s_st.mark(Y.dma_start(out=out_x.ap()[128 * c:128 * (c + 1), :], in_=x[c][:, :]), 16)
    s_st.wait(Y)
    return nc


def prep_inputs(inputs):
    WqT = np.ascontiguousarray(inputs["L0_attn_Wq"].T)
    WkT = np.ascontiguousarray(inputs["L0_attn_Wk"].T)
    WvT = np.ascontiguousarray(inputs["L0_attn_Wv"].T)
    WoT = np.ascontiguousarray(inputs["L0_attn_Wo"].T)
    W_UQ = inputs["L1_attn_W_UQ"]; W_QR = inputs["L1_attn_W_QR"]
    Wqcat = np.zeros((256, NH * 96), np.float32)
    for h in range(NH):
        Wqcat[:, 96 * h:96 * h + 64] = W_UQ[:, 64 * h:64 * (h + 1)]
        Wqcat[:, 96 * h + 64:96 * (h + 1)] = W_QR[:, 32 * h:32 * (h + 1)]

    def rope_tab(nrows, period, d_rope):
        j = np.arange(nrows)
        r = (j % period) // 2
        inv = 10000.0 ** (-2.0 * r / d_rope)
        ang = inv[:, None] * np.arange(S, dtype=np.float64)[None, :]
        cosm = np.cos(ang)
        sinm = np.sin(ang) * np.where(j % 2 == 0, -1.0, 1.0)[:, None]
        return _bfa(cosm), _bfa(sinm)

    rcos0, rsin0 = rope_tab(128, 64, 64)
    rcos1, rsin1 = rope_tab(128, 32, 32)
    tri = _bfa(np.triu(np.ones((128, 128))))
    onesm = _bfa(np.ones((128, 128)))
    ident = _bfa(np.eye(128))
    trif = np.triu(np.ones((128, 128), np.float32))
    identf = np.eye(128, dtype=np.float32)
    jj = np.arange(128)
    mask16f = ((jj[:, None] % 16) == np.arange(16)[None, :]).astype(np.float32)
    mask16fT = np.ascontiguousarray(mask16f.T[:, :128]).astype(np.float32)
    maskPfx16 = (((jj[:, None] % 16) == (jj[None, :] % 16)) &
                 ((jj[:, None] // 16) < (jj[None, :] // 16))).astype(np.float32)
    maskJ16 = ((jj[:, None] % 16) == (jj[None, :] % 16)).astype(np.float32)
    shift8 = np.zeros((16, 16), np.float32)
    for i_ in range(8):
        shift8[i_, i_ + 8] = 1.0
    oh127f = np.zeros((128, 1), np.float32); oh127f[127, 0] = 1.0
    sel8 = np.zeros((8, 8 * 128), np.float32)
    for e in range(E):
        sel8[e, 128 * e:128 * (e + 1)] = 1.0
    sel8 = _bfa(sel8)

    def pcol(v):
        return np.ascontiguousarray(np.asarray(v, np.float32).reshape(NC, 128).T)

    shared = dict(
        WqT=WqT, WkT=WkT, WvT=WvT, WoT=WoT,
        bq_p=pcol(inputs["L0_attn_bq"]), bk_p=pcol(inputs["L0_attn_bk"]),
        bo_p=pcol(inputs["L0_attn_bo"]), bv=inputs["L0_attn_bv"][None, :],
        Wg=inputs["L0_ffn_Wg"], W1=inputs["L0_ffn_W1"], b1=inputs["L0_ffn_b1"],
        W2=inputs["L0_ffn_W2"], b2=inputs["L0_ffn_b2"],
        b1_p=np.ascontiguousarray(
            inputs["L0_ffn_b1"].reshape(E * H_FF // 128, 128).T),
        W_DQ=inputs["L1_attn_W_DQ"], W_DKV=inputs["L1_attn_W_DKV"],
        W_UQ=inputs["L1_attn_W_UQ"], W_QR=inputs["L1_attn_W_QR"], Wqcat=Wqcat,
        W_UK=inputs["L1_attn_W_UK"], W_UV=inputs["L1_attn_W_UV"],
        W_KR=inputs["L1_attn_W_KR"], W_O=inputs["L1_attn_W_O"],
        Win=inputs["L1_ffn_Win"], binc=inputs["L1_ffn_bin_"][:, None],
        Wout=inputs["L1_ffn_Wout"],
        W1f=inputs["L1_ffn_W1"], b1f=inputs["L1_ffn_b1"], W2f=inputs["L1_ffn_W2"],
        b2f=inputs["L1_ffn_b2"][:, None], bout_p=pcol(inputs["L1_ffn_bout"]),
        tri=tri, ones=onesm, ident=ident, trif=trif, identf=identf,
        mask16f=mask16f, mask16fT=mask16fT, maskPfx16=maskPfx16, maskJ16=maskJ16,
        shift8=shift8, oh127f=oh127f, sel8=sel8,
        b2_p=np.ascontiguousarray(inputs["L0_ffn_b2"].reshape(E * NC, 128).T),
        rcos0=rcos0, rsin0=rsin0, rcos1=rcos1, rsin1=rsin1,
    )
    shared = {k: np.ascontiguousarray(np.asarray(v, dtype=np.float32)
                                      if np.asarray(v).dtype == np.float64 else v)
              for k, v in shared.items()}
    in_maps = []
    xfull = inputs["L0_attn_x"]
    for i in range(NCORES):
        m = dict(shared)
        m["x_in"] = np.ascontiguousarray(np.asarray(xfull[i]).T.astype(np.float32))
        lt = np.zeros((8, 1), np.float32); lt[:i, 0] = 1.0
        m["mask_lt"] = lt
        in_maps.append(m)
    return in_maps


def kernel(**inputs):
    in_maps = prep_inputs(inputs)
    nc = build_program(DEV_TAPS)
    res = run_bass_kernel_spmd(nc, in_maps, core_ids=list(range(NCORES)))
    out = np.stack([res.results[i]["out_x"].T for i in range(NCORES)])
    if DEV_TAPS:
        kernel.tap_results = [
            {t: res.results[i]["tap_" + t].T for t in DEV_TAPS} for i in range(NCORES)
        ]
    return np.ascontiguousarray(out, dtype=np.float32)


# version 13
# speedup vs baseline: 1.6842x; 1.0002x over previous
import math
import contextlib
import numpy as np
import ml_dtypes
import concourse.bass as bass
import concourse.mybir as mybir
from concourse.bass_utils import run_bass_kernel_spmd

F32 = mybir.dt.float32
BF16 = mybir.dt.bfloat16
OP = mybir.AluOpType
AF = mybir.ActivationFunctionType
AX = mybir.AxisListType.X

NCORES = 8
S = 2048
D = 1024
NH = 16
DK = 64
TQ = 512
NT = S // TQ
NC = D // 128
E = 8
H_FF = 4096

DEV_TAPS = []
DEV_FORCE_COMBINE1 = False
DEV_FORCE_FLUSH = False
SHUF_EO = []
for _g in range(16):
    SHUF_EO += [2 * _g + 1, 2 * _g]


def _bfa(a):
    return np.asarray(a, dtype=np.float64).astype(ml_dtypes.bfloat16)


class Sem:
    def __init__(self, nc, name):
        self.h = nc.alloc_semaphore(name=name)
        self.n = 0

    def mark(self, inst, k=1):
        inst.then_inc(self.h, k)
        self.n += k
        return self.n

    def wait(self, eng, at=None):
        v = self.n if at is None else at
        if v > 0:
            eng.wait_ge(self.h, v)


def build_program(taps):
    global DEV_FORCE_COMBINE1
    nc = bass.Bass("TRN2", target_bir_lowering=False)
    T, V, A, G, Y = nc.tensor, nc.vector, nc.scalar, nc.gpsimd, nc.sync
    s_w = Sem(nc, "s_w")
    s_io = Sem(nc, "s_io")
    s_mm = Sem(nc, "s_mm")
    s_act = Sem(nc, "s_act")
    s_dve = Sem(nc, "s_dve")
    s_cc = Sem(nc, "s_cc")
    s_st = Sem(nc, "s_st")

    dm = {}

    def din(name, shape, dt=F32):
        dm[name] = nc.dram_tensor(name, list(shape), dt, kind="ExternalInput")
        return dm[name]

    din("x_in", (D, S))
    din("WqT", (D, D)); din("WkT", (D, D)); din("WvT", (D, D)); din("WoT", (D, D))
    din("bq_p", (128, NC)); din("bk_p", (128, NC)); din("bo_p", (128, NC))
    din("bv", (1, D))
    din("Wg", (D, E))
    din("W1", (E, D, H_FF)); din("b1", (E, H_FF))
    din("W2", (E, H_FF, D)); din("b2", (E, D))
    din("W_DQ", (D, 256)); din("W_DKV", (D, 256))
    din("W_UQ", (256, D)); din("W_QR", (256, 512)); din("Wqcat", (256, NH * 96))
    din("W_UK", (256, D)); din("W_UV", (256, D))
    din("W_KR", (D, 32))
    din("W_O", (D, D))
    din("Win", (D, 64)); din("binc", (64, 1)); din("Wout", (64, D))
    din("W1f", (64, 16)); din("b1f", (64, 16)); din("W2f", (64, 16))
    din("b2f", (64, 1)); din("bout_p", (128, NC))
    din("b1_p", (128, E * H_FF // 128))
    din("tri", (128, 128), BF16)
    din("ones", (128, 128), BF16)
    din("ident", (128, 128), BF16)
    din("trif", (128, 128))
    din("identf", (128, 128))
    din("mask16f", (128, 16))
    din("mask16fT", (16, 128))
    din("maskPfx16", (128, 128))
    din("maskJ16", (128, 128))
    din("shift8", (16, 16))
    din("oh127f", (128, 1))
    din("sel8", (8, 8 * 128), BF16)
    din("mask_lt", (8, 1))
    din("b2_p", (128, E * NC // 1))
    din("rcos0", (128, S), BF16); din("rsin0", (128, S), BF16)
    din("rcos1", (128, S), BF16); din("rsin1", (128, S), BF16)

    out_x = nc.dram_tensor("out_x", [D, S], F32, kind="ExternalOutput")
    dbg = nc.dram_tensor("dbg", [128, 4096], F32, kind="ExternalOutput") if "dbg" in taps else None
    cc_in = nc.dram_tensor("cc_in", [1, 16], F32, kind="Internal")
    cc_out = nc.dram_tensor("cc_out", [8, 16], F32, kind="Internal")
    tap_dram = {t: nc.dram_tensor("tap_" + t, [D, S], F32, kind="ExternalOutput")
                for t in taps}

    def sb(name, shape, dt=BF16):
        return nc.alloc_sbuf_tensor(name, list(shape), dt)

    x = [sb(f"x{c}", [128, S], F32) for c in range(NC)]
    xb = [sb(f"xb{c}", [128, S], BF16) for c in range(NC)]
    tri = sb("tri_s", [128, 128]); ones = sb("ones_s", [128, 128])
    onescf = sb("onescf_s", [128, 1], F32)
    w1f_s = sb("w1f_s", [64, 16], F32); b1f_s = sb("b1f_s", [64, 16], F32)
    w2f_s = sb("w2f_s", [64, 16], F32); binc_s = sb("binc_s", [64, 1], F32)
    b2f_s = sb("b2f_s", [64, 1], F32); bout_p = sb("bout_p_s", [128, NC], F32)
    bq_p = sb("bq_p_s", [128, NC], F32); bk_p = sb("bk_p_s", [128, NC], F32)
    bo_p = sb("bo_p_s", [128, NC], F32)
    bvr = sb("bv_s", [1, D])
    identb = sb("identb_s", [128, 128])
    epsln = sb("epsln", [128, 1], F32)

    for c in range(NC):
        s_io.mark(Y.dma_start(out=x[c][:, :], in_=dm["x_in"].ap()[128 * c:128 * (c + 1), :]), 16)
    for name, t_ in (("tri", tri), ("ones", ones), ("ident", identb),
                     ("bq_p", bq_p), ("bk_p", bk_p), ("bo_p", bo_p),
                     ("W1f", w1f_s), ("b1f", b1f_s), ("W2f", w2f_s),
                     ("binc", binc_s), ("b2f", b2f_s), ("bout_p", bout_p)):
        s_io.mark(Y.dma_start(out=t_[:, :], in_=dm[name].ap()), 16)
    s_w.mark(G.dma_start(out=bvr[:, :], in_=dm["bv"].ap()), 16)
    s_dve.mark(V.memset(epsln[:, :], 1e-5))
    s_dve.mark(V.memset(onescf[:, :], 1.0))
    io_consts = s_io.n
    w_consts = s_w.n
    eps_n = s_dve.n

    ps = [nc.alloc_psum_tensor(f"ps{i}", [128, TQ], F32) for i in range(8)]
    bank_free = [(s_mm, 0)] * 8

    def t_take_bank(i):
        sem, n = bank_free[i]
        sem.wait(T, n)
        return ps[i]

    def evac_mark(i, sem):
        bank_free[i] = (sem, sem.n)

    def vseq(inst):
        s_dve.mark(inst)
        s_dve.wait(V)
        return s_dve.n

    def cast_xb():
        s_act.wait(V)
        for c in range(NC):
            i = V.tensor_copy(xb[c][:, :], x[c][:, :])
        s_dve.mark(i)
        return s_dve.n

    def rope_cols(z, cosT, sinT, shufbuf, csl, base=0, nrows=128):
        sl = slice(base, base + nrows)
        V.stream_shuffle(shufbuf[sl, csl], z[sl, csl], SHUF_EO)
        V.tensor_tensor(z[sl, csl], z[sl, csl], cosT[sl, csl], OP.mult)
        V.tensor_tensor(shufbuf[sl, csl], shufbuf[sl, csl], sinT[sl, csl], OP.mult)
        i = V.tensor_tensor(z[sl, csl], z[sl, csl], shufbuf[sl, csl], OP.add)
        s_dve.mark(i)
        return s_dve.n

    def rope_inplace(z, nrows, cosT, sinT, shufbuf, base=0):
        sl = slice(base, base + nrows)
        V.stream_shuffle(shufbuf[sl, :], z[sl, :], SHUF_EO)
        V.tensor_tensor(z[sl, :], z[sl, :], cosT[sl, :], OP.mult)
        V.tensor_tensor(shufbuf[sl, :], shufbuf[sl, :], sinT[sl, :], OP.mult)
        i = V.tensor_tensor(z[sl, :], z[sl, :], shufbuf[sl, :], OP.add)
        s_dve.mark(i)
        return s_dve.n

    def postln():
        with contextlib.ExitStack() as stl:
            uid = str(s_dve.n)
            sqb = [stl.enter_context(nc.sbuf_tensor(f"sqb{uid}_{i}", [128, TQ], BF16))
                   for i in range(2)]
            mtile = [stl.enter_context(nc.sbuf_tensor(f"mt{uid}_{i}", [128, TQ], F32))
                     for i in range(2)]
            ritile = [stl.enter_context(nc.sbuf_tensor(f"rt{uid}_{i}", [128, TQ], F32))
                      for i in range(2)]
            s_act.wait(V)
            s_dve.wait(A)

            def ln_copies(t):
                tsl = slice(TQ * t, TQ * (t + 1))
                for c in range(NC):
                    i1 = A.activation(xb[c][:, tsl], x[c][:, tsl], AF.Copy)
                s_act.mark(i1)
                return s_act.n

            def ln_apply(t, tb):
                tsl = slice(TQ * t, TQ * (t + 1))
                for c in range(NC):
                    V.tensor_tensor(x[c][:, tsl], x[c][:, tsl], mtile[tb][:, :],
                                    OP.subtract)
                    i7 = V.tensor_tensor(x[c][:, tsl], x[c][:, tsl], ritile[tb][:, :],
                                         OP.mult)
                s_dve.mark(i7)

            cp_n = [0] * NT
            cp_n[0] = ln_copies(0)
            sq_free = [(s_mm, 0), (s_mm, 0)]
            for t in range(NT):
                tb = t % 2
                tsl = slice(TQ * t, TQ * (t + 1))
                for c in range(NC):
                    sem, n = sq_free[c % 2]
                    sem.wait(A, n)
                    i2 = A.activation(sqb[c % 2][:, :], xb[c][:, tsl], AF.Square)
                    s_act.mark(i2)
                    if c == 0:
                        b6 = t_take_bank(6 if tb == 0 else 4)
                        b7 = t_take_bank(7 if tb == 0 else 5)
                        T.wait_ge(s_act.h, cp_n[t])
                    T.matmul(b6[:, :], ones[:, :], xb[c][:, tsl],
                             start=(c == 0), stop=(c == NC - 1))
                    s_act.wait(T)
                    i3 = T.matmul(b7[:, :], ones[:, :], sqb[c % 2][:, :],
                                  start=(c == 0), stop=(c == NC - 1))
                    s_mm.mark(i3)
                    sq_free[c % 2] = (s_mm, s_mm.n)
                s_mm.wait(V)
                V.tensor_scalar(mtile[tb][:, :], b6[:, :], 1.0 / D, None, OP.mult)
                V.tensor_scalar(ritile[tb][:, :], b7[:, :], 1.0 / D, None, OP.mult)
                V.tensor_tensor(b6[:, :], mtile[tb][:, :], mtile[tb][:, :], OP.mult)
                i4 = V.tensor_tensor(ritile[tb][:, :], ritile[tb][:, :], b6[:, :],
                                     OP.subtract)
                s_dve.mark(i4)
                evac_mark(6 if tb == 0 else 4, s_dve)
                evac_mark(7 if tb == 0 else 5, s_dve)
                s_dve.wait(A)
                i5 = A.activation(ritile[tb][:, :], ritile[tb][:, :], AF.Sqrt,
                                  bias=epsln[:, :])
                s_act.mark(i5)
                s_act.wait(V)
                V.reciprocal(ritile[tb][:, :], ritile[tb][:, :])
                if t + 1 < NT:
                    cp_n[t + 1] = ln_copies(t + 1)
                if t >= 1:
                    ln_apply(t - 1, (t - 1) % 2)
            ln_apply(NT - 1, (NT - 1) % 2)
        return s_dve.n

    def store_tap(name):
        if name in tap_dram:
            s_dve.wait(Y)
            s_act.wait(Y)
            for c in range(NC):
                s_st.mark(Y.dma_start(out=tap_dram[name].ap()[128 * c:128 * (c + 1), :],
                                      in_=x[c][:, :]), 16)

    def l0_attention(st):
        wbig = st.enter_context(nc.sbuf_tensor("wbig", [128, NC * 1024], BF16))
        vtm = [st.enter_context(nc.sbuf_tensor(f"vtm{t}", [128, D], BF16)) for t in range(16)]
        qa = [st.enter_context(nc.sbuf_tensor(f"qa{i}", [128, S], BF16)) for i in range(2)]
        ka = [st.enter_context(nc.sbuf_tensor(f"ka{i}", [128, S], BF16)) for i in range(2)]
        wqr = [st.enter_context(nc.sbuf_tensor(f"wqr{i}", [128, NC * 128], BF16)) for i in range(2)]
        pt = [st.enter_context(nc.sbuf_tensor(f"pt{k}", [128, TQ], BF16)) for k in range(3)]
        ctxa = [st.enter_context(nc.sbuf_tensor(f"ctxa{c}", [128, S], BF16)) for c in range(6)]
        stA = st.enter_context(contextlib.ExitStack())
        shufb = stA.enter_context(nc.sbuf_tensor("shufb", [128, S], BF16))
        rcos0 = stA.enter_context(nc.sbuf_tensor("rcos0_s", [128, S], BF16))
        rsin0 = stA.enter_context(nc.sbuf_tensor("rsin0_s", [128, S], BF16))
        s_io.mark(Y.dma_start(out=rcos0[:, :], in_=dm["rcos0"].ap()), 16)
        s_io.mark(Y.dma_start(out=rsin0[:, :], in_=dm["rsin0"].ap()), 16)
        print("sbuf left (l0 attn):", nc.sbuf_bytes_remaining)

        s_io.wait(V, io_consts)
        cast_xb()

        wv_n = s_w.mark(G.dma_start(
            out=wbig[:, :].rearrange("p (c n) -> p c n", n=1024),
            in_=dm["WvT"].ap().rearrange("(c p) n -> p c n", p=128)), 16)
        T.wait_ge(s_w.h, wv_n)
        s_dve.wait(T)
        s_io.wait(T, io_consts)
        for tt in range(16):
            tsl = slice(128 * tt, 128 * (tt + 1))
            for half in range(2):
                bi = (2 * tt + half) % 4
                bank = t_take_bank(bi)
                for c in range(NC):
                    T.matmul(bank[:, :], xb[c][:, tsl],
                             wbig[:, 1024 * c + 512 * half:1024 * c + 512 * (half + 1)],
                             start=(c == 0), stop=False)
                i = T.matmul(bank[:, :], ones[0:1, 0:128],
                             bvr[0:1, 512 * half:512 * (half + 1)], start=False, stop=True)
                s_mm.mark(i)
                s_mm.wait(A)
                ia = A.activation(vtm[tt][:, 512 * half:512 * (half + 1)], bank[:, :], AF.Copy)
                s_act.mark(ia)
                evac_mark(bi, s_act)

        G.wait_ge(s_mm.h, s_mm.n)
        wo_n = s_w.mark(G.dma_start(
            out=wbig[:, :].rearrange("p (c n) -> p c n", n=1024),
            in_=dm["WoT"].ap().rearrange("(c p) n -> p c n", p=128)), 16)

        ring_use = [0, 0]
        qk_rope_n = [0, 0]
        wld = [[0] * 8, [0] * 8]

        def l0_load_pair(p):
            csl = slice(128 * p, 128 * (p + 1))
            for wi, wname in enumerate(("WqT", "WkT")):
                G.wait_ge(s_mm.h, ring_use[wi])
                wld[wi][p] = s_w.mark(G.dma_start(
                    out=wqr[wi][:, :].rearrange("p (c n) -> p c n", n=128),
                    in_=dm[wname].ap()[:, csl].rearrange("(c p) n -> p c n", p=128)), 16)

        def l0_gen_pair(p, defer_rope=False):
            pb = p % 2
            for wi, (dst, bias_) in enumerate(((qa[pb], bq_p), (ka[pb], bk_p))):
                ring = wqr[wi]
                T.wait_ge(s_w.h, wld[wi][p])
                for t in range(NT):
                    tsl = slice(TQ * t, TQ * (t + 1))
                    bi = 7 if t % 2 == 0 else 0
                    bank = t_take_bank(bi)
                    for c in range(NC):
                        i = T.matmul(bank[:, :], ring[:, 128 * c:128 * (c + 1)],
                                     xb[c][:, tsl], start=(c == 0), stop=(c == NC - 1))
                    s_mm.mark(i)
                    A.wait_ge(s_mm.h, s_mm.n)
                    ia = A.activation(dst[:, tsl], bank[:, :], AF.Identity,
                                      bias=bias_[:, p:p + 1])
                    s_act.mark(ia)
                    evac_mark(bi, s_act)
                ring_use[wi] = s_mm.n
            if p + 1 < 8:
                l0_load_pair(p + 1)
            if not defer_rope:
                _rope_pair(p)

        def _rope_pair(p):
            pb = p % 2
            s_act.wait(V)
            rope_inplace(qa[pb], 128, rcos0, rsin0, shufb)
            rope_inplace(ka[pb], 128, rcos0, rsin0, shufb)
            qk_rope_n[pb] = s_dve.n

        l0_load_pair(0)
        l0_gen_pair(0)
        pt_reader = [0] * 3
        blk_i = 0
        for p in range(8):
            pb = p % 2
            rope_chunks = []
            if p + 1 < 8:
                dr = (p + 1 < 7)
                l0_gen_pair(p + 1, defer_rope=dr)
                if dr:
                    npb = (p + 1) % 2
                    rope_chunks = [(qa[npb], j) for j in range(NT)] + \
                                  [(ka[npb], j) for j in range(NT)]
            if p == 6:
                stA.close()
                for c in (6, 7):
                    ctxa.append(st.enter_context(nc.sbuf_tensor(f"ctxa{c}", [128, S], BF16)))
            T.wait_ge(s_dve.h, qk_rope_n[pb])
            for qt in range(NT):
                qsl = slice(TQ * qt, TQ * (qt + 1))
                nkc = 4 * qt + 4
                for hh in range(2):
                    head = 2 * p + hh
                    hsl = slice(64 * hh, 64 * hh + 64)
                    dbi, cbi = (3, 4) if blk_i % 2 == 0 else (5, 6)
                    blk_i += 1
                    dps = t_take_bank(dbi)[hsl, :]
                    cps = t_take_bank(cbi)[hsl, :]
                    pt_ready = [0] * 3

                    def emit_S(kc):
                        bi = kc % 3
                        bank = t_take_bank(bi)
                        i = T.matmul(bank[:, :], ka[pb][hsl, 128 * kc:128 * (kc + 1)],
                                     qa[pb][hsl, qsl], start=True, stop=True)
                        s_mm.mark(i)
                        A.wait_ge(s_mm.h, s_mm.n)
                        A.wait_ge(s_mm.h, pt_reader[kc % 3])
                        ia = A.activation(pt[kc % 3][:, :], bank[:, :], AF.Exp, scale=0.125)
                        s_act.mark(ia)
                        evac_mark(bi, s_act)
                        s_ = kc - 4 * qt
                        if s_ >= 0:
                            s_act.wait(V)
                            blk = slice(128 * s_, 128 * (s_ + 1))
                            i = V.tensor_tensor(pt[kc % 3][:, blk], pt[kc % 3][:, blk],
                                                tri[:, :], OP.mult)
                            if s_ > 0:
                                i = V.memset(pt[kc % 3][:, 0:128 * s_], 0.0)
                            s_dve.mark(i)
                            pt_ready[kc % 3] = ("dve", s_dve.n)
                        else:
                            pt_ready[kc % 3] = ("act", s_act.n)

                    emit_S(0)
                    if nkc > 1:
                        emit_S(1)
                    for kc in range(nkc):
                        if kc + 2 < nkc:
                            emit_S(kc + 2)
                        kind, n_ = pt_ready[kc % 3]
                        if kind == "dve":
                            T.wait_ge(s_dve.h, n_)
                        else:
                            T.wait_ge(s_act.h, n_)
                        T.matmul(dps, ones[:, hsl], pt[kc % 3][:, :],
                                 start=(kc == 0), stop=(kc == nkc - 1))
                        i = T.matmul(cps, vtm[kc][:, 64 * head:64 * (head + 1)],
                                     pt[kc % 3][:, :], start=(kc == 0), stop=(kc == nkc - 1))
                        s_mm.mark(i)
                        pt_reader[kc % 3] = s_mm.n
                    s_mm.wait(V)
                    V.tensor_scalar(dps, dps, 1e-8, None, OP.add)
                    V.reciprocal(dps, dps)
                    V.tensor_copy(ctxa[p][hsl, qsl], cps)
                    i = V.tensor_tensor(ctxa[p][hsl, qsl], ctxa[p][hsl, qsl], dps, OP.mult)
                    s_dve.mark(i)
                    evac_mark(dbi, s_dve)
                    evac_mark(cbi, s_dve)
                    for _ in range(2):
                        if rope_chunks:
                            if len(rope_chunks) == 2 * NT:
                                s_act.wait(V)
                            z_, j_ = rope_chunks.pop(0)
                            n_ = rope_cols(z_, rcos0, rsin0, shufb,
                                           slice(TQ * j_, TQ * (j_ + 1)))
                            if not rope_chunks:
                                qk_rope_n[(p + 1) % 2] = n_
        ctx_n = s_dve.n
        T.wait_ge(s_dve.h, ctx_n)
        T.wait_ge(s_w.h, wo_n)
        for qt in range(NT):
            qsl = slice(TQ * qt, TQ * (qt + 1))
            for oc in range(NC):
                bi = 3 + (oc % 2)
                bank = t_take_bank(bi)
                for c in range(NC):
                    T.matmul(bank[:, :],
                             wbig[:, 1024 * c + 128 * oc:1024 * c + 128 * (oc + 1)],
                             ctxa[c][:, qsl], start=(c == 0), stop=False)
                i = T.matmul(bank[:, :], identb[:, :], xb[oc][:, qsl],
                             start=False, stop=True)
                s_mm.mark(i)
                A.wait_ge(s_mm.h, s_mm.n)
                ia = A.activation(x[oc][:, qsl], bank[:, :], AF.Identity,
                                  bias=bo_p[:, oc:oc + 1])
                s_act.mark(ia)
                evac_mark(bi, s_act)

    with contextlib.ExitStack() as st1, nc.named_scope("l0_attn"):
        l0_attention(st1)
    with nc.named_scope("ln0"):
        postln()
    store_tap("x1")

    def l0_moe(st):
        NTC = 16
        h_sb = [st.enter_context(nc.sbuf_tensor(f"hsb{i}", [128, 1024], BF16))
                for i in range(32)]
        w1r = [st.enter_context(nc.sbuf_tensor(f"w1r{i}", [128, 1024], BF16))
               for i in range(4)]
        w2r = [st.enter_context(nc.sbuf_tensor(f"w2r{i}", [128, 2048], BF16))
               for i in range(4)]
        cbc = st.enter_context(nc.sbuf_tensor("cbc", [128, S], BF16))
        cT = st.enter_context(nc.sbuf_tensor("cT", [8, S], BF16))
        gate_e = st.enter_context(nc.sbuf_tensor("gate_e", [128, 8], F32))
        cs_all = st.enter_context(nc.sbuf_tensor("cs_all", [128, 256], F32))
        oh_all = st.enter_context(nc.sbuf_tensor("oh_all", [128, 256], F32))
        w_all = st.enter_context(nc.sbuf_tensor("w_all", [128, 32], F32))
        sc1 = st.enter_context(nc.sbuf_tensor("sc1", [128, 1], F32))
        sc2 = st.enter_context(nc.sbuf_tensor("sc2", [128, 1], F32))
        sc3 = st.enter_context(nc.sbuf_tensor("sc3", [128, 1], F32))
        totb = [st.enter_context(nc.sbuf_tensor(f"totb{i}", [128, 1], F32)) for i in range(2)]
        ltot = st.enter_context(nc.sbuf_tensor("ltot", [16, 1], F32))
        gath = st.enter_context(nc.sbuf_tensor("gath", [8, 16], F32))
        cpfx = st.enter_context(nc.sbuf_tensor("cpfx", [16, 1], F32))
        offrow = st.enter_context(nc.sbuf_tensor("offrow", [1, 256], F32))
        prefb = st.enter_context(nc.sbuf_tensor("prefb", [128, 16], F32))
        tmpf = [st.enter_context(nc.sbuf_tensor("tmpf0", [128, TQ], F32))] * 2
        mm8 = st.enter_context(nc.sbuf_tensor("mm8", [128, 8], F32))
        trif = st.enter_context(nc.sbuf_tensor("trif_s", [128, 128], F32))
        identf = st.enter_context(nc.sbuf_tensor("identf_s", [128, 128], F32))
        mask16f = st.enter_context(nc.sbuf_tensor("mask16f_s", [128, 16], F32))
        mask16fT = st.enter_context(nc.sbuf_tensor("mask16fT_s", [16, 128], F32))
        maskPfx16 = st.enter_context(nc.sbuf_tensor("maskPfx16_s", [128, 128], F32))
        maskJ16 = st.enter_context(nc.sbuf_tensor("maskJ16_s", [128, 128], F32))
        shift8 = st.enter_context(nc.sbuf_tensor("shift8_s", [16, 16], F32))
        oh127f = st.enter_context(nc.sbuf_tensor("oh127f_s", [128, 1], F32))
        sel8 = st.enter_context(nc.sbuf_tensor("sel8_s", [8, 8 * 128], BF16))
        mask_lt = st.enter_context(nc.sbuf_tensor("mask_lt_s", [8, 1], F32))
        wg_s = st.enter_context(nc.sbuf_tensor("wg_s", [128, NC * E], F32))
        b1_p = st.enter_context(nc.sbuf_tensor("b1_p_s", [128, E * 32], F32))
        b2_p = st.enter_context(nc.sbuf_tensor("b2_p_s", [128, E * NC], F32))
        print("sbuf left (moe):", nc.sbuf_bytes_remaining)

        G.wait_ge(s_mm.h, s_mm.n)
        G.wait_ge(s_dve.h, s_dve.n)
        G.wait_ge(s_act.h, s_act.n)
        Y.wait_ge(s_dve.h, s_dve.n)
        Y.wait_ge(s_act.h, s_act.n)
        Y.wait_ge(s_mm.h, s_mm.n)
        for name, t_ in (("trif", trif), ("identf", identf),
                         ("mask16f", mask16f), ("mask16fT", mask16fT),
                         ("maskPfx16", maskPfx16), ("maskJ16", maskJ16),
                         ("shift8", shift8), ("oh127f", oh127f), ("sel8", sel8),
                         ("mask_lt", mask_lt), ("b2_p", b2_p), ("b1_p", b1_p)):
            s_io.mark(Y.dma_start(out=t_[:, :], in_=dm[name].ap()), 16)
        s_io.mark(Y.dma_start(out=wg_s[:, :].rearrange("p (c e) -> p c e", e=E),
                              in_=dm["Wg"].ap().rearrange("(c p) e -> p c e", p=128)), 16)

        cast_n = cast_xb()
        s_io.wait(T)

        for tc in range(NTC):
            tsl = slice(128 * tc, 128 * (tc + 1))
            b5 = t_take_bank(5)
            zp = b5[:, 0:8]
            for c in range(NC):
                i = T.matmul(zp, x[c][:, tsl], wg_s[:, E * c:E * (c + 1)],
                             start=(c == 0), stop=(c == NC - 1))
            s_mm.mark(i)
            s_mm.wait(V)
            i = V.reduce_max(sc1[:, :], zp, axis=AX, negate=True)
            s_dve.mark(i)
            s_dve.wait(A)
            ia = A.activation(gate_e[:, :], zp, AF.Exp, bias=sc1[:, :])
            s_act.mark(ia)
            evac_mark(5, s_act)
            s_act.wait(V)
            V.reduce_sum(sc3[:, :], gate_e[:, :], axis=AX)
            vseq(V.reduce_max(sc1[:, :], gate_e[:, :], axis=AX))
            vseq(V.tensor_scalar(oh_all[:, 16 * tc:16 * tc + 8], gate_e[:, :], sc1[:, :], None, OP.is_equal))
            vseq(V.tensor_scalar(mm8[:, :], oh_all[:, 16 * tc:16 * tc + 8], -1.0, 1.0, OP.mult, OP.add))
            vseq(V.tensor_tensor(mm8[:, :], gate_e[:, :], mm8[:, :], OP.mult))
            vseq(V.reduce_max(sc2[:, :], mm8[:, :], axis=AX))
            vseq(V.tensor_scalar(oh_all[:, 16 * tc + 8:16 * tc + 16], mm8[:, :], sc2[:, :], None, OP.is_equal))
            vseq(V.tensor_scalar(sc3[:, :], sc3[:, :], 1e-9, None, OP.mult))
            vseq(V.tensor_tensor(sc3[:, :], sc3[:, :], sc1[:, :], OP.add))
            vseq(V.tensor_tensor(sc3[:, :], sc3[:, :], sc2[:, :], OP.add))
            vseq(V.reciprocal(sc3[:, :], sc3[:, :]))
            V.tensor_tensor(w_all[:, 2 * tc:2 * tc + 1], sc1[:, :], sc3[:, :], OP.mult)
            i = V.tensor_tensor(w_all[:, 2 * tc + 1:2 * tc + 2], sc2[:, :], sc3[:, :], OP.mult)
            s_dve.mark(i)
            b6 = t_take_bank(6)
            s_dve.wait(T)
            i = T.matmul(b6[:, 0:16], trif[:, :], oh_all[:, 16 * tc:16 * (tc + 1)],
                         start=True, stop=True)
            s_mm.mark(i)
            s_mm.wait(V)
            i = V.tensor_copy(cs_all[:, 16 * tc:16 * (tc + 1)], b6[:, 0:16])
            s_dve.mark(i)
            evac_mark(6, s_dve)
        route_n = s_dve.n

        b7 = t_take_bank(7)
        T.wait_ge(s_dve.h, route_n)
        for half in range(2):
            i = T.matmul(b7[:, half:half + 1], cs_all[:, 128 * half:128 * (half + 1)],
                         oh127f[:, :], start=True, stop=True)
            s_mm.mark(i)
        s_mm.wait(V)
        V.tensor_copy(totb[0][:, :], b7[:, 0:1])
        i = V.tensor_copy(totb[1][:, :], b7[:, 1:2])
        s_dve.mark(i)
        evac_mark(7, s_dve)
        tot_n = s_dve.n
        b7 = t_take_bank(7)
        T.wait_ge(s_dve.h, tot_n)
        for half in range(2):
            i = T.matmul(b7[0:16, 2:3], mask16f[:, :], totb[half][:, :],
                         start=(half == 0), stop=(half == 1))
        s_mm.mark(i)
        s_mm.wait(V)
        i = V.tensor_copy(ltot[:, :], b7[0:16, 2:3])
        s_dve.mark(i)
        evac_mark(7, s_dve)
        G.wait_ge(s_dve.h, s_dve.n)
        s_w.mark(G.dma_start(out=cc_in.ap(), in_=ltot[:, :]), 16)
        G.wait_ge(s_w.h, s_w.n)
        ic = G.collective_compute(
            "AllGather", OP.bypass, replica_groups=[list(range(NCORES))],
            ins=[cc_in.ap()], outs=[cc_out.ap()])
        s_cc.mark(ic)
        G.wait_ge(s_cc.h, s_cc.n)
        gw = s_w.mark(G.dma_start(out=gath[:, :], in_=cc_out.ap()), 16)
        b7 = t_take_bank(7)
        T.wait_ge(s_w.h, gw)
        i = T.matmul(b7[0:16, 0:1], gath[:, :], mask_lt[:, :], start=True, stop=True)
        s_mm.mark(i)
        i = T.matmul(b7[0:16, 1:2], gath[:, :], onescf[0:8, 0:1], start=True, stop=True)
        s_mm.mark(i)
        s_mm.wait(V)
        vseq(V.tensor_copy(cpfx[:, :], b7[0:16, 0:1]))
        i = V.tensor_scalar(ltot[:, :], b7[0:16, 1:2], 2048.0, None, OP.min)
        s_dve.mark(i)
        evac_mark(7, s_dve)
        b7 = t_take_bank(7)
        T.wait_ge(s_dve.h, s_dve.n)
        i = T.matmul(b7[0:16, 3:4], shift8[:, :], ltot[:, :], start=True, stop=True)
        s_mm.mark(i)
        s_mm.wait(V)
        i = vseq(V.tensor_tensor(cpfx[:, :], cpfx[:, :], b7[0:16, 3:4], OP.add))
        i = s_dve
        evac_mark(7, s_dve)
        core_n = s_dve.n
        b7 = t_take_bank(7)
        T.wait_ge(s_dve.h, core_n)
        T.matmul(b7[:, 4:5], maskPfx16[:, :], totb[0][:, :], start=True, stop=False)
        i = T.matmul(b7[:, 4:5], mask16fT[:, :], cpfx[:, :], start=False, stop=True)
        s_mm.mark(i)
        T.matmul(b7[:, 5:6], maskPfx16[:, :], totb[1][:, :], start=True, stop=False)
        T.matmul(b7[:, 5:6], maskJ16[:, :], totb[0][:, :], start=False, stop=False)
        i = T.matmul(b7[:, 5:6], mask16fT[:, :], cpfx[:, :], start=False, stop=True)
        s_mm.mark(i)
        s_mm.wait(V)
        V.tensor_copy(totb[0][:, :], b7[:, 4:5])
        i = V.tensor_copy(totb[1][:, :], b7[:, 5:6])
        s_dve.mark(i)
        evac_mark(7, s_dve)
        off_n = s_dve.n
        for half in range(2):
            b7 = t_take_bank(7)
            T.wait_ge(s_dve.h, s_dve.n)
            i = T.matmul(b7[0:1, 0:128], totb[half][:, :], identf[:, :],
                         start=True, stop=True)
            s_mm.mark(i)
            s_mm.wait(V)
            i = V.tensor_copy(offrow[0:1, 128 * half:128 * (half + 1)], b7[0:1, 0:128])
            s_dve.mark(i)
            evac_mark(7, s_dve)

        for tc in range(NTC):
            half = tc // 8
            b7 = t_take_bank(7)
            T.wait_ge(s_dve.h, s_dve.n)
            i = T.matmul(b7[:, 0:16], trif[0:1, :],
                         offrow[0:1, 16 * tc:16 * (tc + 1)],
                         start=True, stop=True)
            s_mm.mark(i)
            s_mm.wait(V)
            vseq(V.tensor_tensor(prefb[:, :], cs_all[:, 16 * tc:16 * (tc + 1)], b7[:, 0:16], OP.add))
            vseq(V.tensor_scalar(prefb[:, :], prefb[:, :], 2048.5, None, OP.is_lt))
            vseq(V.tensor_tensor(prefb[:, :], prefb[:, :], oh_all[:, 16 * tc:16 * (tc + 1)], OP.mult))
            vseq(V.tensor_scalar(mm8[:, :], prefb[:, 0:8], w_all[:, 2 * tc:2 * tc + 1], None, OP.mult))
            vseq(V.tensor_scalar(prefb[:, 8:16], prefb[:, 8:16], w_all[:, 2 * tc + 1:2 * tc + 2], None, OP.mult))
            vseq(V.tensor_tensor(mm8[:, :], mm8[:, :], prefb[:, 8:16], OP.add))
            evac_mark(7, s_dve)
            i = V.tensor_copy(cbc[:, 8 * tc:8 * (tc + 1)], mm8[:, :])
            s_dve.mark(i)
            b6 = t_take_bank(6)
            s_dve.wait(T)
            i = T.matmul(b6[0:8, 0:128], cbc[:, 8 * tc:8 * (tc + 1)], identb[:, :],
                         start=True, stop=True)
            s_mm.mark(i)
            s_mm.wait(A)
            ia = A.activation(cT[0:8, 128 * tc:128 * (tc + 1)], b6[0:8, 0:128], AF.Copy)
            s_act.mark(ia)
            evac_mark(6, s_act)
        if DEV_FORCE_COMBINE1:
            s_act.wait(V)
            i = V.memset(cT[:, :], 1.0)
            s_dve.mark(i)
            s_dve.wait(A)
            ia = A.activation(cT[:, :], cT[:, :], AF.Copy)
            s_act.mark(ia)
        ct_n = s_act.n
        if dbg is not None:
            s_dve.wait(Y); s_act.wait(Y)
            s_dve.wait(G); s_act.wait(G); s_w.wait(G)
            _nn = nc.allow_non_contiguous_dma(reason="dbg"); _nn.__enter__()
            s_st.mark(Y.dma_start(out=dbg.ap()[0:8, 0:16], in_=gath[:, :]), 16)
            s_st.mark(Y.dma_start(out=dbg.ap()[0:16, 16:17], in_=ltot[:, :]), 16)
            s_st.mark(Y.dma_start(out=dbg.ap()[0:16, 17:18], in_=cpfx[:, :]), 16)
            s_st.mark(Y.dma_start(out=dbg.ap()[0:1, 128:384], in_=offrow[:, :]), 16)
            s_st.mark(Y.dma_start(out=dbg.ap()[:, 384:640], in_=cs_all[:, :]), 16)
            s_st.mark(Y.dma_start(out=dbg.ap()[:, 640:896], in_=oh_all[:, :]), 16)
            s_st.mark(Y.dma_start(out=dbg.ap()[:, 896:928], in_=w_all[:, :]), 16)
            s_st.mark(G.dma_start(out=dbg.ap()[0:8, 1024:3072], in_=cT[:, :]), 16)
            _nn.__exit__(None, None, None)

        w1_use = [0, 0, 0, 0]
        w2_use = [0, 0, 0, 0]
        for e in range(E):
            s_act.wait(T, ct_n)
            for t4 in range(NT):
                bi = 4 + (t4 % 2)
                bank = t_take_bank(bi)
                i = T.matmul(bank[:, :], sel8[:, 128 * e:128 * (e + 1)],
                             cT[:, TQ * t4:TQ * (t4 + 1)], start=True, stop=True)
                s_mm.mark(i)
                s_mm.wait(A)
                ia = A.activation(cbc[:, TQ * t4:TQ * (t4 + 1)], bank[:, :], AF.Copy)
                s_act.mark(ia)
                evac_mark(bi, s_act)
            cbc_n = s_act.n
            for th in range(2):
                hof = 1024 * th
                for hc in range(32):
                    rb = hc % 4
                    G.wait_ge(s_mm.h, w1_use[rb])
                    wn = s_w.mark(G.dma_start(
                        out=w1r[rb][:, :].rearrange("p (c n) -> p c n", n=128),
                        in_=dm["W1"].ap()[e, :, 128 * hc:128 * (hc + 1)]
                            .rearrange("(c p) n -> p c n", p=128)), 16)
                    T.wait_ge(s_w.h, wn)
                    for tt in range(2):
                        tsl = slice(hof + TQ * tt, hof + TQ * (tt + 1))
                        bi = tt
                        bank = t_take_bank(bi)
                        for c in range(NC):
                            i = T.matmul(bank[:, :], w1r[rb][:, 128 * c:128 * (c + 1)],
                                         xb[c][:, tsl], start=(c == 0), stop=(c == NC - 1))
                        s_mm.mark(i)
                        s_mm.wait(A)
                        ia = A.activation(h_sb[hc][:, TQ * tt:TQ * (tt + 1)], bank[:, :],
                                          AF.Gelu_apprx_tanh, bias=b1_p[:, 32 * e + hc:32 * e + hc + 1])
                        s_act.mark(ia)
                        evac_mark(bi, s_act)
                    w1_use[rb] = s_mm.n
                h_n = s_act.n
                for dc in range(NC):
                    for wh in range(2):
                        rb = (dc * 2 + wh) % 4
                        G.wait_ge(s_mm.h, w2_use[rb])
                        wn = s_w.mark(G.dma_start(
                            out=w2r[rb][:, :].rearrange("p (c n) -> p c n", n=128),
                            in_=dm["W2"].ap()[e, 2048 * wh:2048 * (wh + 1), 128 * dc:128 * (dc + 1)]
                                .rearrange("(c p) n -> p c n", p=128)), 16)
                        if wh == 0:
                            wn0 = wn
                        w2_use[rb] = max(w2_use[rb], 1)
                    A_done = h_n
                    T.wait_ge(s_act.h, A_done)
                    for tt in range(2):
                        tsl = slice(hof + TQ * tt, hof + TQ * (tt + 1))
                        bi = 2 + ((dc * 2 + tt) % 2)
                        bank = t_take_bank(bi)
                        for hc in range(32):
                            if hc == 0:
                                T.wait_ge(s_w.h, wn0)
                            if hc == 16:
                                T.wait_ge(s_w.h, wn)
                            rb = (dc * 2 + hc // 16) % 4
                            i = T.matmul(bank[:, :], w2r[rb][:, 128 * (hc % 16):128 * (hc % 16 + 1)],
                                         h_sb[hc][:, TQ * tt:TQ * (tt + 1)],
                                         start=(hc == 0), stop=(hc == 31))
                        s_mm.mark(i)
                        if tt == 1:
                            w2_use[(dc * 2) % 4] = s_mm.n
                            w2_use[(dc * 2 + 1) % 4] = s_mm.n
                        s_mm.wait(V)
                        s_act.wait(V, cbc_n)
                        V.scalar_tensor_tensor(tmpf[tt][:, :], bank[:, :],
                                               b2_p[:, 8 * e + dc:8 * e + dc + 1],
                                               cbc[:, tsl], OP.add, OP.mult)
                        iv = V.tensor_tensor(x[dc][:, tsl], x[dc][:, tsl], tmpf[tt][:, :], OP.add)
                        s_dve.mark(iv)
                        evac_mark(bi, s_dve)

    with contextlib.ExitStack() as st2, nc.named_scope("l0_moe"):
        l0_moe(st2)
    with nc.named_scope("ln1"):
        postln()
    store_tap("x2")

    def l1_mla(st):
        cq = [st.enter_context(nc.sbuf_tensor(f"cq{i}", [128, S], BF16)) for i in range(2)]
        ckv = [st.enter_context(nc.sbuf_tensor(f"ckv{i}", [128, S], BF16)) for i in range(2)]
        kr64 = st.enter_context(nc.sbuf_tensor("kr64", [96, S], BF16))
        vtm1 = [st.enter_context(nc.sbuf_tensor(f"vtm1_{t}", [128, D], BF16)) for t in range(16)]
        q96 = [st.enter_context(nc.sbuf_tensor(f"q96_{i}", [96, S], BF16)) for i in range(2)]
        k96 = [st.enter_context(nc.sbuf_tensor(f"k96_{i}", [96, S], BF16)) for i in range(2)]
        ctxp1 = [st.enter_context(nc.sbuf_tensor(f"ctxp1_{i}", [128, S], BF16))
                 for i in range(2)]
        pt1 = [st.enter_context(nc.sbuf_tensor(f"pt1_{k}", [128, TQ], BF16)) for k in range(3)]
        rsc = st.enter_context(nc.sbuf_tensor("rsc0", [128, S], BF16))
        rcos1 = st.enter_context(nc.sbuf_tensor("rcos1_s", [128, S], BF16))
        rsin1 = st.enter_context(nc.sbuf_tensor("rsin1_s", [128, S], BF16))
        stA = contextlib.ExitStack()
        wdq = stA.enter_context(nc.sbuf_tensor("wdq", [128, 2048], BF16))
        wdkv = stA.enter_context(nc.sbuf_tensor("wdkv", [128, 2048], BF16))
        wuv = stA.enter_context(nc.sbuf_tensor("wuv", [128, 2048], BF16))
        wkr2 = stA.enter_context(nc.sbuf_tensor("wkr2", [128, 256], BF16))
        print("sbuf left (l1 mla A):", nc.sbuf_bytes_remaining)

        G.wait_ge(s_mm.h, s_mm.n)
        G.wait_ge(s_dve.h, s_dve.n)
        G.wait_ge(s_act.h, s_act.n)
        Y.wait_ge(s_dve.h, s_dve.n)
        Y.wait_ge(s_act.h, s_act.n)
        Y.wait_ge(s_mm.h, s_mm.n)
        s_io.mark(Y.dma_start(out=rcos1[:, :], in_=dm["rcos1"].ap()), 16)
        s_io.mark(Y.dma_start(out=rsin1[:, :], in_=dm["rsin1"].ap()), 16)
        cast0 = cast_xb()
        s_io.wait(V)

        ldq = s_w.mark(G.dma_start(out=wdq[:, :].rearrange("p (c m) -> p c m", m=256),
                                   in_=dm["W_DQ"].ap().rearrange("(c p) m -> p c m", p=128)), 16)
        ldkv = s_w.mark(G.dma_start(out=wdkv[:, :].rearrange("p (c m) -> p c m", m=256),
                                    in_=dm["W_DKV"].ap().rearrange("(c p) m -> p c m", p=128)), 16)
        lkr = s_w.mark(G.dma_start(out=wkr2[:, :].rearrange("p (c m) -> p c m", m=32),
                                   in_=dm["W_KR"].ap().rearrange("(c p) m -> p c m", p=128)), 16)
        luv = s_w.mark(G.dma_start(out=wuv[:, :].rearrange("p (c m) -> p c m", m=1024),
                                   in_=dm["W_UV"].ap().rearrange("(c p) m -> p c m", p=128)), 16)
        lwo = luq = luq2 = lqr = None

        s_dve.wait(T)
        T.wait_ge(s_w.h, ldkv)
        for which, (w_, dst) in enumerate(((wdq, cq), (wdkv, ckv))):
            for m2 in range(2):
                for t in range(NT):
                    tsl = slice(TQ * t, TQ * (t + 1))
                    bi = 4 + (t % 2)
                    bank = t_take_bank(bi)
                    for c in range(NC):
                        i = T.matmul(bank[:, :], w_[:, 256 * c + 128 * m2:256 * c + 128 * (m2 + 1)],
                                     xb[c][:, tsl], start=(c == 0), stop=(c == NC - 1))
                    s_mm.mark(i)
                    s_mm.wait(A)
                    ia = A.activation(dst[m2][:, tsl], bank[:, :], AF.Copy)
                    s_act.mark(ia)
                    evac_mark(bi, s_act)

        T.wait_ge(s_w.h, lkr)
        for t in range(NT):
            tsl = slice(TQ * t, TQ * (t + 1))
            bi = 4 + (t % 2)
            bank = t_take_bank(bi)
            for c in range(NC):
                i = T.matmul(bank[64:96, :], wkr2[:, 32 * c:32 * (c + 1)],
                             xb[c][:, tsl], start=(c == 0), stop=(c == NC - 1))
            s_mm.mark(i)
            s_mm.wait(A)
            ia = A.activation(kr64[64:96, tsl], bank[64:96, :], AF.Copy)
            s_act.mark(ia)
            evac_mark(bi, s_act)
        s_act.wait(V)
        rope_inplace(kr64, 32, rcos1, rsin1, rsc, base=64)
        kr_n = s_dve.n

        T.wait_ge(s_w.h, luv)
        s_act.wait(T)
        for tt in range(16):
            tsl = slice(128 * tt, 128 * (tt + 1))
            for half in range(2):
                bi = (2 * tt + half) % 4
                bank = t_take_bank(bi)
                for c2 in range(2):
                    i = T.matmul(bank[:, :], ckv[c2][:, tsl],
                                 wuv[:, 1024 * c2 + 512 * half:1024 * c2 + 512 * (half + 1)],
                                 start=(c2 == 0), stop=(c2 == 1))
                s_mm.mark(i)
                s_mm.wait(A)
                ia = A.activation(vtm1[tt][:, 512 * half:512 * (half + 1)], bank[:, :], AF.Copy)
                s_act.mark(ia)
                evac_mark(bi, s_act)

        stA.close()
        wqc = st.enter_context(nc.sbuf_tensor("wqc", [128, 2 * 1536], BF16))
        wuk = st.enter_context(nc.sbuf_tensor("wuk", [128, 2048], BF16))
        wob = [st.enter_context(nc.sbuf_tensor(f"wob{i}", [128, 1024], BF16))
               for i in range(2)]
        print("sbuf left (l1 mla B):", nc.sbuf_bytes_remaining)
        G.wait_ge(s_mm.h, s_mm.n)
        luq = s_w.mark(G.dma_start(out=wqc[:, :].rearrange("p (c m) -> p c m", m=1536),
                                   in_=dm["Wqcat"].ap().rearrange("(c p) m -> p c m", p=128)), 16)
        lqr = s_w.mark(G.dma_start(out=wuk[:, :].rearrange("p (c m) -> p c m", m=1024),
                                    in_=dm["W_UK"].ap().rearrange("(c p) m -> p c m", p=128)), 16)
        wob_use = [0, 0]
        wob_n = [0, 0]

        def load_wob(p):
            rb = p % 2
            G.wait_ge(s_mm.h, wob_use[rb])
            wob_n[rb] = s_w.mark(G.dma_start(
                out=wob[rb][:, :].rearrange("p (c m) -> p c m", m=128),
                in_=dm["W_O"].ap()[128 * p:128 * (p + 1), :]
                    .rearrange("r (c m) -> r c m", m=128)), 16)
        load_wob(0)
        T.wait_ge(s_w.h, lqr)
        scale1 = 1.0 / math.sqrt(96.0)
        CLO = float(np.exp(np.float64(-80.0) / np.sqrt(np.float64(96.0))))
        CHI = float(np.exp(np.float64(80.0) / np.sqrt(np.float64(96.0))))
        gen_n = [0, 0]

        def l1_gen_head(h, defer_v=False):
            hb = h % 2
            for t in range(NT):
                tsl = slice(TQ * t, TQ * (t + 1))
                bi = 7
                bank = t_take_bank(bi)
                for c2 in range(2):
                    i = T.matmul(bank[0:96, :], wqc[:, 1536 * c2 + 96 * h:1536 * c2 + 96 * (h + 1)],
                                 cq[c2][:, tsl], start=(c2 == 0), stop=(c2 == 1))
                s_mm.mark(i)
                A.wait_ge(s_mm.h, s_mm.n)
                ia = A.activation(q96[hb][0:96, tsl], bank[0:96, :], AF.Copy)
                s_act.mark(ia)
                evac_mark(bi, s_act)
                bi2 = 0
                bank2 = t_take_bank(bi2)
                for c2 in range(2):
                    i = T.matmul(bank2[0:64, :], wuk[:, 1024 * c2 + 64 * h:1024 * c2 + 64 * (h + 1)],
                                 ckv[c2][:, tsl], start=(c2 == 0), stop=(c2 == 1))
                s_mm.mark(i)
                A.wait_ge(s_mm.h, s_mm.n)
                ia = A.activation(k96[hb][0:64, tsl], bank2[0:64, :], AF.Copy)
                s_act.mark(ia)
                evac_mark(bi2, s_act)
            if not defer_v:
                _gen_head_v(h)

        def _gen_head_v(h):
            hb = h % 2
            s_act.wait(V)
            V.wait_ge(s_dve.h, kr_n)
            i = V.tensor_copy(k96[hb][64:96, :], kr64[64:96, :])
            s_dve.mark(i)
            rope_inplace(q96[hb], 32, rcos1, rsin1, rsc, base=64)
            gen_n[hb] = s_dve.n

        l1_gen_head(0)
        pt_reader = [0] * 3
        blk1 = 0
        pend = {"p": 0, "rb": 0, "ctx_n": 0, "waited": False, "tiles": []}
        xb_cast_n = {(q_, o_): cast0 for q_ in range(NT) for o_ in range(NC)}

        def emit_proj_tile(bi):
            if not pend["tiles"]:
                return
            if not pend["waited"]:
                T.wait_ge(s_dve.h, pend["ctx_n"])
                T.wait_ge(s_w.h, wob_n[pend["rb"]])
                pend["waited"] = True
            qt_, oc_ = pend["tiles"].pop(0)
            qsl_ = slice(TQ * qt_, TQ * (qt_ + 1))
            bank = t_take_bank(bi)
            T.wait_ge(s_dve.h, xb_cast_n[(qt_, oc_)])
            T.matmul(bank[:, :], wob[pend["rb"]][:, 128 * oc_:128 * (oc_ + 1)],
                     ctxp1[pend["p"] % 2][:, qsl_], start=True, stop=False)
            i = T.matmul(bank[:, :], identb[:, :], xb[oc_][:, qsl_],
                         start=False, stop=True)
            s_mm.mark(i)
            A.wait_ge(s_mm.h, s_mm.n)
            ia = A.activation(x[oc_][:, qsl_], bank[:, :], AF.Copy)
            s_act.mark(ia)
            evac_mark(bi, s_act)
            V.wait_ge(s_act.h, s_act.n)
            iv = V.tensor_copy(xb[oc_][:, qsl_], x[oc_][:, qsl_])
            s_dve.mark(iv)
            xb_cast_n[(qt_, oc_)] = s_dve.n
            if not pend["tiles"]:
                wob_use[pend["rb"]] = s_mm.n

        for head in range(NH):
            hb = head % 2
            hh = head % 2
            hof = 64 * hh
            hsl = slice(hof, hof + 64)
            genv_chunks = []
            if head + 1 < NH:
                l1_gen_head(head + 1, defer_v=True)
                genv_chunks = [(j, (head + 1) % 2) for j in range(NT)]
            T.wait_ge(s_dve.h, gen_n[hb])
            for qt in range(NT):
                qsl = slice(TQ * qt, TQ * (qt + 1))
                nkc = 4 * qt + 4
                dbi, cbi = (3, 4) if blk1 % 2 == 0 else (5, 6)
                blk1 += 1
                dps = t_take_bank(dbi)[hsl, :]
                cps = t_take_bank(cbi)[hsl, :]
                pt_ready = [0] * 3

                def emit_S1(kc):
                    bi = kc % 3
                    bank = t_take_bank(bi)
                    i = T.matmul(bank[:, :], k96[hb][:, 128 * kc:128 * (kc + 1)],
                                 q96[hb][:, qsl], start=True, stop=True)
                    s_mm.mark(i)
                    A.wait_ge(s_mm.h, s_mm.n)
                    A.wait_ge(s_mm.h, pt_reader[kc % 3])
                    ia = A.activation(pt1[kc % 3][:, :], bank[:, :], AF.Exp, scale=scale1)
                    s_act.mark(ia)
                    evac_mark(bi, s_act)
                    s_act.wait(V)
                    i = V.tensor_scalar(pt1[kc % 3][:, :], pt1[kc % 3][:, :],
                                        CLO, CHI, OP.max, OP.min)
                    s_dve.mark(i)
                    s_ = kc - 4 * qt
                    if s_ >= 0:
                        blk = slice(128 * s_, 128 * (s_ + 1))
                        i = V.tensor_tensor(pt1[kc % 3][:, blk], pt1[kc % 3][:, blk],
                                            tri[:, :], OP.mult)
                        if s_ > 0:
                            i = V.memset(pt1[kc % 3][:, 0:128 * s_], 0.0)
                        s_dve.mark(i)
                    pt_ready[kc % 3] = s_dve.n

                emit_S1(0)
                if nkc > 1:
                    emit_S1(1)
                for kc in range(nkc):
                    if kc + 2 < nkc:
                        emit_S1(kc + 2)
                    T.wait_ge(s_dve.h, pt_ready[kc % 3])
                    T.matmul(dps, ones[:, 0:64], pt1[kc % 3][:, :],
                             start=(kc == 0), stop=(kc == nkc - 1))
                    i = T.matmul(cps, vtm1[kc][:, 64 * head:64 * (head + 1)],
                                 pt1[kc % 3][:, :], start=(kc == 0), stop=(kc == nkc - 1))
                    s_mm.mark(i)
                    pt_reader[kc % 3] = s_mm.n
                s_mm.wait(V)
                V.reciprocal(dps, dps)
                cdst = ctxp1[(head // 2) % 2]
                V.tensor_copy(cdst[hsl, qsl], cps)
                i = V.tensor_tensor(cdst[hsl, qsl], cdst[hsl, qsl], dps, OP.mult)
                s_dve.mark(i)
                evac_mark(dbi, s_dve)
                evac_mark(cbi, s_dve)
                ob1, ob2 = (5, 6) if dbi == 3 else (3, 4)
                for b_ in (7, ob1, ob2, 7):
                    emit_proj_tile(b_)
                for _ in range(2):
                    if genv_chunks:
                        if len(genv_chunks) == NT:
                            s_act.wait(V)
                            V.wait_ge(s_dve.h, kr_n)
                        j_, hb_ = genv_chunks.pop(0)
                        csl_ = slice(TQ * j_, TQ * (j_ + 1))
                        V.tensor_copy(k96[hb_][64:96, csl_], kr64[64:96, csl_])
                        n_ = rope_cols(q96[hb_], rcos1, rsin1, rsc, csl_,
                                       base=64, nrows=32)
                        if not genv_chunks:
                            gen_n[hb_] = n_
            if hh == 1:
                p = head // 2
                while pend["tiles"]:
                    emit_proj_tile(7)
                if p < 7:
                    load_wob(p + 1)
                pend.update(p=p, rb=p % 2, ctx_n=s_dve.n, waited=False,
                            tiles=[(q_, o_) for q_ in range(NT) for o_ in range(NC)])
                if DEV_FORCE_FLUSH:
                    while pend["tiles"]:
                        emit_proj_tile(7)
        rot = 0
        while pend["tiles"]:
            emit_proj_tile((3, 4, 5, 6, 7)[rot % 5])
            rot += 1

    with contextlib.ExitStack() as st3, nc.named_scope("l1_mla"):
        l1_mla(st3)
    with nc.named_scope("ln2"):
        postln()
    store_tap("x3")

    def l1_ffn(st):
        win = st.enter_context(nc.sbuf_tensor("win", [128, NC * 64], BF16))
        wout = st.enter_context(nc.sbuf_tensor("wout", [64, 1024], BF16))
        s_sb = st.enter_context(nc.sbuf_tensor("s_sb", [64, S], F32))
        gA = st.enter_context(nc.sbuf_tensor("gA", [64, S], F32))
        gB = st.enter_context(nc.sbuf_tensor("gB", [64, S], F32))
        gbf = st.enter_context(nc.sbuf_tensor("gbf", [64, S], BF16))
        h_i = [st.enter_context(nc.sbuf_tensor(f"h_i{k}", [64, S], F32))
               for k in range(2)]
        print("sbuf left (l1 ffn):", nc.sbuf_bytes_remaining)

        G.wait_ge(s_mm.h, s_mm.n)
        G.wait_ge(s_dve.h, s_dve.n)
        G.wait_ge(s_act.h, s_act.n)
        cast_xb()
        lwin = s_w.mark(G.dma_start(out=win[:, :].rearrange("p (c m) -> p c m", m=64),
                                    in_=dm["Win"].ap().rearrange("(c p) m -> p c m", p=128)), 16)
        lwout = s_w.mark(G.dma_start(out=wout[:, :], in_=dm["Wout"].ap()), 16)

        s_dve.wait(T)
        T.wait_ge(s_w.h, lwin)
        for t in range(NT):
            tsl = slice(TQ * t, TQ * (t + 1))
            bi = 4 + (t % 2)
            bank = t_take_bank(bi)
            for c in range(NC):
                i = T.matmul(bank[0:64, :], win[:, 64 * c:64 * (c + 1)],
                             xb[c][:, tsl], start=(c == 0), stop=(c == NC - 1))
            s_mm.mark(i)
            s_mm.wait(A)
            ia = A.activation(s_sb[:, tsl], bank[0:64, :], AF.Identity, bias=binc_s[:, :])
            s_act.mark(ia)
            evac_mark(bi, s_act)
        ia = A.activation(gA[:, :], s_sb[:, :], AF.Identity, bias=b2f_s[:, :], scale=0.0)
        s_act.mark(ia)
        gs = [gA, gB]
        h_marks = [0, 0]
        for it in range(16):
            hb_ = it % 2
            A.wait_ge(s_dve.h, h_marks[hb_])
            ia = A.activation(h_i[hb_][:, :], s_sb[:, :], AF.Relu,
                              scale=w1f_s[:, it:it + 1], bias=b1f_s[:, it:it + 1])
            s_act.mark(ia)
            s_act.wait(V)
            dst = gbf if it == 15 else gs[(it + 1) % 2]
            iv = V.scalar_tensor_tensor(dst[:, :], h_i[hb_][:, :], w2f_s[:, it:it + 1],
                                        gs[it % 2][:, :], OP.mult, OP.add)
            s_dve.mark(iv)
            h_marks[hb_] = s_dve.n
        gbf_n = s_dve.n
        T.wait_ge(s_w.h, lwout)
        T.wait_ge(s_dve.h, gbf_n)
        for qt in range(NT):
            qsl = slice(TQ * qt, TQ * (qt + 1))
            for oc in range(NC):
                bi = 4 + ((2 * qt + oc) % 4)
                bank = t_take_bank(bi)
                T.matmul(bank[:, :], wout[:, 128 * oc:128 * (oc + 1)],
                         gbf[:, qsl], start=True, stop=False)
                i = T.matmul(bank[:, :], identb[:, :], xb[oc][:, qsl],
                             start=False, stop=True)
                s_mm.mark(i)
                A.wait_ge(s_mm.h, s_mm.n)
                ia = A.activation(x[oc][:, qsl], bank[:, :], AF.Identity,
                                  bias=bout_p[:, oc:oc + 1])
                s_act.mark(ia)
                evac_mark(bi, s_act)

    with contextlib.ExitStack() as st4, nc.named_scope("l1_ffn"):
        l1_ffn(st4)
    with nc.named_scope("ln3"):
        postln()

    s_dve.wait(Y)
    s_act.wait(Y)
    for c in range(NC):
        s_st.mark(Y.dma_start(out=out_x.ap()[128 * c:128 * (c + 1), :], in_=x[c][:, :]), 16)
    s_st.wait(Y)
    return nc


def prep_inputs(inputs):
    WqT = np.ascontiguousarray(inputs["L0_attn_Wq"].T)
    WkT = np.ascontiguousarray(inputs["L0_attn_Wk"].T)
    WvT = np.ascontiguousarray(inputs["L0_attn_Wv"].T)
    WoT = np.ascontiguousarray(inputs["L0_attn_Wo"].T)
    W_UQ = inputs["L1_attn_W_UQ"]; W_QR = inputs["L1_attn_W_QR"]
    Wqcat = np.zeros((256, NH * 96), np.float32)
    for h in range(NH):
        Wqcat[:, 96 * h:96 * h + 64] = W_UQ[:, 64 * h:64 * (h + 1)]
        Wqcat[:, 96 * h + 64:96 * (h + 1)] = W_QR[:, 32 * h:32 * (h + 1)]

    def rope_tab(nrows, period, d_rope):
        j = np.arange(nrows)
        r = (j % period) // 2
        inv = 10000.0 ** (-2.0 * r / d_rope)
        ang = inv[:, None] * np.arange(S, dtype=np.float64)[None, :]
        cosm = np.cos(ang)
        sinm = np.sin(ang) * np.where(j % 2 == 0, -1.0, 1.0)[:, None]
        return _bfa(cosm), _bfa(sinm)

    rcos0, rsin0 = rope_tab(128, 64, 64)
    rcos1, rsin1 = rope_tab(128, 32, 32)
    tri = _bfa(np.triu(np.ones((128, 128))))
    onesm = _bfa(np.ones((128, 128)))
    ident = _bfa(np.eye(128))
    trif = np.triu(np.ones((128, 128), np.float32))
    identf = np.eye(128, dtype=np.float32)
    jj = np.arange(128)
    mask16f = ((jj[:, None] % 16) == np.arange(16)[None, :]).astype(np.float32)
    mask16fT = np.ascontiguousarray(mask16f.T[:, :128]).astype(np.float32)
    maskPfx16 = (((jj[:, None] % 16) == (jj[None, :] % 16)) &
                 ((jj[:, None] // 16) < (jj[None, :] // 16))).astype(np.float32)
    maskJ16 = ((jj[:, None] % 16) == (jj[None, :] % 16)).astype(np.float32)
    shift8 = np.zeros((16, 16), np.float32)
    for i_ in range(8):
        shift8[i_, i_ + 8] = 1.0
    oh127f = np.zeros((128, 1), np.float32); oh127f[127, 0] = 1.0
    sel8 = np.zeros((8, 8 * 128), np.float32)
    for e in range(E):
        sel8[e, 128 * e:128 * (e + 1)] = 1.0
    sel8 = _bfa(sel8)

    def pcol(v):
        return np.ascontiguousarray(np.asarray(v, np.float32).reshape(NC, 128).T)

    shared = dict(
        WqT=WqT, WkT=WkT, WvT=WvT, WoT=WoT,
        bq_p=pcol(inputs["L0_attn_bq"]), bk_p=pcol(inputs["L0_attn_bk"]),
        bo_p=pcol(inputs["L0_attn_bo"]), bv=inputs["L0_attn_bv"][None, :],
        Wg=inputs["L0_ffn_Wg"], W1=inputs["L0_ffn_W1"], b1=inputs["L0_ffn_b1"],
        W2=inputs["L0_ffn_W2"], b2=inputs["L0_ffn_b2"],
        b1_p=np.ascontiguousarray(
            inputs["L0_ffn_b1"].reshape(E * H_FF // 128, 128).T),
        W_DQ=inputs["L1_attn_W_DQ"], W_DKV=inputs["L1_attn_W_DKV"],
        W_UQ=inputs["L1_attn_W_UQ"], W_QR=inputs["L1_attn_W_QR"], Wqcat=Wqcat,
        W_UK=inputs["L1_attn_W_UK"], W_UV=inputs["L1_attn_W_UV"],
        W_KR=inputs["L1_attn_W_KR"], W_O=inputs["L1_attn_W_O"],
        Win=inputs["L1_ffn_Win"], binc=inputs["L1_ffn_bin_"][:, None],
        Wout=inputs["L1_ffn_Wout"],
        W1f=inputs["L1_ffn_W1"], b1f=inputs["L1_ffn_b1"], W2f=inputs["L1_ffn_W2"],
        b2f=inputs["L1_ffn_b2"][:, None], bout_p=pcol(inputs["L1_ffn_bout"]),
        tri=tri, ones=onesm, ident=ident, trif=trif, identf=identf,
        mask16f=mask16f, mask16fT=mask16fT, maskPfx16=maskPfx16, maskJ16=maskJ16,
        shift8=shift8, oh127f=oh127f, sel8=sel8,
        b2_p=np.ascontiguousarray(inputs["L0_ffn_b2"].reshape(E * NC, 128).T),
        rcos0=rcos0, rsin0=rsin0, rcos1=rcos1, rsin1=rsin1,
    )
    shared = {k: np.ascontiguousarray(np.asarray(v, dtype=np.float32)
                                      if np.asarray(v).dtype == np.float64 else v)
              for k, v in shared.items()}
    in_maps = []
    xfull = inputs["L0_attn_x"]
    for i in range(NCORES):
        m = dict(shared)
        m["x_in"] = np.ascontiguousarray(np.asarray(xfull[i]).T.astype(np.float32))
        lt = np.zeros((8, 1), np.float32); lt[:i, 0] = 1.0
        m["mask_lt"] = lt
        in_maps.append(m)
    return in_maps


def kernel(**inputs):
    in_maps = prep_inputs(inputs)
    nc = build_program(DEV_TAPS)
    res = run_bass_kernel_spmd(nc, in_maps, core_ids=list(range(NCORES)))
    out = np.stack([res.results[i]["out_x"].T for i in range(NCORES)])
    if DEV_TAPS:
        kernel.tap_results = [
            {t: res.results[i]["tap_" + t].T for t in DEV_TAPS} for i in range(NCORES)
        ]
    return np.ascontiguousarray(out, dtype=np.float32)


# version 14
# speedup vs baseline: 1.6959x; 1.0069x over previous
import math
import contextlib
import numpy as np
import ml_dtypes
import concourse.bass as bass
import concourse.mybir as mybir
from concourse.bass_utils import run_bass_kernel_spmd

F32 = mybir.dt.float32
BF16 = mybir.dt.bfloat16
OP = mybir.AluOpType
AF = mybir.ActivationFunctionType
AX = mybir.AxisListType.X

NCORES = 8
S = 2048
D = 1024
NH = 16
DK = 64
TQ = 512
NT = S // TQ
NC = D // 128
E = 8
H_FF = 4096

DEV_TAPS = []
DEV_FORCE_COMBINE1 = False
DEV_FORCE_FLUSH = False
SHUF_EO = []
for _g in range(16):
    SHUF_EO += [2 * _g + 1, 2 * _g]


def _bfa(a):
    return np.asarray(a, dtype=np.float64).astype(ml_dtypes.bfloat16)


class Sem:
    def __init__(self, nc, name):
        self.h = nc.alloc_semaphore(name=name)
        self.n = 0

    def mark(self, inst, k=1):
        inst.then_inc(self.h, k)
        self.n += k
        return self.n

    def wait(self, eng, at=None):
        v = self.n if at is None else at
        if v > 0:
            eng.wait_ge(self.h, v)


def build_program(taps):
    global DEV_FORCE_COMBINE1
    nc = bass.Bass("TRN2", target_bir_lowering=False)
    T, V, A, G, Y = nc.tensor, nc.vector, nc.scalar, nc.gpsimd, nc.sync
    s_w = Sem(nc, "s_w")
    s_io = Sem(nc, "s_io")
    s_mm = Sem(nc, "s_mm")
    s_act = Sem(nc, "s_act")
    s_dve = Sem(nc, "s_dve")
    s_cc = Sem(nc, "s_cc")
    s_st = Sem(nc, "s_st")

    dm = {}

    def din(name, shape, dt=F32):
        dm[name] = nc.dram_tensor(name, list(shape), dt, kind="ExternalInput")
        return dm[name]

    din("x_in", (D, S))
    din("WqT", (D, D)); din("WkT", (D, D)); din("WvT", (D, D)); din("WoT", (D, D))
    din("bq_p", (128, NC)); din("bk_p", (128, NC)); din("bo_p", (128, NC))
    din("bv", (1, D))
    din("Wg", (D, E))
    din("W1", (E, D, H_FF)); din("b1", (E, H_FF))
    din("W2", (E, H_FF, D)); din("b2", (E, D))
    din("W_DQ", (D, 256)); din("W_DKV", (D, 256))
    din("W_UQ", (256, D)); din("W_QR", (256, 512)); din("Wqcat", (256, NH * 96))
    din("W_UK", (256, D)); din("W_UV", (256, D))
    din("W_KR", (D, 32))
    din("W_O", (D, D))
    din("Win", (D, 64)); din("binc", (64, 1)); din("Wout", (64, D))
    din("W1f", (64, 16)); din("b1f", (64, 16)); din("W2f", (64, 16))
    din("b2f", (64, 1)); din("bout_p", (128, NC))
    din("b1_p", (128, E * H_FF // 128))
    din("tri", (128, 128), BF16)
    din("ones", (128, 128), BF16)
    din("ident", (128, 128), BF16)
    din("trif", (128, 128))
    din("identf", (128, 128))
    din("mask16f", (128, 16))
    din("mask16fT", (16, 128))
    din("maskPfx16", (128, 128))
    din("maskJ16", (128, 128))
    din("shift8", (16, 16))
    din("oh127f", (128, 1))
    din("sel8", (8, 8 * 128), BF16)
    din("mask_lt", (8, 1))
    din("b2_p", (128, E * NC // 1))
    din("rcos0", (128, S), BF16); din("rsin0", (128, S), BF16)
    din("rcos1", (128, S), BF16); din("rsin1", (128, S), BF16)

    out_x = nc.dram_tensor("out_x", [D, S], F32, kind="ExternalOutput")
    dbg = nc.dram_tensor("dbg", [128, 4096], F32, kind="ExternalOutput") if "dbg" in taps else None
    cc_in = nc.dram_tensor("cc_in", [1, 16], F32, kind="Internal")
    cc_out = nc.dram_tensor("cc_out", [8, 16], F32, kind="Internal")
    tap_dram = {t: nc.dram_tensor("tap_" + t, [D, S], F32, kind="ExternalOutput")
                for t in taps}

    def sb(name, shape, dt=BF16):
        return nc.alloc_sbuf_tensor(name, list(shape), dt)

    x = [sb(f"x{c}", [128, S], F32) for c in range(NC)]
    xb = [sb(f"xb{c}", [128, S], BF16) for c in range(NC)]
    tri = sb("tri_s", [128, 128]); ones = sb("ones_s", [128, 128])
    onescf = sb("onescf_s", [128, 1], F32)
    w1f_s = sb("w1f_s", [64, 16], F32); b1f_s = sb("b1f_s", [64, 16], F32)
    w2f_s = sb("w2f_s", [64, 16], F32); binc_s = sb("binc_s", [64, 1], F32)
    b2f_s = sb("b2f_s", [64, 1], F32); bout_p = sb("bout_p_s", [128, NC], F32)
    bq_p = sb("bq_p_s", [128, NC], F32); bk_p = sb("bk_p_s", [128, NC], F32)
    bo_p = sb("bo_p_s", [128, NC], F32)
    bvr = sb("bv_s", [1, D])
    identb = sb("identb_s", [128, 128])
    epsln = sb("epsln", [128, 1], F32)

    for c in range(NC):
        s_io.mark(Y.dma_start(out=x[c][:, :], in_=dm["x_in"].ap()[128 * c:128 * (c + 1), :]), 16)
    for name, t_ in (("tri", tri), ("ones", ones), ("ident", identb),
                     ("bq_p", bq_p), ("bk_p", bk_p), ("bo_p", bo_p),
                     ("W1f", w1f_s), ("b1f", b1f_s), ("W2f", w2f_s),
                     ("binc", binc_s), ("b2f", b2f_s), ("bout_p", bout_p)):
        s_io.mark(Y.dma_start(out=t_[:, :], in_=dm[name].ap()), 16)
    s_w.mark(G.dma_start(out=bvr[:, :], in_=dm["bv"].ap()), 16)
    s_dve.mark(V.memset(epsln[:, :], 1e-5))
    s_dve.mark(V.memset(onescf[:, :], 1.0))
    io_consts = s_io.n
    w_consts = s_w.n
    eps_n = s_dve.n

    ps = [nc.alloc_psum_tensor(f"ps{i}", [128, TQ], F32) for i in range(8)]
    bank_free = [(s_mm, 0)] * 8

    def t_take_bank(i):
        sem, n = bank_free[i]
        sem.wait(T, n)
        return ps[i]

    def evac_mark(i, sem):
        bank_free[i] = (sem, sem.n)

    def vseq(inst):
        s_dve.mark(inst)
        s_dve.wait(V)
        return s_dve.n

    def cast_xb():
        s_act.wait(V)
        for c in range(NC):
            i = V.tensor_copy(xb[c][:, :], x[c][:, :])
        s_dve.mark(i)
        return s_dve.n

    def rope_cols(z, cosT, sinT, shufbuf, csl, base=0, nrows=128):
        sl = slice(base, base + nrows)
        V.stream_shuffle(shufbuf[sl, csl], z[sl, csl], SHUF_EO)
        V.tensor_tensor(z[sl, csl], z[sl, csl], cosT[sl, csl], OP.mult)
        V.tensor_tensor(shufbuf[sl, csl], shufbuf[sl, csl], sinT[sl, csl], OP.mult)
        i = V.tensor_tensor(z[sl, csl], z[sl, csl], shufbuf[sl, csl], OP.add)
        s_dve.mark(i)
        return s_dve.n

    def rope_inplace(z, nrows, cosT, sinT, shufbuf, base=0):
        sl = slice(base, base + nrows)
        V.stream_shuffle(shufbuf[sl, :], z[sl, :], SHUF_EO)
        V.tensor_tensor(z[sl, :], z[sl, :], cosT[sl, :], OP.mult)
        V.tensor_tensor(shufbuf[sl, :], shufbuf[sl, :], sinT[sl, :], OP.mult)
        i = V.tensor_tensor(z[sl, :], z[sl, :], shufbuf[sl, :], OP.add)
        s_dve.mark(i)
        return s_dve.n

    def postln(store=False):
        with contextlib.ExitStack() as stl:
            uid = str(s_dve.n)
            sqb = [stl.enter_context(nc.sbuf_tensor(f"sqb{uid}_{i}", [128, TQ], BF16))
                   for i in range(2)]
            mtile = [stl.enter_context(nc.sbuf_tensor(f"mt{uid}_{i}", [128, TQ], F32))
                     for i in range(2)]
            ritile = [stl.enter_context(nc.sbuf_tensor(f"rt{uid}_{i}", [128, TQ], F32))
                      for i in range(2)]
            s_act.wait(V)
            s_dve.wait(A)

            def ln_copies(t):
                tsl = slice(TQ * t, TQ * (t + 1))
                for c in range(NC):
                    i1 = A.activation(xb[c][:, tsl], x[c][:, tsl], AF.Copy)
                s_act.mark(i1)
                return s_act.n

            def ln_apply(t, tb):
                tsl = slice(TQ * t, TQ * (t + 1))
                for c in range(NC):
                    V.tensor_tensor(x[c][:, tsl], x[c][:, tsl], mtile[tb][:, :],
                                    OP.subtract)
                    i7 = V.tensor_tensor(x[c][:, tsl], x[c][:, tsl], ritile[tb][:, :],
                                         OP.mult)
                s_dve.mark(i7)
                if store:
                    s_dve.wait(Y)
                    for c in range(NC):
                        s_st.mark(Y.dma_start(out=out_x.ap()[128 * c:128 * (c + 1), tsl],
                                              in_=x[c][:, tsl]), 16)

            cp_n = [0] * NT
            cp_n[0] = ln_copies(0)
            sq_free = [(s_mm, 0), (s_mm, 0)]
            for t in range(NT):
                tb = t % 2
                tsl = slice(TQ * t, TQ * (t + 1))
                for c in range(NC):
                    sem, n = sq_free[c % 2]
                    sem.wait(A, n)
                    i2 = A.activation(sqb[c % 2][:, :], xb[c][:, tsl], AF.Square)
                    s_act.mark(i2)
                    if c == 0:
                        b6 = t_take_bank(6 if tb == 0 else 4)
                        b7 = t_take_bank(7 if tb == 0 else 5)
                        T.wait_ge(s_act.h, cp_n[t])
                    T.matmul(b6[:, :], ones[:, :], xb[c][:, tsl],
                             start=(c == 0), stop=(c == NC - 1))
                    s_act.wait(T)
                    i3 = T.matmul(b7[:, :], ones[:, :], sqb[c % 2][:, :],
                                  start=(c == 0), stop=(c == NC - 1))
                    s_mm.mark(i3)
                    sq_free[c % 2] = (s_mm, s_mm.n)
                s_mm.wait(V)
                V.tensor_scalar(mtile[tb][:, :], b6[:, :], 1.0 / D, None, OP.mult)
                V.tensor_scalar(ritile[tb][:, :], b7[:, :], 1.0 / D, None, OP.mult)
                V.tensor_tensor(b6[:, :], mtile[tb][:, :], mtile[tb][:, :], OP.mult)
                i4 = V.tensor_tensor(ritile[tb][:, :], ritile[tb][:, :], b6[:, :],
                                     OP.subtract)
                s_dve.mark(i4)
                evac_mark(6 if tb == 0 else 4, s_dve)
                evac_mark(7 if tb == 0 else 5, s_dve)
                s_dve.wait(A)
                i5 = A.activation(ritile[tb][:, :], ritile[tb][:, :], AF.Sqrt,
                                  bias=epsln[:, :])
                s_act.mark(i5)
                s_act.wait(V)
                V.reciprocal(ritile[tb][:, :], ritile[tb][:, :])
                if t + 1 < NT:
                    cp_n[t + 1] = ln_copies(t + 1)
                if t >= 1:
                    ln_apply(t - 1, (t - 1) % 2)
            ln_apply(NT - 1, (NT - 1) % 2)
        return s_dve.n

    def store_tap(name):
        if name in tap_dram:
            s_dve.wait(Y)
            s_act.wait(Y)
            for c in range(NC):
                s_st.mark(Y.dma_start(out=tap_dram[name].ap()[128 * c:128 * (c + 1), :],
                                      in_=x[c][:, :]), 16)

    def l0_attention(st):
        wbig = st.enter_context(nc.sbuf_tensor("wbig", [128, NC * 1024], BF16))
        vtm = [st.enter_context(nc.sbuf_tensor(f"vtm{t}", [128, D], BF16)) for t in range(16)]
        qa = [st.enter_context(nc.sbuf_tensor(f"qa{i}", [128, S], BF16)) for i in range(2)]
        ka = [st.enter_context(nc.sbuf_tensor(f"ka{i}", [128, S], BF16)) for i in range(2)]
        wqr = [st.enter_context(nc.sbuf_tensor(f"wqr{i}", [128, NC * 128], BF16)) for i in range(2)]
        pt = [st.enter_context(nc.sbuf_tensor(f"pt{k}", [128, TQ], BF16)) for k in range(3)]
        ctxa = [st.enter_context(nc.sbuf_tensor(f"ctxa{c}", [128, S], BF16)) for c in range(6)]
        stA = st.enter_context(contextlib.ExitStack())
        shufb = stA.enter_context(nc.sbuf_tensor("shufb", [128, S], BF16))
        rcos0 = stA.enter_context(nc.sbuf_tensor("rcos0_s", [128, S], BF16))
        rsin0 = stA.enter_context(nc.sbuf_tensor("rsin0_s", [128, S], BF16))
        s_io.mark(Y.dma_start(out=rcos0[:, :], in_=dm["rcos0"].ap()), 16)
        s_io.mark(Y.dma_start(out=rsin0[:, :], in_=dm["rsin0"].ap()), 16)
        print("sbuf left (l0 attn):", nc.sbuf_bytes_remaining)

        s_io.wait(V, io_consts)
        cast_xb()

        wv_n = s_w.mark(G.dma_start(
            out=wbig[:, :].rearrange("p (c n) -> p c n", n=1024),
            in_=dm["WvT"].ap().rearrange("(c p) n -> p c n", p=128)), 16)
        T.wait_ge(s_w.h, wv_n)
        s_dve.wait(T)
        s_io.wait(T, io_consts)
        for tt in range(16):
            tsl = slice(128 * tt, 128 * (tt + 1))
            for half in range(2):
                bi = (2 * tt + half) % 4
                bank = t_take_bank(bi)
                for c in range(NC):
                    T.matmul(bank[:, :], xb[c][:, tsl],
                             wbig[:, 1024 * c + 512 * half:1024 * c + 512 * (half + 1)],
                             start=(c == 0), stop=False)
                i = T.matmul(bank[:, :], ones[0:1, 0:128],
                             bvr[0:1, 512 * half:512 * (half + 1)], start=False, stop=True)
                s_mm.mark(i)
                s_mm.wait(A)
                ia = A.activation(vtm[tt][:, 512 * half:512 * (half + 1)], bank[:, :], AF.Copy)
                s_act.mark(ia)
                evac_mark(bi, s_act)

        G.wait_ge(s_mm.h, s_mm.n)
        wo_n = s_w.mark(G.dma_start(
            out=wbig[:, :].rearrange("p (c n) -> p c n", n=1024),
            in_=dm["WoT"].ap().rearrange("(c p) n -> p c n", p=128)), 16)

        ring_use = [0, 0]
        qk_rope_n = [0, 0]
        wld = [[0] * 8, [0] * 8]

        def l0_load_pair(p):
            csl = slice(128 * p, 128 * (p + 1))
            for wi, wname in enumerate(("WqT", "WkT")):
                G.wait_ge(s_mm.h, ring_use[wi])
                wld[wi][p] = s_w.mark(G.dma_start(
                    out=wqr[wi][:, :].rearrange("p (c n) -> p c n", n=128),
                    in_=dm[wname].ap()[:, csl].rearrange("(c p) n -> p c n", p=128)), 16)

        def l0_gen_pair(p, defer_rope=False):
            pb = p % 2
            for wi, (dst, bias_) in enumerate(((qa[pb], bq_p), (ka[pb], bk_p))):
                ring = wqr[wi]
                T.wait_ge(s_w.h, wld[wi][p])
                for t in range(NT):
                    tsl = slice(TQ * t, TQ * (t + 1))
                    bi = 7 if t % 2 == 0 else 0
                    bank = t_take_bank(bi)
                    for c in range(NC):
                        i = T.matmul(bank[:, :], ring[:, 128 * c:128 * (c + 1)],
                                     xb[c][:, tsl], start=(c == 0), stop=(c == NC - 1))
                    s_mm.mark(i)
                    A.wait_ge(s_mm.h, s_mm.n)
                    ia = A.activation(dst[:, tsl], bank[:, :], AF.Identity,
                                      bias=bias_[:, p:p + 1])
                    s_act.mark(ia)
                    evac_mark(bi, s_act)
                ring_use[wi] = s_mm.n
            if p + 1 < 8:
                l0_load_pair(p + 1)
            if not defer_rope:
                _rope_pair(p)

        def _rope_pair(p):
            pb = p % 2
            s_act.wait(V)
            rope_inplace(qa[pb], 128, rcos0, rsin0, shufb)
            rope_inplace(ka[pb], 128, rcos0, rsin0, shufb)
            qk_rope_n[pb] = s_dve.n

        l0_load_pair(0)
        l0_gen_pair(0)
        pt_reader = [0] * 3
        blk_i = 0
        for p in range(8):
            pb = p % 2
            rope_chunks = []
            if p + 1 < 8:
                dr = (p + 1 < 7)
                l0_gen_pair(p + 1, defer_rope=dr)
                if dr:
                    npb = (p + 1) % 2
                    rope_chunks = [(qa[npb], j) for j in range(NT)] + \
                                  [(ka[npb], j) for j in range(NT)]
            if p == 6:
                stA.close()
                for c in (6, 7):
                    ctxa.append(st.enter_context(nc.sbuf_tensor(f"ctxa{c}", [128, S], BF16)))
            T.wait_ge(s_dve.h, qk_rope_n[pb])
            for qt in range(NT):
                qsl = slice(TQ * qt, TQ * (qt + 1))
                nkc = 4 * qt + 4
                for hh in range(2):
                    head = 2 * p + hh
                    hsl = slice(64 * hh, 64 * hh + 64)
                    dbi, cbi = (3, 4) if blk_i % 2 == 0 else (5, 6)
                    blk_i += 1
                    dps = t_take_bank(dbi)[hsl, :]
                    cps = t_take_bank(cbi)[hsl, :]
                    pt_ready = [0] * 3

                    def emit_S(kc):
                        bi = kc % 3
                        bank = t_take_bank(bi)
                        i = T.matmul(bank[:, :], ka[pb][hsl, 128 * kc:128 * (kc + 1)],
                                     qa[pb][hsl, qsl], start=True, stop=True)
                        s_mm.mark(i)
                        A.wait_ge(s_mm.h, s_mm.n)
                        A.wait_ge(s_mm.h, pt_reader[kc % 3])
                        ia = A.activation(pt[kc % 3][:, :], bank[:, :], AF.Exp, scale=0.125)
                        s_act.mark(ia)
                        evac_mark(bi, s_act)
                        s_ = kc - 4 * qt
                        if s_ >= 0:
                            s_act.wait(V)
                            blk = slice(128 * s_, 128 * (s_ + 1))
                            i = V.tensor_tensor(pt[kc % 3][:, blk], pt[kc % 3][:, blk],
                                                tri[:, :], OP.mult)
                            if s_ > 0:
                                i = V.memset(pt[kc % 3][:, 0:128 * s_], 0.0)
                            s_dve.mark(i)
                            pt_ready[kc % 3] = ("dve", s_dve.n)
                        else:
                            pt_ready[kc % 3] = ("act", s_act.n)

                    emit_S(0)
                    if nkc > 1:
                        emit_S(1)
                    for kc in range(nkc):
                        if kc + 2 < nkc:
                            emit_S(kc + 2)
                        kind, n_ = pt_ready[kc % 3]
                        if kind == "dve":
                            T.wait_ge(s_dve.h, n_)
                        else:
                            T.wait_ge(s_act.h, n_)
                        T.matmul(dps, ones[:, hsl], pt[kc % 3][:, :],
                                 start=(kc == 0), stop=(kc == nkc - 1))
                        i = T.matmul(cps, vtm[kc][:, 64 * head:64 * (head + 1)],
                                     pt[kc % 3][:, :], start=(kc == 0), stop=(kc == nkc - 1))
                        s_mm.mark(i)
                        pt_reader[kc % 3] = s_mm.n
                    s_mm.wait(V)
                    V.tensor_scalar(dps, dps, 1e-8, None, OP.add)
                    V.reciprocal(dps, dps)
                    V.tensor_copy(ctxa[p][hsl, qsl], cps)
                    i = V.tensor_tensor(ctxa[p][hsl, qsl], ctxa[p][hsl, qsl], dps, OP.mult)
                    s_dve.mark(i)
                    evac_mark(dbi, s_dve)
                    evac_mark(cbi, s_dve)
                    for _ in range(2):
                        if rope_chunks:
                            if len(rope_chunks) == 2 * NT:
                                s_act.wait(V)
                            z_, j_ = rope_chunks.pop(0)
                            n_ = rope_cols(z_, rcos0, rsin0, shufb,
                                           slice(TQ * j_, TQ * (j_ + 1)))
                            if not rope_chunks:
                                qk_rope_n[(p + 1) % 2] = n_
        ctx_n = s_dve.n
        T.wait_ge(s_dve.h, ctx_n)
        T.wait_ge(s_w.h, wo_n)
        for qt in range(NT):
            qsl = slice(TQ * qt, TQ * (qt + 1))
            for oc in range(NC):
                bi = 3 + (oc % 2)
                bank = t_take_bank(bi)
                for c in range(NC):
                    T.matmul(bank[:, :],
                             wbig[:, 1024 * c + 128 * oc:1024 * c + 128 * (oc + 1)],
                             ctxa[c][:, qsl], start=(c == 0), stop=False)
                i = T.matmul(bank[:, :], identb[:, :], xb[oc][:, qsl],
                             start=False, stop=True)
                s_mm.mark(i)
                A.wait_ge(s_mm.h, s_mm.n)
                ia = A.activation(x[oc][:, qsl], bank[:, :], AF.Identity,
                                  bias=bo_p[:, oc:oc + 1])
                s_act.mark(ia)
                evac_mark(bi, s_act)

    with contextlib.ExitStack() as st1, nc.named_scope("l0_attn"):
        l0_attention(st1)
    with nc.named_scope("ln0"):
        postln()
    store_tap("x1")

    def l0_moe(st):
        NTC = 16
        h_sb = [st.enter_context(nc.sbuf_tensor(f"hsb{i}", [128, 1024], BF16))
                for i in range(32)]
        w1r = [st.enter_context(nc.sbuf_tensor(f"w1r{i}", [128, 1024], BF16))
               for i in range(4)]
        w2r = [st.enter_context(nc.sbuf_tensor(f"w2r{i}", [128, 2048], BF16))
               for i in range(4)]
        cbc = st.enter_context(nc.sbuf_tensor("cbc", [128, S], BF16))
        cT = st.enter_context(nc.sbuf_tensor("cT", [8, S], BF16))
        gate_e = st.enter_context(nc.sbuf_tensor("gate_e", [128, 8], F32))
        cs_all = st.enter_context(nc.sbuf_tensor("cs_all", [128, 256], F32))
        oh_all = st.enter_context(nc.sbuf_tensor("oh_all", [128, 256], F32))
        w_all = st.enter_context(nc.sbuf_tensor("w_all", [128, 32], F32))
        sc1 = st.enter_context(nc.sbuf_tensor("sc1", [128, 1], F32))
        sc2 = st.enter_context(nc.sbuf_tensor("sc2", [128, 1], F32))
        sc3 = st.enter_context(nc.sbuf_tensor("sc3", [128, 1], F32))
        totb = [st.enter_context(nc.sbuf_tensor(f"totb{i}", [128, 1], F32)) for i in range(2)]
        ltot = st.enter_context(nc.sbuf_tensor("ltot", [16, 1], F32))
        gath = st.enter_context(nc.sbuf_tensor("gath", [8, 16], F32))
        cpfx = st.enter_context(nc.sbuf_tensor("cpfx", [16, 1], F32))
        offrow = st.enter_context(nc.sbuf_tensor("offrow", [1, 256], F32))
        prefb = st.enter_context(nc.sbuf_tensor("prefb", [128, 16], F32))
        tmpf = [st.enter_context(nc.sbuf_tensor("tmpf0", [128, TQ], F32))] * 2
        mm8 = st.enter_context(nc.sbuf_tensor("mm8", [128, 8], F32))
        trif = st.enter_context(nc.sbuf_tensor("trif_s", [128, 128], F32))
        identf = st.enter_context(nc.sbuf_tensor("identf_s", [128, 128], F32))
        mask16f = st.enter_context(nc.sbuf_tensor("mask16f_s", [128, 16], F32))
        mask16fT = st.enter_context(nc.sbuf_tensor("mask16fT_s", [16, 128], F32))
        maskPfx16 = st.enter_context(nc.sbuf_tensor("maskPfx16_s", [128, 128], F32))
        maskJ16 = st.enter_context(nc.sbuf_tensor("maskJ16_s", [128, 128], F32))
        shift8 = st.enter_context(nc.sbuf_tensor("shift8_s", [16, 16], F32))
        oh127f = st.enter_context(nc.sbuf_tensor("oh127f_s", [128, 1], F32))
        sel8 = st.enter_context(nc.sbuf_tensor("sel8_s", [8, 8 * 128], BF16))
        mask_lt = st.enter_context(nc.sbuf_tensor("mask_lt_s", [8, 1], F32))
        wg_s = st.enter_context(nc.sbuf_tensor("wg_s", [128, NC * E], F32))
        b1_p = st.enter_context(nc.sbuf_tensor("b1_p_s", [128, E * 32], F32))
        b2_p = st.enter_context(nc.sbuf_tensor("b2_p_s", [128, E * NC], F32))
        print("sbuf left (moe):", nc.sbuf_bytes_remaining)

        G.wait_ge(s_mm.h, s_mm.n)
        G.wait_ge(s_dve.h, s_dve.n)
        G.wait_ge(s_act.h, s_act.n)
        Y.wait_ge(s_dve.h, s_dve.n)
        Y.wait_ge(s_act.h, s_act.n)
        Y.wait_ge(s_mm.h, s_mm.n)
        for name, t_ in (("trif", trif), ("identf", identf),
                         ("mask16f", mask16f), ("mask16fT", mask16fT),
                         ("maskPfx16", maskPfx16), ("maskJ16", maskJ16),
                         ("shift8", shift8), ("oh127f", oh127f), ("sel8", sel8),
                         ("mask_lt", mask_lt), ("b2_p", b2_p), ("b1_p", b1_p)):
            s_io.mark(Y.dma_start(out=t_[:, :], in_=dm[name].ap()), 16)
        s_io.mark(Y.dma_start(out=wg_s[:, :].rearrange("p (c e) -> p c e", e=E),
                              in_=dm["Wg"].ap().rearrange("(c p) e -> p c e", p=128)), 16)

        cast_n = cast_xb()
        s_io.wait(T)

        for tc in range(NTC):
            tsl = slice(128 * tc, 128 * (tc + 1))
            b5 = t_take_bank(5)
            zp = b5[:, 0:8]
            for c in range(NC):
                i = T.matmul(zp, x[c][:, tsl], wg_s[:, E * c:E * (c + 1)],
                             start=(c == 0), stop=(c == NC - 1))
            s_mm.mark(i)
            s_mm.wait(V)
            i = V.reduce_max(sc1[:, :], zp, axis=AX, negate=True)
            s_dve.mark(i)
            s_dve.wait(A)
            ia = A.activation(gate_e[:, :], zp, AF.Exp, bias=sc1[:, :])
            s_act.mark(ia)
            evac_mark(5, s_act)
            s_act.wait(V)
            V.reduce_sum(sc3[:, :], gate_e[:, :], axis=AX)
            vseq(V.reduce_max(sc1[:, :], gate_e[:, :], axis=AX))
            vseq(V.tensor_scalar(oh_all[:, 16 * tc:16 * tc + 8], gate_e[:, :], sc1[:, :], None, OP.is_equal))
            vseq(V.tensor_scalar(mm8[:, :], oh_all[:, 16 * tc:16 * tc + 8], -1.0, 1.0, OP.mult, OP.add))
            vseq(V.tensor_tensor(mm8[:, :], gate_e[:, :], mm8[:, :], OP.mult))
            vseq(V.reduce_max(sc2[:, :], mm8[:, :], axis=AX))
            vseq(V.tensor_scalar(oh_all[:, 16 * tc + 8:16 * tc + 16], mm8[:, :], sc2[:, :], None, OP.is_equal))
            vseq(V.tensor_scalar(sc3[:, :], sc3[:, :], 1e-9, None, OP.mult))
            vseq(V.tensor_tensor(sc3[:, :], sc3[:, :], sc1[:, :], OP.add))
            vseq(V.tensor_tensor(sc3[:, :], sc3[:, :], sc2[:, :], OP.add))
            vseq(V.reciprocal(sc3[:, :], sc3[:, :]))
            V.tensor_tensor(w_all[:, 2 * tc:2 * tc + 1], sc1[:, :], sc3[:, :], OP.mult)
            i = V.tensor_tensor(w_all[:, 2 * tc + 1:2 * tc + 2], sc2[:, :], sc3[:, :], OP.mult)
            s_dve.mark(i)
            b6 = t_take_bank(6)
            s_dve.wait(T)
            i = T.matmul(b6[:, 0:16], trif[:, :], oh_all[:, 16 * tc:16 * (tc + 1)],
                         start=True, stop=True)
            s_mm.mark(i)
            s_mm.wait(V)
            i = V.tensor_copy(cs_all[:, 16 * tc:16 * (tc + 1)], b6[:, 0:16])
            s_dve.mark(i)
            evac_mark(6, s_dve)
        route_n = s_dve.n

        b7 = t_take_bank(7)
        T.wait_ge(s_dve.h, route_n)
        for half in range(2):
            i = T.matmul(b7[:, half:half + 1], cs_all[:, 128 * half:128 * (half + 1)],
                         oh127f[:, :], start=True, stop=True)
            s_mm.mark(i)
        s_mm.wait(V)
        V.tensor_copy(totb[0][:, :], b7[:, 0:1])
        i = V.tensor_copy(totb[1][:, :], b7[:, 1:2])
        s_dve.mark(i)
        evac_mark(7, s_dve)
        tot_n = s_dve.n
        b7 = t_take_bank(7)
        T.wait_ge(s_dve.h, tot_n)
        for half in range(2):
            i = T.matmul(b7[0:16, 2:3], mask16f[:, :], totb[half][:, :],
                         start=(half == 0), stop=(half == 1))
        s_mm.mark(i)
        s_mm.wait(V)
        i = V.tensor_copy(ltot[:, :], b7[0:16, 2:3])
        s_dve.mark(i)
        evac_mark(7, s_dve)
        G.wait_ge(s_dve.h, s_dve.n)
        s_w.mark(G.dma_start(out=cc_in.ap(), in_=ltot[:, :]), 16)
        G.wait_ge(s_w.h, s_w.n)
        ic = G.collective_compute(
            "AllGather", OP.bypass, replica_groups=[list(range(NCORES))],
            ins=[cc_in.ap()], outs=[cc_out.ap()])
        s_cc.mark(ic)
        G.wait_ge(s_cc.h, s_cc.n)
        gw = s_w.mark(G.dma_start(out=gath[:, :], in_=cc_out.ap()), 16)
        b7 = t_take_bank(7)
        T.wait_ge(s_w.h, gw)
        i = T.matmul(b7[0:16, 0:1], gath[:, :], mask_lt[:, :], start=True, stop=True)
        s_mm.mark(i)
        i = T.matmul(b7[0:16, 1:2], gath[:, :], onescf[0:8, 0:1], start=True, stop=True)
        s_mm.mark(i)
        s_mm.wait(V)
        vseq(V.tensor_copy(cpfx[:, :], b7[0:16, 0:1]))
        i = V.tensor_scalar(ltot[:, :], b7[0:16, 1:2], 2048.0, None, OP.min)
        s_dve.mark(i)
        evac_mark(7, s_dve)
        b7 = t_take_bank(7)
        T.wait_ge(s_dve.h, s_dve.n)
        i = T.matmul(b7[0:16, 3:4], shift8[:, :], ltot[:, :], start=True, stop=True)
        s_mm.mark(i)
        s_mm.wait(V)
        i = vseq(V.tensor_tensor(cpfx[:, :], cpfx[:, :], b7[0:16, 3:4], OP.add))
        i = s_dve
        evac_mark(7, s_dve)
        core_n = s_dve.n
        b7 = t_take_bank(7)
        T.wait_ge(s_dve.h, core_n)
        T.matmul(b7[:, 4:5], maskPfx16[:, :], totb[0][:, :], start=True, stop=False)
        i = T.matmul(b7[:, 4:5], mask16fT[:, :], cpfx[:, :], start=False, stop=True)
        s_mm.mark(i)
        T.matmul(b7[:, 5:6], maskPfx16[:, :], totb[1][:, :], start=True, stop=False)
        T.matmul(b7[:, 5:6], maskJ16[:, :], totb[0][:, :], start=False, stop=False)
        i = T.matmul(b7[:, 5:6], mask16fT[:, :], cpfx[:, :], start=False, stop=True)
        s_mm.mark(i)
        s_mm.wait(V)
        V.tensor_copy(totb[0][:, :], b7[:, 4:5])
        i = V.tensor_copy(totb[1][:, :], b7[:, 5:6])
        s_dve.mark(i)
        evac_mark(7, s_dve)
        off_n = s_dve.n
        for half in range(2):
            b7 = t_take_bank(7)
            T.wait_ge(s_dve.h, s_dve.n)
            i = T.matmul(b7[0:1, 0:128], totb[half][:, :], identf[:, :],
                         start=True, stop=True)
            s_mm.mark(i)
            s_mm.wait(V)
            i = V.tensor_copy(offrow[0:1, 128 * half:128 * (half + 1)], b7[0:1, 0:128])
            s_dve.mark(i)
            evac_mark(7, s_dve)

        for tc in range(NTC):
            half = tc // 8
            b7 = t_take_bank(7)
            T.wait_ge(s_dve.h, s_dve.n)
            i = T.matmul(b7[:, 0:16], trif[0:1, :],
                         offrow[0:1, 16 * tc:16 * (tc + 1)],
                         start=True, stop=True)
            s_mm.mark(i)
            s_mm.wait(V)
            vseq(V.tensor_tensor(prefb[:, :], cs_all[:, 16 * tc:16 * (tc + 1)], b7[:, 0:16], OP.add))
            vseq(V.tensor_scalar(prefb[:, :], prefb[:, :], 2048.5, None, OP.is_lt))
            vseq(V.tensor_tensor(prefb[:, :], prefb[:, :], oh_all[:, 16 * tc:16 * (tc + 1)], OP.mult))
            vseq(V.tensor_scalar(mm8[:, :], prefb[:, 0:8], w_all[:, 2 * tc:2 * tc + 1], None, OP.mult))
            vseq(V.tensor_scalar(prefb[:, 8:16], prefb[:, 8:16], w_all[:, 2 * tc + 1:2 * tc + 2], None, OP.mult))
            vseq(V.tensor_tensor(mm8[:, :], mm8[:, :], prefb[:, 8:16], OP.add))
            evac_mark(7, s_dve)
            i = V.tensor_copy(cbc[:, 8 * tc:8 * (tc + 1)], mm8[:, :])
            s_dve.mark(i)
            b6 = t_take_bank(6)
            s_dve.wait(T)
            i = T.matmul(b6[0:8, 0:128], cbc[:, 8 * tc:8 * (tc + 1)], identb[:, :],
                         start=True, stop=True)
            s_mm.mark(i)
            s_mm.wait(A)
            ia = A.activation(cT[0:8, 128 * tc:128 * (tc + 1)], b6[0:8, 0:128], AF.Copy)
            s_act.mark(ia)
            evac_mark(6, s_act)
        if DEV_FORCE_COMBINE1:
            s_act.wait(V)
            i = V.memset(cT[:, :], 1.0)
            s_dve.mark(i)
            s_dve.wait(A)
            ia = A.activation(cT[:, :], cT[:, :], AF.Copy)
            s_act.mark(ia)
        ct_n = s_act.n
        if dbg is not None:
            s_dve.wait(Y); s_act.wait(Y)
            s_dve.wait(G); s_act.wait(G); s_w.wait(G)
            _nn = nc.allow_non_contiguous_dma(reason="dbg"); _nn.__enter__()
            s_st.mark(Y.dma_start(out=dbg.ap()[0:8, 0:16], in_=gath[:, :]), 16)
            s_st.mark(Y.dma_start(out=dbg.ap()[0:16, 16:17], in_=ltot[:, :]), 16)
            s_st.mark(Y.dma_start(out=dbg.ap()[0:16, 17:18], in_=cpfx[:, :]), 16)
            s_st.mark(Y.dma_start(out=dbg.ap()[0:1, 128:384], in_=offrow[:, :]), 16)
            s_st.mark(Y.dma_start(out=dbg.ap()[:, 384:640], in_=cs_all[:, :]), 16)
            s_st.mark(Y.dma_start(out=dbg.ap()[:, 640:896], in_=oh_all[:, :]), 16)
            s_st.mark(Y.dma_start(out=dbg.ap()[:, 896:928], in_=w_all[:, :]), 16)
            s_st.mark(G.dma_start(out=dbg.ap()[0:8, 1024:3072], in_=cT[:, :]), 16)
            _nn.__exit__(None, None, None)

        w1_use = [0, 0, 0, 0]
        w2_use = [0, 0, 0, 0]
        for e in range(E):
            s_act.wait(T, ct_n)
            for t4 in range(NT):
                bi = 4 + (t4 % 2)
                bank = t_take_bank(bi)
                i = T.matmul(bank[:, :], sel8[:, 128 * e:128 * (e + 1)],
                             cT[:, TQ * t4:TQ * (t4 + 1)], start=True, stop=True)
                s_mm.mark(i)
                s_mm.wait(A)
                ia = A.activation(cbc[:, TQ * t4:TQ * (t4 + 1)], bank[:, :], AF.Copy)
                s_act.mark(ia)
                evac_mark(bi, s_act)
            cbc_n = s_act.n
            for th in range(2):
                hof = 1024 * th
                for hc in range(32):
                    rb = hc % 4
                    G.wait_ge(s_mm.h, w1_use[rb])
                    wn = s_w.mark(G.dma_start(
                        out=w1r[rb][:, :].rearrange("p (c n) -> p c n", n=128),
                        in_=dm["W1"].ap()[e, :, 128 * hc:128 * (hc + 1)]
                            .rearrange("(c p) n -> p c n", p=128)), 16)
                    T.wait_ge(s_w.h, wn)
                    for tt in range(2):
                        tsl = slice(hof + TQ * tt, hof + TQ * (tt + 1))
                        bi = tt
                        bank = t_take_bank(bi)
                        for c in range(NC):
                            i = T.matmul(bank[:, :], w1r[rb][:, 128 * c:128 * (c + 1)],
                                         xb[c][:, tsl], start=(c == 0), stop=(c == NC - 1))
                        s_mm.mark(i)
                        s_mm.wait(A)
                        ia = A.activation(h_sb[hc][:, TQ * tt:TQ * (tt + 1)], bank[:, :],
                                          AF.Gelu_apprx_tanh, bias=b1_p[:, 32 * e + hc:32 * e + hc + 1])
                        s_act.mark(ia)
                        evac_mark(bi, s_act)
                    w1_use[rb] = s_mm.n
                h_n = s_act.n
                for dc in range(NC):
                    for wh in range(2):
                        rb = (dc * 2 + wh) % 4
                        G.wait_ge(s_mm.h, w2_use[rb])
                        wn = s_w.mark(G.dma_start(
                            out=w2r[rb][:, :].rearrange("p (c n) -> p c n", n=128),
                            in_=dm["W2"].ap()[e, 2048 * wh:2048 * (wh + 1), 128 * dc:128 * (dc + 1)]
                                .rearrange("(c p) n -> p c n", p=128)), 16)
                        if wh == 0:
                            wn0 = wn
                        w2_use[rb] = max(w2_use[rb], 1)
                    A_done = h_n
                    T.wait_ge(s_act.h, A_done)
                    for tt in range(2):
                        tsl = slice(hof + TQ * tt, hof + TQ * (tt + 1))
                        bi = 2 + ((dc * 2 + tt) % 2)
                        bank = t_take_bank(bi)
                        for hc in range(32):
                            if hc == 0:
                                T.wait_ge(s_w.h, wn0)
                            if hc == 16:
                                T.wait_ge(s_w.h, wn)
                            rb = (dc * 2 + hc // 16) % 4
                            i = T.matmul(bank[:, :], w2r[rb][:, 128 * (hc % 16):128 * (hc % 16 + 1)],
                                         h_sb[hc][:, TQ * tt:TQ * (tt + 1)],
                                         start=(hc == 0), stop=(hc == 31))
                        s_mm.mark(i)
                        if tt == 1:
                            w2_use[(dc * 2) % 4] = s_mm.n
                            w2_use[(dc * 2 + 1) % 4] = s_mm.n
                        s_mm.wait(V)
                        s_act.wait(V, cbc_n)
                        V.scalar_tensor_tensor(tmpf[tt][:, :], bank[:, :],
                                               b2_p[:, 8 * e + dc:8 * e + dc + 1],
                                               cbc[:, tsl], OP.add, OP.mult)
                        iv = V.tensor_tensor(x[dc][:, tsl], x[dc][:, tsl], tmpf[tt][:, :], OP.add)
                        s_dve.mark(iv)
                        evac_mark(bi, s_dve)

    with contextlib.ExitStack() as st2, nc.named_scope("l0_moe"):
        l0_moe(st2)
    with nc.named_scope("ln1"):
        postln()
    store_tap("x2")

    def l1_mla(st):
        cq = [st.enter_context(nc.sbuf_tensor(f"cq{i}", [128, S], BF16)) for i in range(2)]
        ckv = [st.enter_context(nc.sbuf_tensor(f"ckv{i}", [128, S], BF16)) for i in range(2)]
        kr64 = st.enter_context(nc.sbuf_tensor("kr64", [96, S], BF16))
        vtm1 = [st.enter_context(nc.sbuf_tensor(f"vtm1_{t}", [128, D], BF16)) for t in range(16)]
        q96 = [st.enter_context(nc.sbuf_tensor(f"q96_{i}", [96, S], BF16)) for i in range(2)]
        k96 = [st.enter_context(nc.sbuf_tensor(f"k96_{i}", [96, S], BF16)) for i in range(2)]
        ctxp1 = [st.enter_context(nc.sbuf_tensor(f"ctxp1_{i}", [128, S], BF16))
                 for i in range(2)]
        pt1 = [st.enter_context(nc.sbuf_tensor(f"pt1_{k}", [128, TQ], BF16)) for k in range(3)]
        rsc = st.enter_context(nc.sbuf_tensor("rsc0", [128, S], BF16))
        rcos1 = st.enter_context(nc.sbuf_tensor("rcos1_s", [128, S], BF16))
        rsin1 = st.enter_context(nc.sbuf_tensor("rsin1_s", [128, S], BF16))
        stA = contextlib.ExitStack()
        wdq = stA.enter_context(nc.sbuf_tensor("wdq", [128, 2048], BF16))
        wdkv = stA.enter_context(nc.sbuf_tensor("wdkv", [128, 2048], BF16))
        wuv = stA.enter_context(nc.sbuf_tensor("wuv", [128, 2048], BF16))
        wkr2 = stA.enter_context(nc.sbuf_tensor("wkr2", [128, 256], BF16))
        print("sbuf left (l1 mla A):", nc.sbuf_bytes_remaining)

        G.wait_ge(s_mm.h, s_mm.n)
        G.wait_ge(s_dve.h, s_dve.n)
        G.wait_ge(s_act.h, s_act.n)
        Y.wait_ge(s_dve.h, s_dve.n)
        Y.wait_ge(s_act.h, s_act.n)
        Y.wait_ge(s_mm.h, s_mm.n)
        s_io.mark(Y.dma_start(out=rcos1[:, :], in_=dm["rcos1"].ap()), 16)
        s_io.mark(Y.dma_start(out=rsin1[:, :], in_=dm["rsin1"].ap()), 16)
        cast0 = cast_xb()
        s_io.wait(V)

        ldq = s_w.mark(G.dma_start(out=wdq[:, :].rearrange("p (c m) -> p c m", m=256),
                                   in_=dm["W_DQ"].ap().rearrange("(c p) m -> p c m", p=128)), 16)
        ldkv = s_w.mark(G.dma_start(out=wdkv[:, :].rearrange("p (c m) -> p c m", m=256),
                                    in_=dm["W_DKV"].ap().rearrange("(c p) m -> p c m", p=128)), 16)
        lkr = s_w.mark(G.dma_start(out=wkr2[:, :].rearrange("p (c m) -> p c m", m=32),
                                   in_=dm["W_KR"].ap().rearrange("(c p) m -> p c m", p=128)), 16)
        luv = s_w.mark(G.dma_start(out=wuv[:, :].rearrange("p (c m) -> p c m", m=1024),
                                   in_=dm["W_UV"].ap().rearrange("(c p) m -> p c m", p=128)), 16)
        lwo = luq = luq2 = lqr = None

        s_dve.wait(T)
        T.wait_ge(s_w.h, ldkv)
        for which, (w_, dst) in enumerate(((wdq, cq), (wdkv, ckv))):
            for m2 in range(2):
                for t in range(NT):
                    tsl = slice(TQ * t, TQ * (t + 1))
                    bi = 4 + (t % 2)
                    bank = t_take_bank(bi)
                    for c in range(NC):
                        i = T.matmul(bank[:, :], w_[:, 256 * c + 128 * m2:256 * c + 128 * (m2 + 1)],
                                     xb[c][:, tsl], start=(c == 0), stop=(c == NC - 1))
                    s_mm.mark(i)
                    s_mm.wait(A)
                    ia = A.activation(dst[m2][:, tsl], bank[:, :], AF.Copy)
                    s_act.mark(ia)
                    evac_mark(bi, s_act)

        T.wait_ge(s_w.h, lkr)
        for t in range(NT):
            tsl = slice(TQ * t, TQ * (t + 1))
            bi = 4 + (t % 2)
            bank = t_take_bank(bi)
            for c in range(NC):
                i = T.matmul(bank[64:96, :], wkr2[:, 32 * c:32 * (c + 1)],
                             xb[c][:, tsl], start=(c == 0), stop=(c == NC - 1))
            s_mm.mark(i)
            s_mm.wait(A)
            ia = A.activation(kr64[64:96, tsl], bank[64:96, :], AF.Copy)
            s_act.mark(ia)
            evac_mark(bi, s_act)
        s_act.wait(V)
        rope_inplace(kr64, 32, rcos1, rsin1, rsc, base=64)
        kr_n = s_dve.n

        T.wait_ge(s_w.h, luv)
        s_act.wait(T)
        for tt in range(16):
            tsl = slice(128 * tt, 128 * (tt + 1))
            for half in range(2):
                bi = (2 * tt + half) % 4
                bank = t_take_bank(bi)
                for c2 in range(2):
                    i = T.matmul(bank[:, :], ckv[c2][:, tsl],
                                 wuv[:, 1024 * c2 + 512 * half:1024 * c2 + 512 * (half + 1)],
                                 start=(c2 == 0), stop=(c2 == 1))
                s_mm.mark(i)
                s_mm.wait(A)
                ia = A.activation(vtm1[tt][:, 512 * half:512 * (half + 1)], bank[:, :], AF.Copy)
                s_act.mark(ia)
                evac_mark(bi, s_act)

        stA.close()
        wqc = st.enter_context(nc.sbuf_tensor("wqc", [128, 2 * 1536], BF16))
        wuk = st.enter_context(nc.sbuf_tensor("wuk", [128, 2048], BF16))
        wob = [st.enter_context(nc.sbuf_tensor(f"wob{i}", [128, 1024], BF16))
               for i in range(2)]
        print("sbuf left (l1 mla B):", nc.sbuf_bytes_remaining)
        G.wait_ge(s_mm.h, s_mm.n)
        luq = s_w.mark(G.dma_start(out=wqc[:, :].rearrange("p (c m) -> p c m", m=1536),
                                   in_=dm["Wqcat"].ap().rearrange("(c p) m -> p c m", p=128)), 16)
        lqr = s_w.mark(G.dma_start(out=wuk[:, :].rearrange("p (c m) -> p c m", m=1024),
                                    in_=dm["W_UK"].ap().rearrange("(c p) m -> p c m", p=128)), 16)
        wob_use = [0, 0]
        wob_n = [0, 0]

        def load_wob(p):
            rb = p % 2
            G.wait_ge(s_mm.h, wob_use[rb])
            wob_n[rb] = s_w.mark(G.dma_start(
                out=wob[rb][:, :].rearrange("p (c m) -> p c m", m=128),
                in_=dm["W_O"].ap()[128 * p:128 * (p + 1), :]
                    .rearrange("r (c m) -> r c m", m=128)), 16)
        load_wob(0)
        T.wait_ge(s_w.h, lqr)
        scale1 = 1.0 / math.sqrt(96.0)
        CLO = float(np.exp(np.float64(-80.0) / np.sqrt(np.float64(96.0))))
        CHI = float(np.exp(np.float64(80.0) / np.sqrt(np.float64(96.0))))
        gen_n = [0, 0]

        def l1_gen_head(h, defer_v=False):
            hb = h % 2
            for t in range(NT):
                tsl = slice(TQ * t, TQ * (t + 1))
                bi = 7
                bank = t_take_bank(bi)
                for c2 in range(2):
                    i = T.matmul(bank[0:96, :], wqc[:, 1536 * c2 + 96 * h:1536 * c2 + 96 * (h + 1)],
                                 cq[c2][:, tsl], start=(c2 == 0), stop=(c2 == 1))
                s_mm.mark(i)
                A.wait_ge(s_mm.h, s_mm.n)
                ia = A.activation(q96[hb][0:96, tsl], bank[0:96, :], AF.Copy)
                s_act.mark(ia)
                evac_mark(bi, s_act)
                bi2 = 0
                bank2 = t_take_bank(bi2)
                for c2 in range(2):
                    i = T.matmul(bank2[0:64, :], wuk[:, 1024 * c2 + 64 * h:1024 * c2 + 64 * (h + 1)],
                                 ckv[c2][:, tsl], start=(c2 == 0), stop=(c2 == 1))
                s_mm.mark(i)
                A.wait_ge(s_mm.h, s_mm.n)
                ia = A.activation(k96[hb][0:64, tsl], bank2[0:64, :], AF.Copy)
                s_act.mark(ia)
                evac_mark(bi2, s_act)
            if not defer_v:
                _gen_head_v(h)

        def _gen_head_v(h):
            hb = h % 2
            s_act.wait(V)
            V.wait_ge(s_dve.h, kr_n)
            i = V.tensor_copy(k96[hb][64:96, :], kr64[64:96, :])
            s_dve.mark(i)
            rope_inplace(q96[hb], 32, rcos1, rsin1, rsc, base=64)
            gen_n[hb] = s_dve.n

        l1_gen_head(0)
        pt_reader = [0] * 3
        blk1 = 0
        pend = {"p": 0, "rb": 0, "ctx_n": 0, "waited": False, "tiles": []}
        xb_cast_n = {(q_, o_): (s_dve, cast0) for q_ in range(NT) for o_ in range(NC)}

        def emit_proj_tile(bi):
            if not pend["tiles"]:
                return
            if not pend["waited"]:
                T.wait_ge(s_dve.h, pend["ctx_n"])
                T.wait_ge(s_w.h, wob_n[pend["rb"]])
                pend["waited"] = True
            qt_, oc_ = pend["tiles"].pop(0)
            qsl_ = slice(TQ * qt_, TQ * (qt_ + 1))
            bank = t_take_bank(bi)
            sem_, n_ = xb_cast_n[(qt_, oc_)]
            T.wait_ge(sem_.h, n_)
            T.matmul(bank[:, :], wob[pend["rb"]][:, 128 * oc_:128 * (oc_ + 1)],
                     ctxp1[pend["p"] % 2][:, qsl_], start=True, stop=False)
            i = T.matmul(bank[:, :], identb[:, :], xb[oc_][:, qsl_],
                         start=False, stop=True)
            s_mm.mark(i)
            A.wait_ge(s_mm.h, s_mm.n)
            ia = A.activation(x[oc_][:, qsl_], bank[:, :], AF.Copy)
            s_act.mark(ia)
            evac_mark(bi, s_act)
            G.wait_ge(s_act.h, s_act.n)
            nw = s_w.mark(G.dma_start(out=xb[oc_][:, qsl_], in_=x[oc_][:, qsl_]), 16)
            xb_cast_n[(qt_, oc_)] = (s_w, nw)
            if not pend["tiles"]:
                wob_use[pend["rb"]] = s_mm.n

        for head in range(NH):
            hb = head % 2
            hh = head % 2
            hof = 64 * hh
            hsl = slice(hof, hof + 64)
            genv_chunks = []
            if head + 1 < NH:
                l1_gen_head(head + 1, defer_v=True)
                genv_chunks = [(j, (head + 1) % 2) for j in range(NT)]
            T.wait_ge(s_dve.h, gen_n[hb])
            for qt in range(NT):
                qsl = slice(TQ * qt, TQ * (qt + 1))
                nkc = 4 * qt + 4
                dbi, cbi = (3, 4) if blk1 % 2 == 0 else (5, 6)
                blk1 += 1
                dps = t_take_bank(dbi)[hsl, :]
                cps = t_take_bank(cbi)[hsl, :]
                pt_ready = [0] * 3

                def emit_S1(kc):
                    bi = kc % 3
                    bank = t_take_bank(bi)
                    i = T.matmul(bank[:, :], k96[hb][:, 128 * kc:128 * (kc + 1)],
                                 q96[hb][:, qsl], start=True, stop=True)
                    s_mm.mark(i)
                    A.wait_ge(s_mm.h, s_mm.n)
                    A.wait_ge(s_mm.h, pt_reader[kc % 3])
                    ia = A.activation(pt1[kc % 3][:, :], bank[:, :], AF.Exp, scale=scale1)
                    s_act.mark(ia)
                    evac_mark(bi, s_act)
                    s_act.wait(V)
                    i = V.tensor_scalar(pt1[kc % 3][:, :], pt1[kc % 3][:, :],
                                        CLO, CHI, OP.max, OP.min)
                    s_dve.mark(i)
                    s_ = kc - 4 * qt
                    if s_ >= 0:
                        blk = slice(128 * s_, 128 * (s_ + 1))
                        i = V.tensor_tensor(pt1[kc % 3][:, blk], pt1[kc % 3][:, blk],
                                            tri[:, :], OP.mult)
                        if s_ > 0:
                            i = V.memset(pt1[kc % 3][:, 0:128 * s_], 0.0)
                        s_dve.mark(i)
                    pt_ready[kc % 3] = s_dve.n

                emit_S1(0)
                if nkc > 1:
                    emit_S1(1)
                for kc in range(nkc):
                    if kc + 2 < nkc:
                        emit_S1(kc + 2)
                    T.wait_ge(s_dve.h, pt_ready[kc % 3])
                    T.matmul(dps, ones[:, 0:64], pt1[kc % 3][:, :],
                             start=(kc == 0), stop=(kc == nkc - 1))
                    i = T.matmul(cps, vtm1[kc][:, 64 * head:64 * (head + 1)],
                                 pt1[kc % 3][:, :], start=(kc == 0), stop=(kc == nkc - 1))
                    s_mm.mark(i)
                    pt_reader[kc % 3] = s_mm.n
                s_mm.wait(V)
                V.reciprocal(dps, dps)
                cdst = ctxp1[(head // 2) % 2]
                V.tensor_copy(cdst[hsl, qsl], cps)
                i = V.tensor_tensor(cdst[hsl, qsl], cdst[hsl, qsl], dps, OP.mult)
                s_dve.mark(i)
                evac_mark(dbi, s_dve)
                evac_mark(cbi, s_dve)
                ob1, ob2 = (5, 6) if dbi == 3 else (3, 4)
                for b_ in (7, ob1, ob2, 7):
                    emit_proj_tile(b_)
                for _ in range(2):
                    if genv_chunks:
                        if len(genv_chunks) == NT:
                            s_act.wait(V)
                            V.wait_ge(s_dve.h, kr_n)
                        j_, hb_ = genv_chunks.pop(0)
                        csl_ = slice(TQ * j_, TQ * (j_ + 1))
                        V.tensor_copy(k96[hb_][64:96, csl_], kr64[64:96, csl_])
                        n_ = rope_cols(q96[hb_], rcos1, rsin1, rsc, csl_,
                                       base=64, nrows=32)
                        if not genv_chunks:
                            gen_n[hb_] = n_
            if hh == 1:
                p = head // 2
                while pend["tiles"]:
                    emit_proj_tile(7)
                if p < 7:
                    load_wob(p + 1)
                pend.update(p=p, rb=p % 2, ctx_n=s_dve.n, waited=False,
                            tiles=[(q_, o_) for q_ in range(NT) for o_ in range(NC)])
                if DEV_FORCE_FLUSH:
                    while pend["tiles"]:
                        emit_proj_tile(7)
        rot = 0
        while pend["tiles"]:
            emit_proj_tile((3, 4, 5, 6, 7)[rot % 5])
            rot += 1

    with contextlib.ExitStack() as st3, nc.named_scope("l1_mla"):
        l1_mla(st3)
    with nc.named_scope("ln2"):
        postln()
    store_tap("x3")

    def l1_ffn(st):
        win = st.enter_context(nc.sbuf_tensor("win", [128, NC * 64], BF16))
        wout = st.enter_context(nc.sbuf_tensor("wout", [64, 1024], BF16))
        s_sb = st.enter_context(nc.sbuf_tensor("s_sb", [64, S], F32))
        gA = st.enter_context(nc.sbuf_tensor("gA", [64, S], F32))
        gB = st.enter_context(nc.sbuf_tensor("gB", [64, S], F32))
        gbf = st.enter_context(nc.sbuf_tensor("gbf", [64, S], BF16))
        h_i = [st.enter_context(nc.sbuf_tensor(f"h_i{k}", [64, S], F32))
               for k in range(2)]
        print("sbuf left (l1 ffn):", nc.sbuf_bytes_remaining)

        G.wait_ge(s_mm.h, s_mm.n)
        G.wait_ge(s_dve.h, s_dve.n)
        G.wait_ge(s_act.h, s_act.n)
        cast_xb()
        lwin = s_w.mark(G.dma_start(out=win[:, :].rearrange("p (c m) -> p c m", m=64),
                                    in_=dm["Win"].ap().rearrange("(c p) m -> p c m", p=128)), 16)
        lwout = s_w.mark(G.dma_start(out=wout[:, :], in_=dm["Wout"].ap()), 16)

        s_dve.wait(T)
        T.wait_ge(s_w.h, lwin)
        for t in range(NT):
            tsl = slice(TQ * t, TQ * (t + 1))
            bi = 4 + (t % 2)
            bank = t_take_bank(bi)
            for c in range(NC):
                i = T.matmul(bank[0:64, :], win[:, 64 * c:64 * (c + 1)],
                             xb[c][:, tsl], start=(c == 0), stop=(c == NC - 1))
            s_mm.mark(i)
            s_mm.wait(A)
            ia = A.activation(s_sb[:, tsl], bank[0:64, :], AF.Identity, bias=binc_s[:, :])
            s_act.mark(ia)
            evac_mark(bi, s_act)
        ia = A.activation(gA[:, :], s_sb[:, :], AF.Identity, bias=b2f_s[:, :], scale=0.0)
        s_act.mark(ia)
        gs = [gA, gB]
        h_marks = [0, 0]
        for it in range(16):
            hb_ = it % 2
            A.wait_ge(s_dve.h, h_marks[hb_])
            ia = A.activation(h_i[hb_][:, :], s_sb[:, :], AF.Relu,
                              scale=w1f_s[:, it:it + 1], bias=b1f_s[:, it:it + 1])
            s_act.mark(ia)
            s_act.wait(V)
            dst = gbf if it == 15 else gs[(it + 1) % 2]
            iv = V.scalar_tensor_tensor(dst[:, :], h_i[hb_][:, :], w2f_s[:, it:it + 1],
                                        gs[it % 2][:, :], OP.mult, OP.add)
            s_dve.mark(iv)
            h_marks[hb_] = s_dve.n
        gbf_n = s_dve.n
        T.wait_ge(s_w.h, lwout)
        T.wait_ge(s_dve.h, gbf_n)
        for qt in range(NT):
            qsl = slice(TQ * qt, TQ * (qt + 1))
            for oc in range(NC):
                bi = 4 + ((2 * qt + oc) % 4)
                bank = t_take_bank(bi)
                T.matmul(bank[:, :], wout[:, 128 * oc:128 * (oc + 1)],
                         gbf[:, qsl], start=True, stop=False)
                i = T.matmul(bank[:, :], identb[:, :], xb[oc][:, qsl],
                             start=False, stop=True)
                s_mm.mark(i)
                A.wait_ge(s_mm.h, s_mm.n)
                ia = A.activation(x[oc][:, qsl], bank[:, :], AF.Identity,
                                  bias=bout_p[:, oc:oc + 1])
                s_act.mark(ia)
                evac_mark(bi, s_act)

    with contextlib.ExitStack() as st4, nc.named_scope("l1_ffn"):
        l1_ffn(st4)
    with nc.named_scope("ln3"):
        postln(store=True)

    s_st.wait(Y)
    return nc


def prep_inputs(inputs):
    WqT = np.ascontiguousarray(inputs["L0_attn_Wq"].T)
    WkT = np.ascontiguousarray(inputs["L0_attn_Wk"].T)
    WvT = np.ascontiguousarray(inputs["L0_attn_Wv"].T)
    WoT = np.ascontiguousarray(inputs["L0_attn_Wo"].T)
    W_UQ = inputs["L1_attn_W_UQ"]; W_QR = inputs["L1_attn_W_QR"]
    Wqcat = np.zeros((256, NH * 96), np.float32)
    for h in range(NH):
        Wqcat[:, 96 * h:96 * h + 64] = W_UQ[:, 64 * h:64 * (h + 1)]
        Wqcat[:, 96 * h + 64:96 * (h + 1)] = W_QR[:, 32 * h:32 * (h + 1)]

    def rope_tab(nrows, period, d_rope):
        j = np.arange(nrows)
        r = (j % period) // 2
        inv = 10000.0 ** (-2.0 * r / d_rope)
        ang = inv[:, None] * np.arange(S, dtype=np.float64)[None, :]
        cosm = np.cos(ang)
        sinm = np.sin(ang) * np.where(j % 2 == 0, -1.0, 1.0)[:, None]
        return _bfa(cosm), _bfa(sinm)

    rcos0, rsin0 = rope_tab(128, 64, 64)
    rcos1, rsin1 = rope_tab(128, 32, 32)
    tri = _bfa(np.triu(np.ones((128, 128))))
    onesm = _bfa(np.ones((128, 128)))
    ident = _bfa(np.eye(128))
    trif = np.triu(np.ones((128, 128), np.float32))
    identf = np.eye(128, dtype=np.float32)
    jj = np.arange(128)
    mask16f = ((jj[:, None] % 16) == np.arange(16)[None, :]).astype(np.float32)
    mask16fT = np.ascontiguousarray(mask16f.T[:, :128]).astype(np.float32)
    maskPfx16 = (((jj[:, None] % 16) == (jj[None, :] % 16)) &
                 ((jj[:, None] // 16) < (jj[None, :] // 16))).astype(np.float32)
    maskJ16 = ((jj[:, None] % 16) == (jj[None, :] % 16)).astype(np.float32)
    shift8 = np.zeros((16, 16), np.float32)
    for i_ in range(8):
        shift8[i_, i_ + 8] = 1.0
    oh127f = np.zeros((128, 1), np.float32); oh127f[127, 0] = 1.0
    sel8 = np.zeros((8, 8 * 128), np.float32)
    for e in range(E):
        sel8[e, 128 * e:128 * (e + 1)] = 1.0
    sel8 = _bfa(sel8)

    def pcol(v):
        return np.ascontiguousarray(np.asarray(v, np.float32).reshape(NC, 128).T)

    shared = dict(
        WqT=WqT, WkT=WkT, WvT=WvT, WoT=WoT,
        bq_p=pcol(inputs["L0_attn_bq"]), bk_p=pcol(inputs["L0_attn_bk"]),
        bo_p=pcol(inputs["L0_attn_bo"]), bv=inputs["L0_attn_bv"][None, :],
        Wg=inputs["L0_ffn_Wg"], W1=inputs["L0_ffn_W1"], b1=inputs["L0_ffn_b1"],
        W2=inputs["L0_ffn_W2"], b2=inputs["L0_ffn_b2"],
        b1_p=np.ascontiguousarray(
            inputs["L0_ffn_b1"].reshape(E * H_FF // 128, 128).T),
        W_DQ=inputs["L1_attn_W_DQ"], W_DKV=inputs["L1_attn_W_DKV"],
        W_UQ=inputs["L1_attn_W_UQ"], W_QR=inputs["L1_attn_W_QR"], Wqcat=Wqcat,
        W_UK=inputs["L1_attn_W_UK"], W_UV=inputs["L1_attn_W_UV"],
        W_KR=inputs["L1_attn_W_KR"], W_O=inputs["L1_attn_W_O"],
        Win=inputs["L1_ffn_Win"], binc=inputs["L1_ffn_bin_"][:, None],
        Wout=inputs["L1_ffn_Wout"],
        W1f=inputs["L1_ffn_W1"], b1f=inputs["L1_ffn_b1"], W2f=inputs["L1_ffn_W2"],
        b2f=inputs["L1_ffn_b2"][:, None], bout_p=pcol(inputs["L1_ffn_bout"]),
        tri=tri, ones=onesm, ident=ident, trif=trif, identf=identf,
        mask16f=mask16f, mask16fT=mask16fT, maskPfx16=maskPfx16, maskJ16=maskJ16,
        shift8=shift8, oh127f=oh127f, sel8=sel8,
        b2_p=np.ascontiguousarray(inputs["L0_ffn_b2"].reshape(E * NC, 128).T),
        rcos0=rcos0, rsin0=rsin0, rcos1=rcos1, rsin1=rsin1,
    )
    shared = {k: np.ascontiguousarray(np.asarray(v, dtype=np.float32)
                                      if np.asarray(v).dtype == np.float64 else v)
              for k, v in shared.items()}
    in_maps = []
    xfull = inputs["L0_attn_x"]
    for i in range(NCORES):
        m = dict(shared)
        m["x_in"] = np.ascontiguousarray(np.asarray(xfull[i]).T.astype(np.float32))
        lt = np.zeros((8, 1), np.float32); lt[:i, 0] = 1.0
        m["mask_lt"] = lt
        in_maps.append(m)
    return in_maps


def kernel(**inputs):
    in_maps = prep_inputs(inputs)
    nc = build_program(DEV_TAPS)
    res = run_bass_kernel_spmd(nc, in_maps, core_ids=list(range(NCORES)))
    out = np.stack([res.results[i]["out_x"].T for i in range(NCORES)])
    if DEV_TAPS:
        kernel.tap_results = [
            {t: res.results[i]["tap_" + t].T for t in DEV_TAPS} for i in range(NCORES)
        ]
    return np.ascontiguousarray(out, dtype=np.float32)


# version 15
# speedup vs baseline: 1.7059x; 1.0059x over previous
import math
import contextlib
import numpy as np
import ml_dtypes
import concourse.bass as bass
import concourse.mybir as mybir
from concourse.bass_utils import run_bass_kernel_spmd

F32 = mybir.dt.float32
BF16 = mybir.dt.bfloat16
OP = mybir.AluOpType
AF = mybir.ActivationFunctionType
AX = mybir.AxisListType.X

NCORES = 8
S = 2048
D = 1024
NH = 16
DK = 64
TQ = 512
NT = S // TQ
NC = D // 128
E = 8
H_FF = 4096

DEV_TAPS = []
DEV_FORCE_COMBINE1 = False
DEV_FORCE_FLUSH = False
SHUF_EO = []
for _g in range(16):
    SHUF_EO += [2 * _g + 1, 2 * _g]


def _bfa(a):
    return np.asarray(a, dtype=np.float64).astype(ml_dtypes.bfloat16)


class Sem:
    def __init__(self, nc, name):
        self.h = nc.alloc_semaphore(name=name)
        self.n = 0

    def mark(self, inst, k=1):
        inst.then_inc(self.h, k)
        self.n += k
        return self.n

    def wait(self, eng, at=None):
        v = self.n if at is None else at
        if v > 0:
            eng.wait_ge(self.h, v)


def build_program(taps):
    global DEV_FORCE_COMBINE1
    nc = bass.Bass("TRN2", target_bir_lowering=False)
    T, V, A, G, Y = nc.tensor, nc.vector, nc.scalar, nc.gpsimd, nc.sync
    s_w = Sem(nc, "s_w")
    s_io = Sem(nc, "s_io")
    s_mm = Sem(nc, "s_mm")
    s_act = Sem(nc, "s_act")
    s_dve = Sem(nc, "s_dve")
    s_cc = Sem(nc, "s_cc")
    s_st = Sem(nc, "s_st")

    dm = {}

    def din(name, shape, dt=F32):
        dm[name] = nc.dram_tensor(name, list(shape), dt, kind="ExternalInput")
        return dm[name]

    din("x_in", (D, S))
    din("WqT", (D, D)); din("WkT", (D, D)); din("WvT", (D, D)); din("WoT", (D, D))
    din("bq_p", (128, NC)); din("bk_p", (128, NC)); din("bo_p", (128, NC))
    din("bv", (1, D))
    din("Wg", (D, E))
    din("W1", (E, D, H_FF)); din("b1", (E, H_FF))
    din("W2", (E, H_FF, D)); din("b2", (E, D))
    din("W_DQ", (D, 256)); din("W_DKV", (D, 256))
    din("W_UQ", (256, D)); din("W_QR", (256, 512)); din("Wqcat", (256, NH * 96))
    din("W_UK", (256, D)); din("W_UV", (256, D))
    din("W_KR", (D, 32))
    din("W_O", (D, D))
    din("Win", (D, 64)); din("binc", (64, 1)); din("Wout", (64, D))
    din("W1f", (64, 16)); din("b1f", (64, 16)); din("W2f", (64, 16))
    din("b2f", (64, 1)); din("bout_p", (128, NC))
    din("b1_p", (128, E * H_FF // 128))
    din("tri", (128, 128), BF16)
    din("ones", (128, 128), BF16)
    din("ident", (128, 128), BF16)
    din("trif", (128, 128))
    din("identf", (128, 128))
    din("mask16f", (128, 16))
    din("mask16fT", (16, 128))
    din("maskPfx16", (128, 128))
    din("maskJ16", (128, 128))
    din("shift8", (16, 16))
    din("oh127f", (128, 1))
    din("sel8", (8, 8 * 128), BF16)
    din("mask_lt", (8, 1))
    din("b2_p", (128, E * NC // 1))
    din("rcos0", (128, S), BF16); din("rsin0", (128, S), BF16)
    din("rcos1", (128, S), BF16); din("rsin1", (128, S), BF16)

    out_x = nc.dram_tensor("out_x", [D, S], F32, kind="ExternalOutput")
    dbg = nc.dram_tensor("dbg", [128, 4096], F32, kind="ExternalOutput") if "dbg" in taps else None
    cc_in = nc.dram_tensor("cc_in", [1, 16], F32, kind="Internal")
    cc_out = nc.dram_tensor("cc_out", [8, 16], F32, kind="Internal")
    tap_dram = {t: nc.dram_tensor("tap_" + t, [D, S], F32, kind="ExternalOutput")
                for t in taps}

    def sb(name, shape, dt=BF16):
        return nc.alloc_sbuf_tensor(name, list(shape), dt)

    x = [sb(f"x{c}", [128, S], F32) for c in range(NC)]
    xb = [sb(f"xb{c}", [128, S], BF16) for c in range(NC)]
    tri = sb("tri_s", [128, 128]); ones = sb("ones_s", [128, 128])
    onescf = sb("onescf_s", [128, 1], F32)
    w1f_s = sb("w1f_s", [64, 16], F32); b1f_s = sb("b1f_s", [64, 16], F32)
    w2f_s = sb("w2f_s", [64, 16], F32); binc_s = sb("binc_s", [64, 1], F32)
    b2f_s = sb("b2f_s", [64, 1], F32); bout_p = sb("bout_p_s", [128, NC], F32)
    bq_p = sb("bq_p_s", [128, NC], F32); bk_p = sb("bk_p_s", [128, NC], F32)
    bo_p = sb("bo_p_s", [128, NC], F32)
    bvr = sb("bv_s", [1, D])
    identb = sb("identb_s", [128, 128])
    epsln = sb("epsln", [128, 1], F32)

    for c in range(NC):
        s_io.mark(Y.dma_start(out=x[c][:, :], in_=dm["x_in"].ap()[128 * c:128 * (c + 1), :]), 16)
    for name, t_ in (("tri", tri), ("ones", ones), ("ident", identb),
                     ("bq_p", bq_p), ("bk_p", bk_p), ("bo_p", bo_p),
                     ("W1f", w1f_s), ("b1f", b1f_s), ("W2f", w2f_s),
                     ("binc", binc_s), ("b2f", b2f_s), ("bout_p", bout_p)):
        s_io.mark(Y.dma_start(out=t_[:, :], in_=dm[name].ap()), 16)
    s_w.mark(G.dma_start(out=bvr[:, :], in_=dm["bv"].ap()), 16)
    s_dve.mark(V.memset(epsln[:, :], 1e-5))
    s_dve.mark(V.memset(onescf[:, :], 1.0))
    io_consts = s_io.n
    w_consts = s_w.n
    eps_n = s_dve.n

    ps = [nc.alloc_psum_tensor(f"ps{i}", [128, TQ], F32) for i in range(8)]
    bank_free = [(s_mm, 0)] * 8

    def t_take_bank(i):
        sem, n = bank_free[i]
        sem.wait(T, n)
        return ps[i]

    def evac_mark(i, sem):
        bank_free[i] = (sem, sem.n)

    def vseq(inst):
        s_dve.mark(inst)
        s_dve.wait(V)
        return s_dve.n

    def cast_xb():
        s_act.wait(V)
        for c in range(NC):
            i = V.tensor_copy(xb[c][:, :], x[c][:, :])
        s_dve.mark(i)
        return s_dve.n

    def rope_cols(z, cosT, sinT, shufbuf, csl, base=0, nrows=128):
        sl = slice(base, base + nrows)
        V.stream_shuffle(shufbuf[sl, csl], z[sl, csl], SHUF_EO)
        V.tensor_tensor(z[sl, csl], z[sl, csl], cosT[sl, csl], OP.mult)
        V.tensor_tensor(shufbuf[sl, csl], shufbuf[sl, csl], sinT[sl, csl], OP.mult)
        i = V.tensor_tensor(z[sl, csl], z[sl, csl], shufbuf[sl, csl], OP.add)
        s_dve.mark(i)
        return s_dve.n

    def rope_inplace(z, nrows, cosT, sinT, shufbuf, base=0):
        sl = slice(base, base + nrows)
        V.stream_shuffle(shufbuf[sl, :], z[sl, :], SHUF_EO)
        V.tensor_tensor(z[sl, :], z[sl, :], cosT[sl, :], OP.mult)
        V.tensor_tensor(shufbuf[sl, :], shufbuf[sl, :], sinT[sl, :], OP.mult)
        i = V.tensor_tensor(z[sl, :], z[sl, :], shufbuf[sl, :], OP.add)
        s_dve.mark(i)
        return s_dve.n

    def postln(store=False):
        with contextlib.ExitStack() as stl:
            uid = str(s_dve.n)
            sqb = [stl.enter_context(nc.sbuf_tensor(f"sqb{uid}_{i}", [128, TQ], BF16))
                   for i in range(2)]
            mtile = [stl.enter_context(nc.sbuf_tensor(f"mt{uid}_{i}", [128, TQ], F32))
                     for i in range(2)]
            ritile = [stl.enter_context(nc.sbuf_tensor(f"rt{uid}_{i}", [128, TQ], F32))
                      for i in range(2)]
            s_act.wait(V)
            s_dve.wait(A)

            def ln_copies(t):
                tsl = slice(TQ * t, TQ * (t + 1))
                for c in range(NC):
                    i1 = A.activation(xb[c][:, tsl], x[c][:, tsl], AF.Copy)
                s_act.mark(i1)
                return s_act.n

            def ln_apply(t, tb):
                tsl = slice(TQ * t, TQ * (t + 1))
                for c in range(NC):
                    V.tensor_tensor(x[c][:, tsl], x[c][:, tsl], mtile[tb][:, :],
                                    OP.subtract)
                    i7 = V.tensor_tensor(x[c][:, tsl], x[c][:, tsl], ritile[tb][:, :],
                                         OP.mult)
                s_dve.mark(i7)
                if store:
                    s_dve.wait(Y)
                    for c in range(NC):
                        s_st.mark(Y.dma_start(out=out_x.ap()[128 * c:128 * (c + 1), tsl],
                                              in_=x[c][:, tsl]), 16)

            cp_n = [0] * NT
            cp_n[0] = ln_copies(0)
            sq_free = [(s_mm, 0), (s_mm, 0)]
            for t in range(NT):
                tb = t % 2
                tsl = slice(TQ * t, TQ * (t + 1))
                for c in range(NC):
                    sem, n = sq_free[c % 2]
                    sem.wait(A, n)
                    i2 = A.activation(sqb[c % 2][:, :], xb[c][:, tsl], AF.Square)
                    s_act.mark(i2)
                    if c == 0:
                        b6 = t_take_bank(6 if tb == 0 else 4)
                        b7 = t_take_bank(7 if tb == 0 else 5)
                        T.wait_ge(s_act.h, cp_n[t])
                    T.matmul(b6[:, :], ones[:, :], xb[c][:, tsl],
                             start=(c == 0), stop=(c == NC - 1))
                    s_act.wait(T)
                    i3 = T.matmul(b7[:, :], ones[:, :], sqb[c % 2][:, :],
                                  start=(c == 0), stop=(c == NC - 1))
                    s_mm.mark(i3)
                    sq_free[c % 2] = (s_mm, s_mm.n)
                s_mm.wait(V)
                V.tensor_scalar(mtile[tb][:, :], b6[:, :], 1.0 / D, None, OP.mult)
                V.tensor_scalar(ritile[tb][:, :], b7[:, :], 1.0 / D, None, OP.mult)
                V.tensor_tensor(b6[:, :], mtile[tb][:, :], mtile[tb][:, :], OP.mult)
                i4 = V.tensor_tensor(ritile[tb][:, :], ritile[tb][:, :], b6[:, :],
                                     OP.subtract)
                s_dve.mark(i4)
                evac_mark(6 if tb == 0 else 4, s_dve)
                evac_mark(7 if tb == 0 else 5, s_dve)
                s_dve.wait(A)
                i5 = A.activation(ritile[tb][:, :], ritile[tb][:, :], AF.Sqrt,
                                  bias=epsln[:, :])
                s_act.mark(i5)
                s_act.wait(V)
                V.reciprocal(ritile[tb][:, :], ritile[tb][:, :])
                if t + 1 < NT:
                    cp_n[t + 1] = ln_copies(t + 1)
                if t >= 1:
                    ln_apply(t - 1, (t - 1) % 2)
            ln_apply(NT - 1, (NT - 1) % 2)
        return s_dve.n

    def store_tap(name):
        if name in tap_dram:
            s_dve.wait(Y)
            s_act.wait(Y)
            for c in range(NC):
                s_st.mark(Y.dma_start(out=tap_dram[name].ap()[128 * c:128 * (c + 1), :],
                                      in_=x[c][:, :]), 16)

    def l0_attention(st):
        wbig = st.enter_context(nc.sbuf_tensor("wbig", [128, NC * 1024], BF16))
        vtm = [st.enter_context(nc.sbuf_tensor(f"vtm{t}", [128, D], BF16)) for t in range(16)]
        qa = [st.enter_context(nc.sbuf_tensor(f"qa{i}", [128, S], BF16)) for i in range(2)]
        ka = [st.enter_context(nc.sbuf_tensor(f"ka{i}", [128, S], BF16)) for i in range(2)]
        wqr = [st.enter_context(nc.sbuf_tensor(f"wqr{i}", [128, NC * 128], BF16)) for i in range(2)]
        pt = [st.enter_context(nc.sbuf_tensor(f"pt{k}", [128, TQ], BF16)) for k in range(3)]
        ctxa = [st.enter_context(nc.sbuf_tensor(f"ctxa{c}", [128, S], BF16)) for c in range(6)]
        stA = st.enter_context(contextlib.ExitStack())
        shufb = stA.enter_context(nc.sbuf_tensor("shufb", [128, S], BF16))
        rcos0 = stA.enter_context(nc.sbuf_tensor("rcos0_s", [128, S], BF16))
        rsin0 = stA.enter_context(nc.sbuf_tensor("rsin0_s", [128, S], BF16))
        s_io.mark(Y.dma_start(out=rcos0[:, :], in_=dm["rcos0"].ap()), 16)
        s_io.mark(Y.dma_start(out=rsin0[:, :], in_=dm["rsin0"].ap()), 16)
        print("sbuf left (l0 attn):", nc.sbuf_bytes_remaining)

        s_io.wait(V, io_consts)
        cast_xb()

        wv_n = s_w.mark(G.dma_start(
            out=wbig[:, :].rearrange("p (c n) -> p c n", n=1024),
            in_=dm["WvT"].ap().rearrange("(c p) n -> p c n", p=128)), 16)
        T.wait_ge(s_w.h, wv_n)
        s_dve.wait(T)
        s_io.wait(T, io_consts)
        for tt in range(16):
            tsl = slice(128 * tt, 128 * (tt + 1))
            for half in range(2):
                bi = (2 * tt + half) % 4
                bank = t_take_bank(bi)
                for c in range(NC):
                    T.matmul(bank[:, :], xb[c][:, tsl],
                             wbig[:, 1024 * c + 512 * half:1024 * c + 512 * (half + 1)],
                             start=(c == 0), stop=False)
                i = T.matmul(bank[:, :], ones[0:1, 0:128],
                             bvr[0:1, 512 * half:512 * (half + 1)], start=False, stop=True)
                s_mm.mark(i)
                s_mm.wait(A)
                ia = A.activation(vtm[tt][:, 512 * half:512 * (half + 1)], bank[:, :], AF.Copy)
                s_act.mark(ia)
                evac_mark(bi, s_act)

        G.wait_ge(s_mm.h, s_mm.n)
        wo_n = s_w.mark(G.dma_start(
            out=wbig[:, :].rearrange("p (c n) -> p c n", n=1024),
            in_=dm["WoT"].ap().rearrange("(c p) n -> p c n", p=128)), 16)

        ring_use = [0, 0]
        qk_rope_n = [0, 0]
        wld = [[0] * 8, [0] * 8]

        def l0_load_pair(p):
            csl = slice(128 * p, 128 * (p + 1))
            for wi, wname in enumerate(("WqT", "WkT")):
                G.wait_ge(s_mm.h, ring_use[wi])
                wld[wi][p] = s_w.mark(G.dma_start(
                    out=wqr[wi][:, :].rearrange("p (c n) -> p c n", n=128),
                    in_=dm[wname].ap()[:, csl].rearrange("(c p) n -> p c n", p=128)), 16)

        def l0_gen_pair(p, defer_rope=False):
            pb = p % 2
            for wi, (dst, bias_) in enumerate(((qa[pb], bq_p), (ka[pb], bk_p))):
                ring = wqr[wi]
                T.wait_ge(s_w.h, wld[wi][p])
                for t in range(NT):
                    tsl = slice(TQ * t, TQ * (t + 1))
                    bi = 7 if t % 2 == 0 else 0
                    bank = t_take_bank(bi)
                    for c in range(NC):
                        i = T.matmul(bank[:, :], ring[:, 128 * c:128 * (c + 1)],
                                     xb[c][:, tsl], start=(c == 0), stop=(c == NC - 1))
                    s_mm.mark(i)
                    A.wait_ge(s_mm.h, s_mm.n)
                    ia = A.activation(dst[:, tsl], bank[:, :], AF.Identity,
                                      bias=bias_[:, p:p + 1])
                    s_act.mark(ia)
                    evac_mark(bi, s_act)
                ring_use[wi] = s_mm.n
            if p + 1 < 8:
                l0_load_pair(p + 1)
            if not defer_rope:
                _rope_pair(p)

        def _rope_pair(p):
            pb = p % 2
            s_act.wait(V)
            rope_inplace(qa[pb], 128, rcos0, rsin0, shufb)
            rope_inplace(ka[pb], 128, rcos0, rsin0, shufb)
            qk_rope_n[pb] = s_dve.n

        l0_load_pair(0)
        l0_gen_pair(0)
        pt_reader = [0] * 3
        blk_i = 0
        for p in range(8):
            pb = p % 2
            rope_chunks = []
            if p + 1 < 8:
                dr = (p + 1 < 7)
                l0_gen_pair(p + 1, defer_rope=dr)
                if dr:
                    npb = (p + 1) % 2
                    rope_chunks = [(qa[npb], j) for j in range(NT)] + \
                                  [(ka[npb], j) for j in range(NT)]
            if p == 6:
                stA.close()
                for c in (6, 7):
                    ctxa.append(st.enter_context(nc.sbuf_tensor(f"ctxa{c}", [128, S], BF16)))
            T.wait_ge(s_dve.h, qk_rope_n[pb])
            for qt in range(NT):
                qsl = slice(TQ * qt, TQ * (qt + 1))
                nkc = 4 * qt + 4
                for hh in range(2):
                    head = 2 * p + hh
                    hsl = slice(64 * hh, 64 * hh + 64)
                    dbi, cbi = (3, 4) if blk_i % 2 == 0 else (5, 6)
                    blk_i += 1
                    dps = t_take_bank(dbi)[hsl, :]
                    cps = t_take_bank(cbi)[hsl, :]
                    pt_ready = [0] * 3

                    def emit_S(kc):
                        bi = kc % 3
                        bank = t_take_bank(bi)
                        i = T.matmul(bank[:, :], ka[pb][hsl, 128 * kc:128 * (kc + 1)],
                                     qa[pb][hsl, qsl], start=True, stop=True)
                        s_mm.mark(i)
                        A.wait_ge(s_mm.h, s_mm.n)
                        A.wait_ge(s_mm.h, pt_reader[kc % 3])
                        ia = A.activation(pt[kc % 3][:, :], bank[:, :], AF.Exp, scale=0.125)
                        s_act.mark(ia)
                        evac_mark(bi, s_act)
                        s_ = kc - 4 * qt
                        if s_ >= 0:
                            s_act.wait(V)
                            blk = slice(128 * s_, 128 * (s_ + 1))
                            i = V.tensor_tensor(pt[kc % 3][:, blk], pt[kc % 3][:, blk],
                                                tri[:, :], OP.mult)
                            if s_ > 0:
                                i = V.memset(pt[kc % 3][:, 0:128 * s_], 0.0)
                            s_dve.mark(i)
                            pt_ready[kc % 3] = ("dve", s_dve.n)
                        else:
                            pt_ready[kc % 3] = ("act", s_act.n)

                    emit_S(0)
                    if nkc > 1:
                        emit_S(1)
                    for kc in range(nkc):
                        if kc + 2 < nkc:
                            emit_S(kc + 2)
                        kind, n_ = pt_ready[kc % 3]
                        if kind == "dve":
                            T.wait_ge(s_dve.h, n_)
                        else:
                            T.wait_ge(s_act.h, n_)
                        T.matmul(dps, ones[:, hsl], pt[kc % 3][:, :],
                                 start=(kc == 0), stop=(kc == nkc - 1))
                        i = T.matmul(cps, vtm[kc][:, 64 * head:64 * (head + 1)],
                                     pt[kc % 3][:, :], start=(kc == 0), stop=(kc == nkc - 1))
                        s_mm.mark(i)
                        pt_reader[kc % 3] = s_mm.n
                    s_mm.wait(V)
                    V.tensor_scalar(dps, dps, 1e-8, None, OP.add)
                    V.reciprocal(dps, dps)
                    V.tensor_copy(ctxa[p][hsl, qsl], cps)
                    i = V.tensor_tensor(ctxa[p][hsl, qsl], ctxa[p][hsl, qsl], dps, OP.mult)
                    s_dve.mark(i)
                    evac_mark(dbi, s_dve)
                    evac_mark(cbi, s_dve)
                    for _ in range(2):
                        if rope_chunks:
                            if len(rope_chunks) == 2 * NT:
                                s_act.wait(V)
                            z_, j_ = rope_chunks.pop(0)
                            n_ = rope_cols(z_, rcos0, rsin0, shufb,
                                           slice(TQ * j_, TQ * (j_ + 1)))
                            if not rope_chunks:
                                qk_rope_n[(p + 1) % 2] = n_
        ctx_n = s_dve.n
        T.wait_ge(s_dve.h, ctx_n)
        T.wait_ge(s_w.h, wo_n)
        for qt in range(NT):
            qsl = slice(TQ * qt, TQ * (qt + 1))
            for oc in range(NC):
                bi = 3 + (oc % 2)
                bank = t_take_bank(bi)
                for c in range(NC):
                    T.matmul(bank[:, :],
                             wbig[:, 1024 * c + 128 * oc:1024 * c + 128 * (oc + 1)],
                             ctxa[c][:, qsl], start=(c == 0), stop=False)
                i = T.matmul(bank[:, :], identb[:, :], xb[oc][:, qsl],
                             start=False, stop=True)
                s_mm.mark(i)
                A.wait_ge(s_mm.h, s_mm.n)
                ia = A.activation(x[oc][:, qsl], bank[:, :], AF.Identity,
                                  bias=bo_p[:, oc:oc + 1])
                s_act.mark(ia)
                evac_mark(bi, s_act)

    with contextlib.ExitStack() as st1, nc.named_scope("l0_attn"):
        l0_attention(st1)
    with nc.named_scope("ln0"):
        postln()
    store_tap("x1")

    def l0_moe(st):
        NTC = 16
        h_sb = [st.enter_context(nc.sbuf_tensor(f"hsb{i}", [128, 1024], BF16))
                for i in range(32)]
        w1r = [st.enter_context(nc.sbuf_tensor(f"w1r{i}", [128, 1024], BF16))
               for i in range(4)]
        w2r = [st.enter_context(nc.sbuf_tensor(f"w2r{i}", [128, 2048], BF16))
               for i in range(4)]
        cbc = st.enter_context(nc.sbuf_tensor("cbc", [128, S], BF16))
        cT = st.enter_context(nc.sbuf_tensor("cT", [8, S], BF16))
        gate_e = st.enter_context(nc.sbuf_tensor("gate_e", [128, 8], F32))
        cs_all = st.enter_context(nc.sbuf_tensor("cs_all", [128, 256], F32))
        oh_all = st.enter_context(nc.sbuf_tensor("oh_all", [128, 256], F32))
        w_all = st.enter_context(nc.sbuf_tensor("w_all", [128, 32], F32))
        sc1 = st.enter_context(nc.sbuf_tensor("sc1", [128, 1], F32))
        sc2 = st.enter_context(nc.sbuf_tensor("sc2", [128, 1], F32))
        sc3 = st.enter_context(nc.sbuf_tensor("sc3", [128, 1], F32))
        totb = [st.enter_context(nc.sbuf_tensor(f"totb{i}", [128, 1], F32)) for i in range(2)]
        ltot = st.enter_context(nc.sbuf_tensor("ltot", [16, 1], F32))
        gath = st.enter_context(nc.sbuf_tensor("gath", [8, 16], F32))
        cpfx = st.enter_context(nc.sbuf_tensor("cpfx", [16, 1], F32))
        offrow = st.enter_context(nc.sbuf_tensor("offrow", [1, 256], F32))
        prefb = st.enter_context(nc.sbuf_tensor("prefb", [128, 16], F32))
        tmpf = [st.enter_context(nc.sbuf_tensor("tmpf0", [128, TQ], F32))] * 2
        mm8 = st.enter_context(nc.sbuf_tensor("mm8", [128, 8], F32))
        trif = st.enter_context(nc.sbuf_tensor("trif_s", [128, 128], F32))
        identf = st.enter_context(nc.sbuf_tensor("identf_s", [128, 128], F32))
        mask16f = st.enter_context(nc.sbuf_tensor("mask16f_s", [128, 16], F32))
        mask16fT = st.enter_context(nc.sbuf_tensor("mask16fT_s", [16, 128], F32))
        maskPfx16 = st.enter_context(nc.sbuf_tensor("maskPfx16_s", [128, 128], F32))
        maskJ16 = st.enter_context(nc.sbuf_tensor("maskJ16_s", [128, 128], F32))
        shift8 = st.enter_context(nc.sbuf_tensor("shift8_s", [16, 16], F32))
        oh127f = st.enter_context(nc.sbuf_tensor("oh127f_s", [128, 1], F32))
        sel8 = st.enter_context(nc.sbuf_tensor("sel8_s", [8, 8 * 128], BF16))
        mask_lt = st.enter_context(nc.sbuf_tensor("mask_lt_s", [8, 1], F32))
        wg_s = st.enter_context(nc.sbuf_tensor("wg_s", [128, NC * E], F32))
        b1_p = st.enter_context(nc.sbuf_tensor("b1_p_s", [128, E * 32], F32))
        b2_p = st.enter_context(nc.sbuf_tensor("b2_p_s", [128, E * NC], F32))
        print("sbuf left (moe):", nc.sbuf_bytes_remaining)

        G.wait_ge(s_mm.h, s_mm.n)
        G.wait_ge(s_dve.h, s_dve.n)
        G.wait_ge(s_act.h, s_act.n)
        Y.wait_ge(s_dve.h, s_dve.n)
        Y.wait_ge(s_act.h, s_act.n)
        Y.wait_ge(s_mm.h, s_mm.n)
        for name, t_ in (("trif", trif), ("identf", identf),
                         ("mask16f", mask16f), ("mask16fT", mask16fT),
                         ("maskPfx16", maskPfx16), ("maskJ16", maskJ16),
                         ("shift8", shift8), ("oh127f", oh127f), ("sel8", sel8),
                         ("mask_lt", mask_lt), ("b2_p", b2_p), ("b1_p", b1_p)):
            s_io.mark(Y.dma_start(out=t_[:, :], in_=dm[name].ap()), 16)
        s_io.mark(Y.dma_start(out=wg_s[:, :].rearrange("p (c e) -> p c e", e=E),
                              in_=dm["Wg"].ap().rearrange("(c p) e -> p c e", p=128)), 16)

        cast_n = cast_xb()
        s_io.wait(T)

        for tc in range(NTC):
            tsl = slice(128 * tc, 128 * (tc + 1))
            b5 = t_take_bank(5)
            zp = b5[:, 0:8]
            for c in range(NC):
                i = T.matmul(zp, x[c][:, tsl], wg_s[:, E * c:E * (c + 1)],
                             start=(c == 0), stop=(c == NC - 1))
            s_mm.mark(i)
            s_mm.wait(V)
            i = V.reduce_max(sc1[:, :], zp, axis=AX, negate=True)
            s_dve.mark(i)
            s_dve.wait(A)
            ia = A.activation(gate_e[:, :], zp, AF.Exp, bias=sc1[:, :])
            s_act.mark(ia)
            evac_mark(5, s_act)
            s_act.wait(V)
            V.reduce_sum(sc3[:, :], gate_e[:, :], axis=AX)
            vseq(V.reduce_max(sc1[:, :], gate_e[:, :], axis=AX))
            vseq(V.tensor_scalar(oh_all[:, 16 * tc:16 * tc + 8], gate_e[:, :], sc1[:, :], None, OP.is_equal))
            vseq(V.tensor_scalar(mm8[:, :], oh_all[:, 16 * tc:16 * tc + 8], -1.0, 1.0, OP.mult, OP.add))
            vseq(V.tensor_tensor(mm8[:, :], gate_e[:, :], mm8[:, :], OP.mult))
            vseq(V.reduce_max(sc2[:, :], mm8[:, :], axis=AX))
            vseq(V.tensor_scalar(oh_all[:, 16 * tc + 8:16 * tc + 16], mm8[:, :], sc2[:, :], None, OP.is_equal))
            vseq(V.tensor_scalar(sc3[:, :], sc3[:, :], 1e-9, None, OP.mult))
            vseq(V.tensor_tensor(sc3[:, :], sc3[:, :], sc1[:, :], OP.add))
            vseq(V.tensor_tensor(sc3[:, :], sc3[:, :], sc2[:, :], OP.add))
            vseq(V.reciprocal(sc3[:, :], sc3[:, :]))
            V.tensor_tensor(w_all[:, 2 * tc:2 * tc + 1], sc1[:, :], sc3[:, :], OP.mult)
            i = V.tensor_tensor(w_all[:, 2 * tc + 1:2 * tc + 2], sc2[:, :], sc3[:, :], OP.mult)
            s_dve.mark(i)
            b6 = t_take_bank(6)
            s_dve.wait(T)
            i = T.matmul(b6[:, 0:16], trif[:, :], oh_all[:, 16 * tc:16 * (tc + 1)],
                         start=True, stop=True)
            s_mm.mark(i)
            s_mm.wait(V)
            i = V.tensor_copy(cs_all[:, 16 * tc:16 * (tc + 1)], b6[:, 0:16])
            s_dve.mark(i)
            evac_mark(6, s_dve)
        route_n = s_dve.n

        b7 = t_take_bank(7)
        T.wait_ge(s_dve.h, route_n)
        for half in range(2):
            i = T.matmul(b7[:, half:half + 1], cs_all[:, 128 * half:128 * (half + 1)],
                         oh127f[:, :], start=True, stop=True)
            s_mm.mark(i)
        s_mm.wait(V)
        V.tensor_copy(totb[0][:, :], b7[:, 0:1])
        i = V.tensor_copy(totb[1][:, :], b7[:, 1:2])
        s_dve.mark(i)
        evac_mark(7, s_dve)
        tot_n = s_dve.n
        b7 = t_take_bank(7)
        T.wait_ge(s_dve.h, tot_n)
        for half in range(2):
            i = T.matmul(b7[0:16, 2:3], mask16f[:, :], totb[half][:, :],
                         start=(half == 0), stop=(half == 1))
        s_mm.mark(i)
        s_mm.wait(V)
        i = V.tensor_copy(ltot[:, :], b7[0:16, 2:3])
        s_dve.mark(i)
        evac_mark(7, s_dve)
        G.wait_ge(s_dve.h, s_dve.n)
        s_w.mark(G.dma_start(out=cc_in.ap(), in_=ltot[:, :]), 16)
        G.wait_ge(s_w.h, s_w.n)
        ic = G.collective_compute(
            "AllGather", OP.bypass, replica_groups=[list(range(NCORES))],
            ins=[cc_in.ap()], outs=[cc_out.ap()])
        s_cc.mark(ic)
        G.wait_ge(s_cc.h, s_cc.n)
        gw = s_w.mark(G.dma_start(out=gath[:, :], in_=cc_out.ap()), 16)
        b7 = t_take_bank(7)
        T.wait_ge(s_w.h, gw)
        i = T.matmul(b7[0:16, 0:1], gath[:, :], mask_lt[:, :], start=True, stop=True)
        s_mm.mark(i)
        i = T.matmul(b7[0:16, 1:2], gath[:, :], onescf[0:8, 0:1], start=True, stop=True)
        s_mm.mark(i)
        s_mm.wait(V)
        vseq(V.tensor_copy(cpfx[:, :], b7[0:16, 0:1]))
        i = V.tensor_scalar(ltot[:, :], b7[0:16, 1:2], 2048.0, None, OP.min)
        s_dve.mark(i)
        evac_mark(7, s_dve)
        b7 = t_take_bank(7)
        T.wait_ge(s_dve.h, s_dve.n)
        i = T.matmul(b7[0:16, 3:4], shift8[:, :], ltot[:, :], start=True, stop=True)
        s_mm.mark(i)
        s_mm.wait(V)
        i = vseq(V.tensor_tensor(cpfx[:, :], cpfx[:, :], b7[0:16, 3:4], OP.add))
        i = s_dve
        evac_mark(7, s_dve)
        core_n = s_dve.n
        b7 = t_take_bank(7)
        T.wait_ge(s_dve.h, core_n)
        T.matmul(b7[:, 4:5], maskPfx16[:, :], totb[0][:, :], start=True, stop=False)
        i = T.matmul(b7[:, 4:5], mask16fT[:, :], cpfx[:, :], start=False, stop=True)
        s_mm.mark(i)
        T.matmul(b7[:, 5:6], maskPfx16[:, :], totb[1][:, :], start=True, stop=False)
        T.matmul(b7[:, 5:6], maskJ16[:, :], totb[0][:, :], start=False, stop=False)
        i = T.matmul(b7[:, 5:6], mask16fT[:, :], cpfx[:, :], start=False, stop=True)
        s_mm.mark(i)
        s_mm.wait(V)
        V.tensor_copy(totb[0][:, :], b7[:, 4:5])
        i = V.tensor_copy(totb[1][:, :], b7[:, 5:6])
        s_dve.mark(i)
        evac_mark(7, s_dve)
        off_n = s_dve.n
        for half in range(2):
            b7 = t_take_bank(7)
            T.wait_ge(s_dve.h, s_dve.n)
            i = T.matmul(b7[0:1, 0:128], totb[half][:, :], identf[:, :],
                         start=True, stop=True)
            s_mm.mark(i)
            s_mm.wait(V)
            i = V.tensor_copy(offrow[0:1, 128 * half:128 * (half + 1)], b7[0:1, 0:128])
            s_dve.mark(i)
            evac_mark(7, s_dve)

        for tc in range(NTC):
            half = tc // 8
            b7 = t_take_bank(7)
            T.wait_ge(s_dve.h, s_dve.n)
            i = T.matmul(b7[:, 0:16], trif[0:1, :],
                         offrow[0:1, 16 * tc:16 * (tc + 1)],
                         start=True, stop=True)
            s_mm.mark(i)
            s_mm.wait(V)
            vseq(V.tensor_tensor(prefb[:, :], cs_all[:, 16 * tc:16 * (tc + 1)], b7[:, 0:16], OP.add))
            vseq(V.tensor_scalar(prefb[:, :], prefb[:, :], 2048.5, None, OP.is_lt))
            vseq(V.tensor_tensor(prefb[:, :], prefb[:, :], oh_all[:, 16 * tc:16 * (tc + 1)], OP.mult))
            vseq(V.tensor_scalar(mm8[:, :], prefb[:, 0:8], w_all[:, 2 * tc:2 * tc + 1], None, OP.mult))
            vseq(V.tensor_scalar(prefb[:, 8:16], prefb[:, 8:16], w_all[:, 2 * tc + 1:2 * tc + 2], None, OP.mult))
            vseq(V.tensor_tensor(mm8[:, :], mm8[:, :], prefb[:, 8:16], OP.add))
            evac_mark(7, s_dve)
            i = V.tensor_copy(cbc[:, 8 * tc:8 * (tc + 1)], mm8[:, :])
            s_dve.mark(i)
            b6 = t_take_bank(6)
            s_dve.wait(T)
            i = T.matmul(b6[0:8, 0:128], cbc[:, 8 * tc:8 * (tc + 1)], identb[:, :],
                         start=True, stop=True)
            s_mm.mark(i)
            s_mm.wait(A)
            ia = A.activation(cT[0:8, 128 * tc:128 * (tc + 1)], b6[0:8, 0:128], AF.Copy)
            s_act.mark(ia)
            evac_mark(6, s_act)
        if DEV_FORCE_COMBINE1:
            s_act.wait(V)
            i = V.memset(cT[:, :], 1.0)
            s_dve.mark(i)
            s_dve.wait(A)
            ia = A.activation(cT[:, :], cT[:, :], AF.Copy)
            s_act.mark(ia)
        ct_n = s_act.n
        if dbg is not None:
            s_dve.wait(Y); s_act.wait(Y)
            s_dve.wait(G); s_act.wait(G); s_w.wait(G)
            _nn = nc.allow_non_contiguous_dma(reason="dbg"); _nn.__enter__()
            s_st.mark(Y.dma_start(out=dbg.ap()[0:8, 0:16], in_=gath[:, :]), 16)
            s_st.mark(Y.dma_start(out=dbg.ap()[0:16, 16:17], in_=ltot[:, :]), 16)
            s_st.mark(Y.dma_start(out=dbg.ap()[0:16, 17:18], in_=cpfx[:, :]), 16)
            s_st.mark(Y.dma_start(out=dbg.ap()[0:1, 128:384], in_=offrow[:, :]), 16)
            s_st.mark(Y.dma_start(out=dbg.ap()[:, 384:640], in_=cs_all[:, :]), 16)
            s_st.mark(Y.dma_start(out=dbg.ap()[:, 640:896], in_=oh_all[:, :]), 16)
            s_st.mark(Y.dma_start(out=dbg.ap()[:, 896:928], in_=w_all[:, :]), 16)
            s_st.mark(G.dma_start(out=dbg.ap()[0:8, 1024:3072], in_=cT[:, :]), 16)
            _nn.__exit__(None, None, None)

        w1_use = [0, 0, 0, 0]
        w2_use = [0, 0, 0, 0]
        for e in range(E):
            s_act.wait(T, ct_n)
            for t4 in range(NT):
                bi = 4 + (t4 % 2)
                bank = t_take_bank(bi)
                i = T.matmul(bank[:, :], sel8[:, 128 * e:128 * (e + 1)],
                             cT[:, TQ * t4:TQ * (t4 + 1)], start=True, stop=True)
                s_mm.mark(i)
                s_mm.wait(A)
                ia = A.activation(cbc[:, TQ * t4:TQ * (t4 + 1)], bank[:, :], AF.Copy)
                s_act.mark(ia)
                evac_mark(bi, s_act)
            cbc_n = s_act.n
            for th in range(2):
                hof = 1024 * th
                for hc in range(32):
                    rb = hc % 4
                    G.wait_ge(s_mm.h, w1_use[rb])
                    wn = s_w.mark(G.dma_start(
                        out=w1r[rb][:, :].rearrange("p (c n) -> p c n", n=128),
                        in_=dm["W1"].ap()[e, :, 128 * hc:128 * (hc + 1)]
                            .rearrange("(c p) n -> p c n", p=128)), 16)
                    T.wait_ge(s_w.h, wn)
                    for tt in range(2):
                        tsl = slice(hof + TQ * tt, hof + TQ * (tt + 1))
                        bi = tt
                        bank = t_take_bank(bi)
                        for c in range(NC):
                            i = T.matmul(bank[:, :], w1r[rb][:, 128 * c:128 * (c + 1)],
                                         xb[c][:, tsl], start=(c == 0), stop=(c == NC - 1))
                        s_mm.mark(i)
                        s_mm.wait(A)
                        ia = A.activation(h_sb[hc][:, TQ * tt:TQ * (tt + 1)], bank[:, :],
                                          AF.Gelu_apprx_tanh, bias=b1_p[:, 32 * e + hc:32 * e + hc + 1])
                        s_act.mark(ia)
                        evac_mark(bi, s_act)
                    w1_use[rb] = s_mm.n
                h_n = s_act.n
                for dc in range(NC):
                    for wh in range(2):
                        rb = (dc * 2 + wh) % 4
                        G.wait_ge(s_mm.h, w2_use[rb])
                        wn = s_w.mark(G.dma_start(
                            out=w2r[rb][:, :].rearrange("p (c n) -> p c n", n=128),
                            in_=dm["W2"].ap()[e, 2048 * wh:2048 * (wh + 1), 128 * dc:128 * (dc + 1)]
                                .rearrange("(c p) n -> p c n", p=128)), 16)
                        if wh == 0:
                            wn0 = wn
                        w2_use[rb] = max(w2_use[rb], 1)
                    A_done = h_n
                    T.wait_ge(s_act.h, A_done)
                    for tt in range(2):
                        tsl = slice(hof + TQ * tt, hof + TQ * (tt + 1))
                        bi = 2 + ((dc * 2 + tt) % 2)
                        bank = t_take_bank(bi)
                        for hc in range(32):
                            if hc == 0:
                                T.wait_ge(s_w.h, wn0)
                            if hc == 16:
                                T.wait_ge(s_w.h, wn)
                            rb = (dc * 2 + hc // 16) % 4
                            i = T.matmul(bank[:, :], w2r[rb][:, 128 * (hc % 16):128 * (hc % 16 + 1)],
                                         h_sb[hc][:, TQ * tt:TQ * (tt + 1)],
                                         start=(hc == 0), stop=(hc == 31))
                        s_mm.mark(i)
                        if tt == 1:
                            w2_use[(dc * 2) % 4] = s_mm.n
                            w2_use[(dc * 2 + 1) % 4] = s_mm.n
                        s_mm.wait(V)
                        s_act.wait(V, cbc_n)
                        V.scalar_tensor_tensor(tmpf[tt][:, :], bank[:, :],
                                               b2_p[:, 8 * e + dc:8 * e + dc + 1],
                                               cbc[:, tsl], OP.add, OP.mult)
                        iv = V.tensor_tensor(x[dc][:, tsl], x[dc][:, tsl], tmpf[tt][:, :], OP.add)
                        s_dve.mark(iv)
                        evac_mark(bi, s_dve)

    with contextlib.ExitStack() as st2, nc.named_scope("l0_moe"):
        l0_moe(st2)
    with nc.named_scope("ln1"):
        postln()
    store_tap("x2")

    def l1_mla(st):
        cq = [st.enter_context(nc.sbuf_tensor(f"cq{i}", [128, S], BF16)) for i in range(2)]
        ckv = [st.enter_context(nc.sbuf_tensor(f"ckv{i}", [128, S], BF16)) for i in range(2)]
        kr64 = st.enter_context(nc.sbuf_tensor("kr64", [96, S], BF16))
        vtm1 = [st.enter_context(nc.sbuf_tensor(f"vtm1_{t}", [128, D], BF16)) for t in range(16)]
        q96 = [st.enter_context(nc.sbuf_tensor(f"q96_{i}", [96, S], BF16)) for i in range(2)]
        k96 = [st.enter_context(nc.sbuf_tensor(f"k96_{i}", [96, S], BF16)) for i in range(2)]
        ctxp1 = [st.enter_context(nc.sbuf_tensor(f"ctxp1_{i}", [128, S], BF16))
                 for i in range(2)]
        pt1 = [st.enter_context(nc.sbuf_tensor(f"pt1_{k}", [128, TQ], BF16)) for k in range(3)]
        rsc = st.enter_context(nc.sbuf_tensor("rsc0", [128, S], BF16))
        rcos1 = st.enter_context(nc.sbuf_tensor("rcos1_s", [128, S], BF16))
        rsin1 = st.enter_context(nc.sbuf_tensor("rsin1_s", [128, S], BF16))
        stA = contextlib.ExitStack()
        wdq = stA.enter_context(nc.sbuf_tensor("wdq", [128, 2048], BF16))
        wdkv = stA.enter_context(nc.sbuf_tensor("wdkv", [128, 2048], BF16))
        wuv = stA.enter_context(nc.sbuf_tensor("wuv", [128, 2048], BF16))
        wkr2 = stA.enter_context(nc.sbuf_tensor("wkr2", [128, 256], BF16))
        print("sbuf left (l1 mla A):", nc.sbuf_bytes_remaining)

        G.wait_ge(s_mm.h, s_mm.n)
        G.wait_ge(s_dve.h, s_dve.n)
        G.wait_ge(s_act.h, s_act.n)
        Y.wait_ge(s_dve.h, s_dve.n)
        Y.wait_ge(s_act.h, s_act.n)
        Y.wait_ge(s_mm.h, s_mm.n)
        s_io.mark(Y.dma_start(out=rcos1[:, :], in_=dm["rcos1"].ap()), 16)
        s_io.mark(Y.dma_start(out=rsin1[:, :], in_=dm["rsin1"].ap()), 16)
        cast0 = cast_xb()
        s_io.wait(V)

        ldq = s_w.mark(G.dma_start(out=wdq[:, :].rearrange("p (c m) -> p c m", m=256),
                                   in_=dm["W_DQ"].ap().rearrange("(c p) m -> p c m", p=128)), 16)
        ldkv = s_w.mark(G.dma_start(out=wdkv[:, :].rearrange("p (c m) -> p c m", m=256),
                                    in_=dm["W_DKV"].ap().rearrange("(c p) m -> p c m", p=128)), 16)
        lkr = s_w.mark(G.dma_start(out=wkr2[:, :].rearrange("p (c m) -> p c m", m=32),
                                   in_=dm["W_KR"].ap().rearrange("(c p) m -> p c m", p=128)), 16)
        luv = s_w.mark(G.dma_start(out=wuv[:, :].rearrange("p (c m) -> p c m", m=1024),
                                   in_=dm["W_UV"].ap().rearrange("(c p) m -> p c m", p=128)), 16)
        lwo = luq = luq2 = lqr = None

        s_dve.wait(T)
        T.wait_ge(s_w.h, ldkv)
        for which, (w_, dst) in enumerate(((wdq, cq), (wdkv, ckv))):
            for m2 in range(2):
                for t in range(NT):
                    tsl = slice(TQ * t, TQ * (t + 1))
                    bi = 4 + (t % 2)
                    bank = t_take_bank(bi)
                    for c in range(NC):
                        i = T.matmul(bank[:, :], w_[:, 256 * c + 128 * m2:256 * c + 128 * (m2 + 1)],
                                     xb[c][:, tsl], start=(c == 0), stop=(c == NC - 1))
                    s_mm.mark(i)
                    s_mm.wait(A)
                    ia = A.activation(dst[m2][:, tsl], bank[:, :], AF.Copy)
                    s_act.mark(ia)
                    evac_mark(bi, s_act)

        T.wait_ge(s_w.h, lkr)
        for t in range(NT):
            tsl = slice(TQ * t, TQ * (t + 1))
            bi = 4 + (t % 2)
            bank = t_take_bank(bi)
            for c in range(NC):
                i = T.matmul(bank[64:96, :], wkr2[:, 32 * c:32 * (c + 1)],
                             xb[c][:, tsl], start=(c == 0), stop=(c == NC - 1))
            s_mm.mark(i)
            s_mm.wait(A)
            ia = A.activation(kr64[64:96, tsl], bank[64:96, :], AF.Copy)
            s_act.mark(ia)
            evac_mark(bi, s_act)
        s_act.wait(V)
        rope_inplace(kr64, 32, rcos1, rsin1, rsc, base=64)
        kr_n = s_dve.n

        T.wait_ge(s_w.h, luv)
        s_act.wait(T)
        for tt in range(16):
            tsl = slice(128 * tt, 128 * (tt + 1))
            for half in range(2):
                bi = (2 * tt + half) % 4
                bank = t_take_bank(bi)
                for c2 in range(2):
                    i = T.matmul(bank[:, :], ckv[c2][:, tsl],
                                 wuv[:, 1024 * c2 + 512 * half:1024 * c2 + 512 * (half + 1)],
                                 start=(c2 == 0), stop=(c2 == 1))
                s_mm.mark(i)
                s_mm.wait(A)
                ia = A.activation(vtm1[tt][:, 512 * half:512 * (half + 1)], bank[:, :], AF.Copy)
                s_act.mark(ia)
                evac_mark(bi, s_act)

        stA.close()
        wqc = st.enter_context(nc.sbuf_tensor("wqc", [128, 2 * 1536], BF16))
        wuk = st.enter_context(nc.sbuf_tensor("wuk", [128, 2048], BF16))
        wob = [st.enter_context(nc.sbuf_tensor(f"wob{i}", [128, 1024], BF16))
               for i in range(2)]
        print("sbuf left (l1 mla B):", nc.sbuf_bytes_remaining)
        G.wait_ge(s_mm.h, s_mm.n)
        luq = s_w.mark(G.dma_start(out=wqc[:, :].rearrange("p (c m) -> p c m", m=1536),
                                   in_=dm["Wqcat"].ap().rearrange("(c p) m -> p c m", p=128)), 16)
        lqr = s_w.mark(G.dma_start(out=wuk[:, :].rearrange("p (c m) -> p c m", m=1024),
                                    in_=dm["W_UK"].ap().rearrange("(c p) m -> p c m", p=128)), 16)
        wob_use = [0, 0]
        wob_n = [0, 0]

        def load_wob(p):
            rb = p % 2
            G.wait_ge(s_mm.h, wob_use[rb])
            wob_n[rb] = s_w.mark(G.dma_start(
                out=wob[rb][:, :].rearrange("p (c m) -> p c m", m=128),
                in_=dm["W_O"].ap()[128 * p:128 * (p + 1), :]
                    .rearrange("r (c m) -> r c m", m=128)), 16)
        load_wob(0)
        T.wait_ge(s_w.h, lqr)
        scale1 = 1.0 / math.sqrt(96.0)
        CLO = float(np.exp(np.float64(-80.0) / np.sqrt(np.float64(96.0))))
        CHI = float(np.exp(np.float64(80.0) / np.sqrt(np.float64(96.0))))
        gen_n = [0, 0]

        def l1_gen_head(h, defer_v=False):
            hb = h % 2
            for t in range(NT):
                tsl = slice(TQ * t, TQ * (t + 1))
                bi = 7
                bank = t_take_bank(bi)
                for c2 in range(2):
                    i = T.matmul(bank[0:96, :], wqc[:, 1536 * c2 + 96 * h:1536 * c2 + 96 * (h + 1)],
                                 cq[c2][:, tsl], start=(c2 == 0), stop=(c2 == 1))
                s_mm.mark(i)
                A.wait_ge(s_mm.h, s_mm.n)
                ia = A.activation(q96[hb][0:96, tsl], bank[0:96, :], AF.Copy)
                s_act.mark(ia)
                evac_mark(bi, s_act)
                bi2 = 0
                bank2 = t_take_bank(bi2)
                for c2 in range(2):
                    i = T.matmul(bank2[0:64, :], wuk[:, 1024 * c2 + 64 * h:1024 * c2 + 64 * (h + 1)],
                                 ckv[c2][:, tsl], start=(c2 == 0), stop=(c2 == 1))
                s_mm.mark(i)
                A.wait_ge(s_mm.h, s_mm.n)
                ia = A.activation(k96[hb][0:64, tsl], bank2[0:64, :], AF.Copy)
                s_act.mark(ia)
                evac_mark(bi2, s_act)
            if not defer_v:
                _gen_head_v(h)

        def _gen_head_v(h):
            hb = h % 2
            s_act.wait(V)
            V.wait_ge(s_dve.h, kr_n)
            i = V.tensor_copy(k96[hb][64:96, :], kr64[64:96, :])
            s_dve.mark(i)
            rope_inplace(q96[hb], 32, rcos1, rsin1, rsc, base=64)
            gen_n[hb] = s_dve.n

        l1_gen_head(0)
        pt_reader = [0] * 3
        blk1 = 0
        pend = {"p": 0, "rb": 0, "ctx_n": 0, "waited": False, "tiles": []}
        xb_cast_n = {(q_, o_): (s_dve, cast0) for q_ in range(NT) for o_ in range(NC)}

        def emit_proj_tile(bi):
            if not pend["tiles"]:
                return
            if not pend["waited"]:
                T.wait_ge(s_dve.h, pend["ctx_n"])
                T.wait_ge(s_w.h, wob_n[pend["rb"]])
                pend["waited"] = True
            qt_, oc_ = pend["tiles"].pop(0)
            qsl_ = slice(TQ * qt_, TQ * (qt_ + 1))
            bank = t_take_bank(bi)
            sem_, n_ = xb_cast_n[(qt_, oc_)]
            T.wait_ge(sem_.h, n_)
            T.matmul(bank[:, :], wob[pend["rb"]][:, 128 * oc_:128 * (oc_ + 1)],
                     ctxp1[pend["p"] % 2][:, qsl_], start=True, stop=False)
            i = T.matmul(bank[:, :], identb[:, :], xb[oc_][:, qsl_],
                         start=False, stop=True)
            s_mm.mark(i)
            A.wait_ge(s_mm.h, s_mm.n)
            ia = A.activation(x[oc_][:, qsl_], bank[:, :], AF.Copy)
            s_act.mark(ia)
            evac_mark(bi, s_act)
            G.wait_ge(s_act.h, s_act.n)
            nw = s_w.mark(G.dma_start(out=xb[oc_][:, qsl_], in_=x[oc_][:, qsl_]), 16)
            xb_cast_n[(qt_, oc_)] = (s_w, nw)
            if not pend["tiles"]:
                wob_use[pend["rb"]] = s_mm.n

        for head in range(NH):
            hb = head % 2
            hh = head % 2
            hof = 64 * hh
            hsl = slice(hof, hof + 64)
            genv_chunks = []
            if head + 1 < NH:
                l1_gen_head(head + 1, defer_v=True)
                genv_chunks = [(j, (head + 1) % 2) for j in range(NT)]
            T.wait_ge(s_dve.h, gen_n[hb])
            for qt in range(NT):
                qsl = slice(TQ * qt, TQ * (qt + 1))
                nkc = 4 * qt + 4
                dbi, cbi = (3, 4) if blk1 % 2 == 0 else (5, 6)
                blk1 += 1
                dps = t_take_bank(dbi)[hsl, :]
                cps = t_take_bank(cbi)[hsl, :]
                pt_ready = [0] * 3

                def emit_S1(kc):
                    bi = kc % 3
                    bank = t_take_bank(bi)
                    i = T.matmul(bank[:, :], k96[hb][:, 128 * kc:128 * (kc + 1)],
                                 q96[hb][:, qsl], start=True, stop=True)
                    s_mm.mark(i)
                    A.wait_ge(s_mm.h, s_mm.n)
                    A.wait_ge(s_mm.h, pt_reader[kc % 3])
                    ia = A.activation(pt1[kc % 3][:, :], bank[:, :], AF.Exp, scale=scale1)
                    s_act.mark(ia)
                    evac_mark(bi, s_act)
                    s_act.wait(V)
                    i = V.tensor_scalar(pt1[kc % 3][:, :], pt1[kc % 3][:, :],
                                        CLO, CHI, OP.max, OP.min)
                    s_dve.mark(i)
                    s_ = kc - 4 * qt
                    if s_ >= 0:
                        blk = slice(128 * s_, 128 * (s_ + 1))
                        i = V.tensor_tensor(pt1[kc % 3][:, blk], pt1[kc % 3][:, blk],
                                            tri[:, :], OP.mult)
                        if s_ > 0:
                            i = V.memset(pt1[kc % 3][:, 0:128 * s_], 0.0)
                        s_dve.mark(i)
                    pt_ready[kc % 3] = s_dve.n

                emit_S1(0)
                if nkc > 1:
                    emit_S1(1)
                for kc in range(nkc):
                    if kc + 2 < nkc:
                        emit_S1(kc + 2)
                    T.wait_ge(s_dve.h, pt_ready[kc % 3])
                    T.matmul(dps, ones[:, 0:64], pt1[kc % 3][:, :],
                             start=(kc == 0), stop=(kc == nkc - 1))
                    i = T.matmul(cps, vtm1[kc][:, 64 * head:64 * (head + 1)],
                                 pt1[kc % 3][:, :], start=(kc == 0), stop=(kc == nkc - 1))
                    s_mm.mark(i)
                    pt_reader[kc % 3] = s_mm.n
                s_mm.wait(V)
                s_mm.wait(A)
                V.reciprocal(dps, dps)
                cdst = ctxp1[(head // 2) % 2]
                ic = A.activation(cdst[hsl, qsl], cps, AF.Copy)
                s_act.mark(ic)
                evac_mark(cbi, s_act)
                s_act.wait(V)
                i = V.tensor_tensor(cdst[hsl, qsl], cdst[hsl, qsl], dps, OP.mult)
                s_dve.mark(i)
                evac_mark(dbi, s_dve)
                ob1, ob2 = (5, 6) if dbi == 3 else (3, 4)
                for b_ in (7, ob1, ob2, 7):
                    emit_proj_tile(b_)
                for _ in range(2):
                    if genv_chunks:
                        if len(genv_chunks) == NT:
                            s_act.wait(V)
                            V.wait_ge(s_dve.h, kr_n)
                        j_, hb_ = genv_chunks.pop(0)
                        csl_ = slice(TQ * j_, TQ * (j_ + 1))
                        V.tensor_copy(k96[hb_][64:96, csl_], kr64[64:96, csl_])
                        n_ = rope_cols(q96[hb_], rcos1, rsin1, rsc, csl_,
                                       base=64, nrows=32)
                        if not genv_chunks:
                            gen_n[hb_] = n_
            if hh == 1:
                p = head // 2
                while pend["tiles"]:
                    emit_proj_tile(7)
                if p < 7:
                    load_wob(p + 1)
                pend.update(p=p, rb=p % 2, ctx_n=s_dve.n, waited=False,
                            tiles=[(q_, o_) for q_ in range(NT) for o_ in range(NC)])
                if DEV_FORCE_FLUSH:
                    while pend["tiles"]:
                        emit_proj_tile(7)
        rot = 0
        while pend["tiles"]:
            emit_proj_tile((3, 4, 5, 6, 7)[rot % 5])
            rot += 1

    with contextlib.ExitStack() as st3, nc.named_scope("l1_mla"):
        l1_mla(st3)
    with nc.named_scope("ln2"):
        postln()
    store_tap("x3")

    def l1_ffn(st):
        win = st.enter_context(nc.sbuf_tensor("win", [128, NC * 64], BF16))
        wout = st.enter_context(nc.sbuf_tensor("wout", [64, 1024], BF16))
        s_sb = st.enter_context(nc.sbuf_tensor("s_sb", [64, S], F32))
        gA = st.enter_context(nc.sbuf_tensor("gA", [64, S], F32))
        gB = st.enter_context(nc.sbuf_tensor("gB", [64, S], F32))
        gbf = st.enter_context(nc.sbuf_tensor("gbf", [64, S], BF16))
        h_i = [st.enter_context(nc.sbuf_tensor(f"h_i{k}", [64, S], F32))
               for k in range(2)]
        print("sbuf left (l1 ffn):", nc.sbuf_bytes_remaining)

        G.wait_ge(s_mm.h, s_mm.n)
        G.wait_ge(s_dve.h, s_dve.n)
        G.wait_ge(s_act.h, s_act.n)
        cast_xb()
        lwin = s_w.mark(G.dma_start(out=win[:, :].rearrange("p (c m) -> p c m", m=64),
                                    in_=dm["Win"].ap().rearrange("(c p) m -> p c m", p=128)), 16)
        lwout = s_w.mark(G.dma_start(out=wout[:, :], in_=dm["Wout"].ap()), 16)

        s_dve.wait(T)
        T.wait_ge(s_w.h, lwin)
        for t in range(NT):
            tsl = slice(TQ * t, TQ * (t + 1))
            bi = 4 + (t % 2)
            bank = t_take_bank(bi)
            for c in range(NC):
                i = T.matmul(bank[0:64, :], win[:, 64 * c:64 * (c + 1)],
                             xb[c][:, tsl], start=(c == 0), stop=(c == NC - 1))
            s_mm.mark(i)
            s_mm.wait(A)
            ia = A.activation(s_sb[:, tsl], bank[0:64, :], AF.Identity, bias=binc_s[:, :])
            s_act.mark(ia)
            evac_mark(bi, s_act)
        ia = A.activation(gA[:, :], s_sb[:, :], AF.Identity, bias=b2f_s[:, :], scale=0.0)
        s_act.mark(ia)
        gs = [gA, gB]
        h_marks = [0, 0]
        for it in range(16):
            hb_ = it % 2
            A.wait_ge(s_dve.h, h_marks[hb_])
            ia = A.activation(h_i[hb_][:, :], s_sb[:, :], AF.Relu,
                              scale=w1f_s[:, it:it + 1], bias=b1f_s[:, it:it + 1])
            s_act.mark(ia)
            s_act.wait(V)
            dst = gbf if it == 15 else gs[(it + 1) % 2]
            iv = V.scalar_tensor_tensor(dst[:, :], h_i[hb_][:, :], w2f_s[:, it:it + 1],
                                        gs[it % 2][:, :], OP.mult, OP.add)
            s_dve.mark(iv)
            h_marks[hb_] = s_dve.n
        gbf_n = s_dve.n
        T.wait_ge(s_w.h, lwout)
        T.wait_ge(s_dve.h, gbf_n)
        for qt in range(NT):
            qsl = slice(TQ * qt, TQ * (qt + 1))
            for oc in range(NC):
                bi = 4 + ((2 * qt + oc) % 4)
                bank = t_take_bank(bi)
                T.matmul(bank[:, :], wout[:, 128 * oc:128 * (oc + 1)],
                         gbf[:, qsl], start=True, stop=False)
                i = T.matmul(bank[:, :], identb[:, :], xb[oc][:, qsl],
                             start=False, stop=True)
                s_mm.mark(i)
                A.wait_ge(s_mm.h, s_mm.n)
                ia = A.activation(x[oc][:, qsl], bank[:, :], AF.Identity,
                                  bias=bout_p[:, oc:oc + 1])
                s_act.mark(ia)
                evac_mark(bi, s_act)

    with contextlib.ExitStack() as st4, nc.named_scope("l1_ffn"):
        l1_ffn(st4)
    with nc.named_scope("ln3"):
        postln(store=True)

    s_st.wait(Y)
    return nc


def prep_inputs(inputs):
    WqT = np.ascontiguousarray(inputs["L0_attn_Wq"].T)
    WkT = np.ascontiguousarray(inputs["L0_attn_Wk"].T)
    WvT = np.ascontiguousarray(inputs["L0_attn_Wv"].T)
    WoT = np.ascontiguousarray(inputs["L0_attn_Wo"].T)
    W_UQ = inputs["L1_attn_W_UQ"]; W_QR = inputs["L1_attn_W_QR"]
    Wqcat = np.zeros((256, NH * 96), np.float32)
    for h in range(NH):
        Wqcat[:, 96 * h:96 * h + 64] = W_UQ[:, 64 * h:64 * (h + 1)]
        Wqcat[:, 96 * h + 64:96 * (h + 1)] = W_QR[:, 32 * h:32 * (h + 1)]

    def rope_tab(nrows, period, d_rope):
        j = np.arange(nrows)
        r = (j % period) // 2
        inv = 10000.0 ** (-2.0 * r / d_rope)
        ang = inv[:, None] * np.arange(S, dtype=np.float64)[None, :]
        cosm = np.cos(ang)
        sinm = np.sin(ang) * np.where(j % 2 == 0, -1.0, 1.0)[:, None]
        return _bfa(cosm), _bfa(sinm)

    rcos0, rsin0 = rope_tab(128, 64, 64)
    rcos1, rsin1 = rope_tab(128, 32, 32)
    tri = _bfa(np.triu(np.ones((128, 128))))
    onesm = _bfa(np.ones((128, 128)))
    ident = _bfa(np.eye(128))
    trif = np.triu(np.ones((128, 128), np.float32))
    identf = np.eye(128, dtype=np.float32)
    jj = np.arange(128)
    mask16f = ((jj[:, None] % 16) == np.arange(16)[None, :]).astype(np.float32)
    mask16fT = np.ascontiguousarray(mask16f.T[:, :128]).astype(np.float32)
    maskPfx16 = (((jj[:, None] % 16) == (jj[None, :] % 16)) &
                 ((jj[:, None] // 16) < (jj[None, :] // 16))).astype(np.float32)
    maskJ16 = ((jj[:, None] % 16) == (jj[None, :] % 16)).astype(np.float32)
    shift8 = np.zeros((16, 16), np.float32)
    for i_ in range(8):
        shift8[i_, i_ + 8] = 1.0
    oh127f = np.zeros((128, 1), np.float32); oh127f[127, 0] = 1.0
    sel8 = np.zeros((8, 8 * 128), np.float32)
    for e in range(E):
        sel8[e, 128 * e:128 * (e + 1)] = 1.0
    sel8 = _bfa(sel8)

    def pcol(v):
        return np.ascontiguousarray(np.asarray(v, np.float32).reshape(NC, 128).T)

    shared = dict(
        WqT=WqT, WkT=WkT, WvT=WvT, WoT=WoT,
        bq_p=pcol(inputs["L0_attn_bq"]), bk_p=pcol(inputs["L0_attn_bk"]),
        bo_p=pcol(inputs["L0_attn_bo"]), bv=inputs["L0_attn_bv"][None, :],
        Wg=inputs["L0_ffn_Wg"], W1=inputs["L0_ffn_W1"], b1=inputs["L0_ffn_b1"],
        W2=inputs["L0_ffn_W2"], b2=inputs["L0_ffn_b2"],
        b1_p=np.ascontiguousarray(
            inputs["L0_ffn_b1"].reshape(E * H_FF // 128, 128).T),
        W_DQ=inputs["L1_attn_W_DQ"], W_DKV=inputs["L1_attn_W_DKV"],
        W_UQ=inputs["L1_attn_W_UQ"], W_QR=inputs["L1_attn_W_QR"], Wqcat=Wqcat,
        W_UK=inputs["L1_attn_W_UK"], W_UV=inputs["L1_attn_W_UV"],
        W_KR=inputs["L1_attn_W_KR"], W_O=inputs["L1_attn_W_O"],
        Win=inputs["L1_ffn_Win"], binc=inputs["L1_ffn_bin_"][:, None],
        Wout=inputs["L1_ffn_Wout"],
        W1f=inputs["L1_ffn_W1"], b1f=inputs["L1_ffn_b1"], W2f=inputs["L1_ffn_W2"],
        b2f=inputs["L1_ffn_b2"][:, None], bout_p=pcol(inputs["L1_ffn_bout"]),
        tri=tri, ones=onesm, ident=ident, trif=trif, identf=identf,
        mask16f=mask16f, mask16fT=mask16fT, maskPfx16=maskPfx16, maskJ16=maskJ16,
        shift8=shift8, oh127f=oh127f, sel8=sel8,
        b2_p=np.ascontiguousarray(inputs["L0_ffn_b2"].reshape(E * NC, 128).T),
        rcos0=rcos0, rsin0=rsin0, rcos1=rcos1, rsin1=rsin1,
    )
    shared = {k: np.ascontiguousarray(np.asarray(v, dtype=np.float32)
                                      if np.asarray(v).dtype == np.float64 else v)
              for k, v in shared.items()}
    in_maps = []
    xfull = inputs["L0_attn_x"]
    for i in range(NCORES):
        m = dict(shared)
        m["x_in"] = np.ascontiguousarray(np.asarray(xfull[i]).T.astype(np.float32))
        lt = np.zeros((8, 1), np.float32); lt[:i, 0] = 1.0
        m["mask_lt"] = lt
        in_maps.append(m)
    return in_maps


def kernel(**inputs):
    in_maps = prep_inputs(inputs)
    nc = build_program(DEV_TAPS)
    res = run_bass_kernel_spmd(nc, in_maps, core_ids=list(range(NCORES)))
    out = np.stack([res.results[i]["out_x"].T for i in range(NCORES)])
    if DEV_TAPS:
        kernel.tap_results = [
            {t: res.results[i]["tap_" + t].T for t in DEV_TAPS} for i in range(NCORES)
        ]
    return np.ascontiguousarray(out, dtype=np.float32)
